# version 06
# baseline (speedup 1.0000x reference)
import numpy as np
import concourse.bass as bass
import concourse.mybir as mybir
from concourse.bass_utils import run_bass_kernel_spmd
from concourse.aot_env import allow_ici_all_to_all

F32 = mybir.dt.float32
BF16 = mybir.dt.bfloat16
AF = mybir.ActivationFunctionType
ALU = mybir.AluOpType
AXL = mybir.AxisListType
AP = bass.AP

R, B, S, D, NST, L = 8, 8, 4096, 1024, 16, 128
SL = S // R
GL = SL // L
GG = S // L
CB = GL * B
DCH = D // R
T = B
EPS = 1e-5
KB = 1024
NRND = 11


PHASES = 7



def _ul(r):
    return (1 << r) if r <= 6 else 0


def _cl(r):
    return (1 << r) if r <= 6 else (1 if r == 7 else 0)


def chain_waves():
    waves = []
    bank = 0
    for r in range(NRND):
        colw = _ul(r) + _cl(r) + 32
        per = max(1, min(8, 512 // colw))
        ch0 = 0
        while ch0 < 32:
            nch = min(per, 32 - ch0)
            waves.append((r, ch0, nch, colw, bank))
            bank ^= 1
            ch0 += nch
    return waves


CWAVES = chain_waves()
RSTART = {}
for _i, (_r, _c0, _n, _cw, _bk) in enumerate(CWAVES):
    RSTART.setdefault(_r, _i)

PHASE_END = {
    1: [("r1d", 16 * 32), ("ln1", 32), ("cst", 16 * 6), ("gld", 16 * 22)],
    2: [("che", 1 + len(CWAVES)), ("chm", 2 + len(CWAVES)),
        ("r1d", 16 * 32), ("cst", 16 * 6)],
    2.2: [("kbe", 2)],
    2.21: [("kbm", 1)],
    2.3: [("kbe", 6), ("kbm", 4)],
    2.4: [("kbe", 10), ("kbm", 8)],
    2.45: [("kbe", 14), ("kbm", 12)],
    2.5: [("kbe", 46), ("r1d", 16 * 32), ("cst", 16 * 6)],
    3: [("yev", 64), ("sev", 16), ("a21", 1), ("r1d", 16 * 32),
        ("cst", 16 * 6)],
    4: [("wev", 64), ("r1d", 16 * 32), ("cst", 16 * 6)],
    5: [("l2f", 2), ("asp", 16 * 8), ("aev", 128), ("gev", 32),
        ("wgt", 16 * 4)],
    6: [("gmm", 64), ("cbfs", 9), ("pwd", 16 * 64), ("x2d", 16 * 8),
        ("cup", 64), ("gac", 64), ("wgt", 16 * 8)],
}


def build_nc(dump=()):
    nc = bass.Bass("TRN2", target_bir_lowering=False, debug=False)

    x_in = nc.dram_tensor("x", [B, SL, D], F32, kind="ExternalInput")
    a_in = nc.dram_tensor("attn_A", [DCH, NST, NST], F32, kind="ExternalInput")
    bm_in = nc.dram_tensor("attn_Bm", [DCH, NST], F32, kind="ExternalInput")
    cm_in = nc.dram_tensor("attn_Cm", [DCH, NST], F32, kind="ExternalInput")
    dm_in = nc.dram_tensor("attn_Dm", [DCH], F32, kind="ExternalInput")
    lnw_in = nc.dram_tensor("ln_w", [D], F32, kind="ExternalInput")
    lnb_in = nc.dram_tensor("ln_b", [D], F32, kind="ExternalInput")
    wout_in = nc.dram_tensor("W_out", [D, D], F32, kind="ExternalInput")
    bout_in = nc.dram_tensor("b_out", [D], F32, kind="ExternalInput")
    fw = {nm: nc.dram_tensor(nm + "T", [D, D], F32, kind="ExternalInput")
          for nm in ("wf", "uf", "wi", "ui", "wc", "uc", "wo", "uo")}
    woutT_in = nc.dram_tensor("WoutT", [D, D], F32, kind="ExternalInput")
    fv = {nm: nc.dram_tensor(nm, [D], F32, kind="ExternalInput")
          for nm in ("bf", "bi", "bc", "bo", "in_c")}
    out_t = nc.dram_tensor("out", [T, 8, 128, SL], F32, kind="ExternalOutput")

    KSEC = 128 * DCH * 128
    USEC = 128 * DCH * NST
    WSEC = 128 * 32 * 128
    BLOB = KSEC + USEC + WSEC
    blob_loc = nc.dram_tensor("blob_loc", [BLOB], BF16)
    blob_ag = nc.dram_tensor("blob_ag", [R * BLOB], BF16)
    SSEC = 128 * 32 * GL * B
    s_blob = nc.dram_tensor("s_blob", [R * SSEC], BF16)
    s_a2a = nc.dram_tensor("s_a2a", [R * SSEC], BF16)
    h_blob = nc.dram_tensor("h_blob", [R * SSEC], BF16)
    h_a2a = nc.dram_tensor("h_a2a", [R * SSEC], BF16)
    r1_sp = nc.dram_tensor("r1_sp", [32 * 128 * D], BF16)
    at_sp = nc.dram_tensor("at_sp", [T * 8 * 128 * SL], BF16)
    xl2_sp = nc.dram_tensor("xl2_sp", [T * 8 * 128 * SL], BF16)
    pwo_sp = nc.dram_tensor("pwo_sp", [T * 8 * 128 * SL], F32)

    dbg = {}
    def DBG(name, shape, dt=F32):
        if name in dump and name not in dbg:
            dbg[name] = nc.dram_tensor("dbg_" + name, shape, dt,
                                       kind="ExternalOutput")
        return dbg.get(name)

    sbA = nc.alloc_sbuf_tensor
    identb = sbA("identb", [128, 128], BF16)
    revb = sbA("revb", [128, 128], BF16)
    widen = sbA("widen", [128, 256], BF16)
    onesb = sbA("onesb", [128, 128], BF16)
    lnw_rep = sbA("lnw_rep", [128, D], BF16)
    lnb_rep = sbA("lnb_rep", [128, D], BF16)
    fvec = sbA("fvec", [128, 6 * 8], F32)
    FB = {"bout": 0, "bf": 1, "bi": 2, "bc": 3, "bo": 4, "inc": 5}
    xstg = sbA("xstg", [128, 2 * D], F32)
    stat1 = sbA("stat1", [128, 16], F32)
    kstr = sbA("kstr", [128, 2 * 16 * 128], BF16)
    wstr = sbA("wstr", [128, 2 * 4 * 128], BF16)
    hstr = sbA("hstr", [128, 2 * 4 * CB], BF16)
    vscr = sbA("vscr", [128, 64], F32)
    dmrow = sbA("dmrow", [1, DCH], F32)

    ARNB = 175 * KB
    arena = nc.alloc_sbuf_tensor("arena", [128, ARNB // 4], F32)
    arb = arena.bitcast(BF16)
    PSF = ARNB // 4
    PSB = ARNB // 2

    def BA(offb, part0, nparts, dims):
        assert offb % 2 == 0
        return AP(tensor=arb, offset=part0 * PSB + offb // 2,
                  ap=[[PSB, nparts]] + [list(d) for d in dims])

    def FA(offb, part0, nparts, dims):
        assert offb % 4 == 0
        return AP(tensor=arena, offset=part0 * PSF + offb // 4,
                  ap=[[PSF, nparts]] + [list(d) for d in dims])

    O_XL1 = 0
    O_PC, O_PN, O_PTC, O_PTN = 64 * KB, 65 * KB, 66 * KB, 67 * KB
    O_UST, O_CST, O_PRD, O_ALP = 68 * KB, 76 * KB, 85 * KB, 167 * KB
    O_BMB, O_KRT, O_CMS = 98 * KB, 98 * KB + 256, 98 * KB + 1024
    O_KLOC, O_ULOC = 99 * KB, 131 * KB
    O_Y = 64 * KB
    O_SST = 99 * KB
    O_MU3, O_RS3 = O_Y, O_Y + 1024
    O_USTR = 135 * KB
    O_SQ1, O_R1T = 139 * KB, 143 * KB
    O_H, O_H2 = 135 * KB, 151 * KB
    O_WOT, O_WSTG = 135 * KB, 151 * KB
    O_ZB, O_GF, O_R1D, O_ATS = 151 * KB, 155 * KB, 159 * KB, 167 * KB
    O_SQ2 = 131 * KB
    OF = {"wf": 0, "uf": 16 * KB, "wi": 32 * KB, "ui": 48 * KB,
          "wc": 64 * KB, "uc": 80 * KB, "wo": 135 * KB, "uo": 99 * KB}
    O_X2DB, O_ATD2, O_PWEV = 151 * KB, 167 * KB, 123 * KB
    O_CF, O_CBF = 135 * KB, 115 * KB
    O_GTF, O_GTI, O_GTG, O_CT1, O_UOS = (123 * KB, 125 * KB, 127 * KB,
                                         129 * KB, 131 * KB)
    O_PWDB, O_UOO, O_HSL, O_OSL = 64 * KB, 80 * KB, 151 * KB, 135 * KB

    pbk = [nc.alloc_psum_tensor("pb%d" % i, [128, 512], F32) for i in range(8)]
    pbb = [p.bitcast(BF16) for p in pbk]

    def PA(bank, part0, nparts, offe, dims):
        return AP(tensor=pbk[bank], offset=part0 * 512 + offe,
                  ap=[[512, nparts]] + [list(d) for d in dims])

    def PAB(bank, part0, nparts, offe, dims):
        return AP(tensor=pbb[bank], offset=part0 * 1024 + offe,
                  ap=[[1024, nparts]] + [list(d) for d in dims])

    class Sem:
        def __init__(self, h):
            self.h = h
            self.n = 0

    sems = {}
    ctxs = []
    for _n in ("x cst seed sqa ln1s sqt ln1 r1c r1d chm che kbm kbe bd ag1 "
               "ust umm sev sd a21 kst kmm yev h0 pfm pfa hsh hd a22 hld wmm "
               "wev wld iln trm gev aev l2m l2e asp r1l l2f xsl x2c x2d pwm "
               "pwe pwd x2l gmm gac cup cbfs uom uoe pwl atl hsl oev od "
               "wcst pini gtrs gfe gld wgt").split():
        cm = nc.semaphore("s_" + _n)
        sems[_n] = Sem(cm.__enter__())
        ctxs.append(cm)

    def W(eng, name, val):
        eng.wait_ge(sems[name].h, val)

    def I(name, instr, k=1):
        instr.then_inc(sems[name].h, k)
        sems[name].n += k
        return instr

    def DSER(eng, name):
        if sems[name].n:
            eng.wait_ge(sems[name].h, sems[name].n)

    blkcm = nc.Block()
    block = blkcm.__enter__()

    @block.vector
    def _(v):
        for i in range(8):
            v.memset(pbk[i][:, :], 0.0)
        v.memset(BA(O_UST, 0, 128, [[1, 32 * 128]]), 0.0)
        v.memset(BA(O_CST, 0, 128, [[1, 32 * 132]]), 0.0)
        v.memset(BA(O_BMB, 0, 128, [[1, 32]]), 0.0)
        v.memset(BA(O_CMS, 0, 128, [[1, 32]]), 0.0)
        v.memset(BA(O_PC, 0, 128, [[1, 512]]), 0.0)
        v.drain()
        I("pini", v.nop())

    @block.gpsimd
    def _(g):
        g.memset(onesb[:, :], 1.0)
        g.memset(hstr[:, :], 0.0)
        g.drain()
        g.affine_select(identb[:, :], onesb[:, :], [[-1, 128]], ALU.is_equal,
                        0.0, base=0, channel_multiplier=1)
        g.affine_select(revb[:, :], onesb[:, :], [[1, 128]], ALU.is_equal,
                        0.0, base=-127, channel_multiplier=1)
        g.memset(widen[:, :], 0.0)
        g.drain()
        g.tensor_copy(widen[:, 0:128], revb[:, :])
        g.drain()
        I("seed", g.drain())
        W(g, "pini", 1)
        for q in range(4):
            src = AP(tensor=a_in, offset=256 * q,
                     ap=[[16, 16], [1024, 32], [1, 16]])
            DSER(g, "gld")
            I("gld", g.dma_start(out=BA(O_PC, 32 * q, 16,
                                        [[16, 32], [1, 16]]), in_=src), 16)
        with nc.allow_non_contiguous_dma(reason="tiny seed scatters"):
            for q in range(4):
                src = AP(tensor=bm_in, offset=16 * q, ap=[[1, 16], [64, 32]])
                DSER(g, "gld")
                I("gld", g.dma_start(out=BA(O_BMB, 32 * q, 16, [[1, 32]]),
                                     in_=src), 16)
            for q in range(4):
                src = AP(tensor=bm_in, offset=16 * q, ap=[[1, 16], [64, 32]])
                DSER(g, "gld")
                I("gld", g.dma_start(out=BA(O_UST, 32 * q, 16, [[128, 32]]),
                                     in_=src), 16)
            for q in range(4):
                src = AP(tensor=cm_in, offset=16 * q, ap=[[1, 16], [64, 32]])
                DSER(g, "gld")
                I("gld", g.dma_start(out=BA(O_CST, 32 * q, 16, [[132, 32]]),
                                     in_=src), 16)
        with nc.allow_non_contiguous_dma(reason="tiny Dm scatter"):
            for q in range(4):
                src = AP(tensor=dm_in, offset=q, ap=[[0, 16], [4, 32]])
                DSER(g, "gld")
                I("gld", g.dma_start(out=BA(O_CMS, 32 * q, 16, [[1, 32]]),
                                     in_=src), 16)
        repw = AP(tensor=lnw_in, offset=0, ap=[[0, 128], [1, D]])
        DSER(g, "gld")
        I("gld", g.dma_start(out=lnw_rep[:, :], in_=repw), 16)
        repb = AP(tensor=lnb_in, offset=0, ap=[[0, 128], [1, D]])
        DSER(g, "gld")
        I("gld", g.dma_start(out=lnb_rep[:, :], in_=repb), 16)

    @block.sync
    def _(sync):
        for nm in ("bout", "bf", "bi", "bc", "bo", "inc"):
            tens = {"bout": bout_in, "bf": fv["bf"], "bi": fv["bi"],
                    "bc": fv["bc"], "bo": fv["bo"], "inc": fv["in_c"]}[nm]
            src = AP(tensor=tens, offset=0, ap=[[1, 128], [128, 8]])
            dstv = AP(tensor=fvec, offset=FB[nm] * 8, ap=[[48, 128], [1, 8]])
            with nc.allow_non_contiguous_dma(reason="tiny vec transpose"):
                I("cst", sync.dma_start(out=dstv, in_=src), 16)
        def r1spill(blk):
            W(sync, "r1c", blk + 1)
            DSER(sync, "r1d")
            I("r1d", sync.dma_start(
                out=AP(tensor=r1_sp, offset=blk * 128 * D,
                       ap=[[D, 128], [1, D]]),
                in_=BA(O_R1T + (blk % 2) * 2 * D, 0, 128, [[1, D]])), 16)
        for blk in range(32):
            b, c = blk // GL, blk % GL
            if blk >= 2:
                W(sync, "ln1", blk - 1)
                r1spill(blk - 2)
            DSER(sync, "x")
            I("x", sync.dma_start(
                out=xstg[:, (blk % 2) * D:(blk % 2) * D + D],
                in_=x_in[b, c * 128:(c + 1) * 128, :]), 16)
        r1spill(30)
        r1spill(31)

    @block.scalar
    def _(sc):
        for blk in range(32):
            W(sc, "x", 16 * (blk + 1))
            xt = xstg[:, (blk % 2) * D:(blk % 2) * D + D]
            sc.activation(FA(O_SQ1, 0, 128, [[1, D]]), xt, AF.Square,
                          accum_out=stat1[:, 1:2])
            sc.drain()
            I("sqa", sc.nop())
            W(sc, "ln1s", blk + 1)
            sc.activation(stat1[:, 3:4], stat1[:, 4:5], AF.Sqrt)
            sc.drain()
            I("sqt", sc.nop())
            W(sc, "ln1", blk + 1)
            if blk >= 2:
                W(sc, "r1d", 16 * (blk - 1))
            sc.activation(BA(O_R1T + (blk % 2) * 2 * D, 0, 128, [[1, D]]),
                          xt, AF.Identity, bias=stat1[:, 5:6],
                          scale=stat1[:, 6:7])
            sc.drain()
            I("r1c", sc.nop())

    @block.vector
    def _(v):
        for blk in range(32):
            b, c = blk // GL, blk % GL
            W(v, "x", 16 * (blk + 1))
            xt = xstg[:, (blk % 2) * D:(blk % 2) * D + D]
            v.reduce_sum(stat1[:, 0:1], xt, axis=AXL.X)
            W(v, "sqa", blk + 1)
            v.drain()
            v.tensor_scalar(stat1[:, 2:3], stat1[:, 0:1], 1.0 / D, None,
                            ALU.mult)
            v.tensor_scalar(stat1[:, 4:5], stat1[:, 1:2], 1.0 / D, None,
                            ALU.mult)
            v.drain()
            v.scalar_tensor_tensor(stat1[:, 4:5], stat1[:, 2:3],
                                   stat1[:, 2:3], stat1[:, 4:5],
                                   ALU.mult, ALU.subtract)
            v.drain()
            v.tensor_scalar(stat1[:, 4:5], stat1[:, 4:5], -1.0, EPS,
                            ALU.mult, ALU.add)
            v.drain()
            v.reciprocal(stat1[:, 4:5], stat1[:, 4:5])
            v.drain()
            I("ln1s", v.nop())
            W(v, "sqt", blk + 1)
            v.tensor_scalar(stat1[:, 5:6], stat1[:, 2:3], stat1[:, 3:4],
                            -1.0, ALU.mult, ALU.mult)
            v.tensor_scalar(stat1[:, 6:7], stat1[:, 3:4], 1.0, None, ALU.add)
            v.drain()
            v.tensor_scalar(BA(O_XL1 + 2 * (c * B + b), 0, 128, [[CB, D]]),
                            xt, stat1[:, 2:3], stat1[:, 3:4],
                            ALU.subtract, ALU.mult)
            v.drain()
            I("ln1", v.nop())

    @block.tensor
    def _(t):
        if PHASES < 2:
            return
        W(t, "gld", 16 * 16)
        W(t, "seed", 1)
        W(t, "pini", 1)
        for chhi in range(32):
            bank = chhi % 2
            for q in range(4):
                t.transpose(PAB(bank, 32 * q, 16, 16 * (chhi // 2), [[1, 16]]),
                            BA(O_PC + 32 * chhi, 32 * q, 16, [[1, 16]]),
                            identb[32 * q:32 * q + 16, 32 * q:32 * q + 16],
                            tile_position=(32 * q, 32 * q))
        I("chm", t.drain(), 2)
        for widx, (r, ch0, nch, colw, bank) in enumerate(CWAVES):
            ul, cl = _ul(r), _cl(r)
            pco = O_PC if r % 2 == 0 else O_PN
            pto = O_PTC if r % 2 == 0 else O_PTN
            need = 1 + RSTART[r]
            if widx >= 2:
                need = max(need, 1 + widx - 1)
            W(t, "che", need)
            for ci in range(nch):
                chhi = ch0 + ci
                cb0 = ci * colw
                for q in range(4):
                    tp = (32 * q, 32 * q)
                    ptt = BA(pto + 32 * chhi, 32 * q, 16, [[1, 16]])
                    pti = BA(pco + 32 * chhi, 32 * q, 16, [[1, 16]])
                    if ul:
                        t.matmul(PA(bank, 32 * q, 16, cb0, [[1, ul]]),
                                 ptt, BA(O_UST + 256 * chhi, 32 * q, 16,
                                         [[1, ul]]),
                                 start=True, stop=True, tile_position=tp)
                    if cl:
                        t.matmul(PA(bank, 32 * q, 16, cb0 + ul, [[1, cl]]),
                                 pti, BA(O_CST + 264 * chhi, 32 * q, 16,
                                         [[1, cl]]),
                                 start=True, stop=True, tile_position=tp)
                    t.matmul(PA(bank, 32 * q, 16, cb0 + ul + cl, [[1, 16]]),
                             ptt, pti, start=True, stop=True, tile_position=tp)
                    t.matmul(PA(bank, 32 * q, 16, cb0 + ul + cl + 16,
                                [[1, 16]]),
                             pti, ptt, start=True, stop=True, tile_position=tp)
            I("chm", t.drain())

    @block.vector
    def _(v):
        if PHASES < 2:
            return
        W(v, "chm", 2)
        for bank in range(2):
            v.tensor_copy(BA(O_PTC + 32 * bank, 0, 128, [[32, 16], [1, 16]]),
                          PAB(bank, 0, 128, 0, [[16, 16], [1, 16]]))
        v.drain()
        I("che", v.nop())
        for widx, (r, ch0, nch, colw, bank) in enumerate(CWAVES):
            ul, cl = _ul(r), _cl(r)
            pno = O_PN if r % 2 == 0 else O_PC
            ptno = O_PTN if r % 2 == 0 else O_PTC
            W(v, "chm", 3 + widx)
            if ul:
                v.tensor_copy(
                    BA(O_UST + 2 * (ch0 * 128 + (1 << r)), 0, 128,
                       [[128, nch], [1, ul]]),
                    PA(bank, 0, 128, 0, [[colw, nch], [1, ul]]))
            if cl:
                cdst = (1 << r) if r <= 6 else 128
                v.tensor_copy(
                    BA(O_CST + 2 * (ch0 * 132 + cdst), 0, 128,
                       [[132, nch], [1, cl]]),
                    PA(bank, 0, 128, ul, [[colw, nch], [1, cl]]))
            v.tensor_copy(BA(pno + 32 * ch0, 0, 128, [[16, nch], [1, 16]]),
                          PA(bank, 0, 128, ul + cl, [[colw, nch], [1, 16]]))
            v.tensor_copy(BA(ptno + 32 * ch0, 0, 128, [[16, nch], [1, 16]]),
                          PA(bank, 0, 128, ul + cl + 16,
                             [[colw, nch], [1, 16]]))
            if r >= 6:
                v.tensor_copy(
                    BA(O_ALP + 2 * ((r - 6) * 512 + ch0 * 16), 0, 128,
                       [[16, nch], [1, 16]]),
                    PA(bank, 0, 128, ul + cl + 16, [[colw, nch], [1, 16]]))
            v.drain()
            I("che", v.nop())

    CH_DONE = 1 + len(CWAVES)

    @block.vector
    def _(v):
        if PHASES < 2.2:
            return
        W(v, "che", CH_DONE)
        v.tensor_tensor(BA(O_PRD, 0, 128, [[128, 32], [1, 128]]),
                        BA(O_CST, 0, 128, [[132, 32], [1, 128]]),
                        BA(O_BMB, 0, 128, [[1, 32], [0, 128]]),
                        ALU.mult)
        v.drain()
        I("kbe", v.nop())
        W(v, "gld", 16 * 20)
        v.scalar_tensor_tensor(BA(O_PRD, 0, 128, [[128, 32]]),
                               BA(O_CMS, 0, 128, [[1, 32]]), 1.0 / 16,
                               BA(O_PRD, 0, 128, [[128, 32]]),
                               ALU.mult, ALU.add)
        v.drain()
        I("kbe", v.nop())

    O_UBSCR = 64 * KB

    @block.tensor
    def _(t):
        if PHASES < 2.21:
            return
        W(t, "kbe", 2)
        NGRP_KR = 1 if PHASES < 2.3 else 4
        for grp in range(NGRP_KR):
            bank = 2 + (grp % 2)
            if grp >= 2:
                W(t, "kbe", grp + 1)
            for q in range(4):
                for k8 in range(8):
                    ci = k8 * 4 + q
                    chl = grp * 32 + ci
                    chhi = chl // 4
                    lhsT = AP(tensor=arb,
                              offset=32 * q * PSB + (O_PRD // 2) + chhi * 128,
                              ap=[[PSB, 16], [1, 128]])
                    t.matmul(PA(bank, 0, 128, ci, [[1, 1]]),
                             lhsT, onesb[32 * q:32 * q + 16, 0:1],
                             start=True, stop=True,
                             tile_position=(32 * q, 0))
                t.drain()
            I("kbm", t.drain())
        if PHASES < 2.4:
            return
        for grp in range(4):
            bank = 4 + (grp % 2)
            if grp >= 2:
                W(t, "kbe", grp + 5)
            for q in range(4):
                for k8 in range(8):
                    ci = k8 * 4 + q
                    chl = grp * 32 + ci
                    chhi = chl // 4
                    lhsT = AP(tensor=arb,
                              offset=32 * q * PSB + (O_UST // 2) + chhi * 128,
                              ap=[[PSB, 16], [1, 128]])
                    t.matmul(PA(bank, 0, 128, 16 * ci, [[1, 16]]),
                             lhsT, identb[32 * q:32 * q + 16,
                                          32 * q:32 * q + 16],
                             start=True, stop=True,
                             tile_position=(32 * q, 0))
                t.drain()
            I("kbm", t.drain())
        if PHASES < 2.45:
            return
        for grp in range(4):
            bank = 6 + (grp % 2)
            W(t, "kbe", 7 + grp)
            if grp >= 2:
                W(t, "kbe", 11 + grp - 2)
            t.matmul(PA(bank, 0, 128, 0, [[1, 512]]),
                     revb[:, :],
                     BA(O_UBSCR + (grp % 2) * 1024, 0, 128, [[1, 512]]),
                     start=True, stop=True)
            I("kbm", t.drain())
        if PHASES < 2.5:
            return
        W(t, "kbe", 14)
        for jj in range(32):
            bank = jj % 2
            if jj >= 2:
                W(t, "kbe", 14 + jj - 1)
            for j4 in range(4):
                j = jj * 4 + j4
                t.matmul(PA(bank, 0, 128, 128 * j4, [[1, 128]]),
                         widen[:, 127 - j:127 - j + 128],
                         BA(O_KRT, 0, 128, [[1, 128]]),
                         start=True, stop=True)
            I("kbm", t.drain())

    @block.vector
    def _(v):
        if PHASES < 2.3:
            return
        for grp in range(4):
            bank = 2 + (grp % 2)
            W(v, "kbm", grp + 1)
            with nc.allow_low_precision(reason="bf16 kernel coeffs"):
                v.tensor_copy(BA(O_KRT + 2 * (32 * grp), 0, 128, [[1, 32]]),
                              PA(bank, 0, 128, 0, [[1, 32]]))
            v.drain()
            I("kbe", v.nop())
        if PHASES < 2.4:
            return
        for grp in range(4):
            bank = 4 + (grp % 2)
            W(v, "kbm", 4 + grp + 1)
            if grp >= 2:
                W(v, "kbm", 9 + grp - 2)
            with nc.allow_low_precision(reason="bf16 U operator"):
                v.tensor_copy(BA(O_UBSCR + (grp % 2) * 1024, 0, 128,
                                 [[16, 32], [1, 16]]),
                              PA(bank, 0, 128, 0, [[16, 32], [1, 16]]))
            v.drain()
            I("kbe", v.nop())
        if PHASES < 2.45:
            return
        for grp in range(4):
            bank = 6 + (grp % 2)
            W(v, "kbm", 9 + grp)
            v.tensor_copy(BA(O_ULOC + 2 * (32 * grp * 16), 0, 128,
                             [[16, 32], [1, 16]]),
                          PA(bank, 0, 128, 0, [[16, 32], [1, 16]]))
            v.drain()
            I("kbe", v.nop())
        if PHASES < 2.5:
            return
        for jj in range(32):
            bank = jj % 2
            W(v, "kbm", 12 + jj + 1)
            v.tensor_copy(
                BA(O_KLOC + 2 * (jj * 4), 0, 128, [[128, 128], [1, 4]]),
                PA(bank, 0, 128, 0, [[1, 128], [128, 4]]))
            v.drain()
            I("kbe", v.nop())

    KB_DONE = 46

    @block.sync
    def _(sync):
        if PHASES < 3:
            return
        W(sync, "kbe", KB_DONE)
        I("bd", sync.dma_start(
            out=AP(tensor=blob_loc, offset=0, ap=[[DCH * 128, 128], [1, DCH * 128]]),
            in_=BA(O_KLOC, 0, 128, [[1, DCH * 128]])), 16)
        I("bd", sync.dma_start(
            out=AP(tensor=blob_loc, offset=KSEC, ap=[[DCH * 16, 128], [1, DCH * 16]]),
            in_=BA(O_ULOC, 0, 128, [[1, DCH * 16]])), 16)
        I("bd", sync.dma_start(
            out=AP(tensor=blob_loc, offset=KSEC + USEC,
                   ap=[[32 * 128, 128], [1, 32 * 128]]),
            in_=BA(O_CST + 2, 0, 128, [[132, 32], [1, 128]])), 16)

    @block.gpsimd
    def _(g):
        if PHASES < 3:
            return
        W(g, "bd", 16 * 3)
        I("ag1", g.collective_compute(
            "AllGather", ALU.bypass, replica_groups=[list(range(R))],
            ins=[blob_loc.ap().opt()], outs=[blob_ag.ap().opt()]))

    @block.sync
    def _(sync):
        if PHASES < 3:
            return
        for wv in range(16):
            gch0 = wv * 64
            rank, chl0 = gch0 // DCH, gch0 % DCH
            if wv >= 2:
                W(sync, "umm", wv - 1)
            W(sync, "ag1", 1)
            src = AP(tensor=blob_ag, offset=rank * BLOB + KSEC + chl0 * 16,
                     ap=[[DCH * 16, 128], [1, 64 * 16]])
            DSER(sync, "ust")
            I("ust", sync.dma_start(
                out=AP(tensor=arb, offset=(O_USTR // 2) + (wv % 2) * 1024,
                       ap=[[PSB, 128], [1, 1024]]),
                in_=src), 16)

    @block.tensor
    def _(t):
        if PHASES < 3:
            return
        W(t, "ln1", 32)
        for wv in range(16):
            W(t, "ust", 16 * (wv + 1))
            bank = 6 + (wv % 2)
            if wv >= 2:
                W(t, "sev", wv - 1)
            for q in range(4):
                for k16 in range(16):
                    ci = k16 * 4 + q
                    ch = wv * 64 + ci
                    chhi = (ch // 4) % 16
                    lhs = AP(tensor=arb,
                             offset=(O_USTR // 2) + (wv % 2) * 1024 + ci * 16,
                             ap=[[PSB, 128], [1, 16]])
                    rhs = BA(O_XL1 + 2 * ch * CB, 0, 128, [[1, CB]])
                    t.matmul(PA(bank, 32 * q, 16, 32 * chhi, [[1, CB]]),
                             lhs, rhs, start=True, stop=True,
                             tile_position=(0, 32 * q))
                t.drain()
            I("umm", t.drain())

    @block.vector
    def _(v):
        if PHASES < 3:
            return
        for wv in range(16):
            W(v, "umm", wv + 1)
            bank = 6 + (wv % 2)
            v.tensor_copy(BA(O_SST + 2 * (wv * 16 * CB), 0, 128,
                             [[CB, 16], [1, CB]]),
                          PA(bank, 0, 128, 0, [[CB, 16], [1, CB]]))
            v.drain()
            I("sev", v.nop())

    @block.sync
    def _(sync):
        if PHASES < 3:
            return
        W(sync, "sev", 16)
        for tgt in range(R):
            I("sd", sync.dma_start(
                out=AP(tensor=s_blob, offset=tgt * SSEC,
                       ap=[[32 * CB, 128], [1, 32 * CB]]),
                in_=BA(O_SST + 2 * (tgt * 32 * CB), 0, 128, [[1, 32 * CB]])), 16)

    @block.gpsimd
    def _(g):
        if PHASES < 3:
            return
        W(g, "sd", 16 * R)
        with allow_ici_all_to_all():
            I("a21", g.collective_compute(
                "AllToAll", ALU.bypass, replica_groups=[list(range(R))],
                ins=[s_blob.ap().opt()], outs=[s_a2a.ap().opt()]))

    @block.sync
    def _(sync):
        if PHASES < 3:
            return
        for bt in range(64):
            gch0 = bt * 16
            rank, chl0 = gch0 // DCH, gch0 % DCH
            if bt >= 2:
                W(sync, "kmm", bt - 1)
            W(sync, "ag1", 1)
            src = AP(tensor=blob_ag, offset=rank * BLOB + chl0 * 128,
                     ap=[[DCH * 128, 128], [1, 16 * 128]])
            DSER(sync, "kst")
            I("kst", sync.dma_start(
                out=AP(tensor=kstr, offset=(bt % 2) * 2048,
                       ap=[[4096, 128], [1, 2048]]),
                in_=src), 16)

    @block.tensor
    def _(t):
        if PHASES < 3:
            return
        for bt in range(64):
            W(t, "kst", 16 * (bt + 1))
            bank = bt % 2
            if bt >= 2:
                W(t, "yev", bt - 1)
            for ci in range(16):
                ch = bt * 16 + ci
                lhs = AP(tensor=kstr, offset=(bt % 2) * 2048 + ci * 128,
                         ap=[[4096, 128], [1, 128]])
                rhs = BA(O_XL1 + 2 * ch * CB, 0, 128, [[1, CB]])
                t.matmul(PA(bank, 0, 128, CB * ci, [[1, CB]]),
                         lhs, rhs, start=True, stop=True)
            I("kmm", t.drain())

    @block.vector
    def _(v):
        if PHASES < 3:
            return
        W(v, "sd", 16 * R)
        for bt in range(64):
            W(v, "kmm", bt + 1)
            bank = bt % 2
            v.tensor_copy(
                BA(O_Y + 2 * (bt * 16), 0, 128,
                   [[1, 16], [D, GL], [GL * D, B]]),
                PA(bank, 0, 128, 0, [[CB, 16], [B, GL], [1, B]]))
            v.drain()
            I("yev", v.nop())

    @block.sync
    def _(sync):
        if PHASES < 4:
            return
        W(sync, "a21", 1)
        W(sync, "r1d", 16 * 32)
        I("h0", sync.dma_start(
            out=AP(tensor=arb, offset=O_H // 2,
                   ap=[[PSB, 128], [1024, 8], [1, 1024]]),
            in_=AP(tensor=s_a2a, offset=0,
                   ap=[[1024, 128], [SSEC, 8], [1, 1024]])), 16)


    @block.tensor
    def _(t):
        if PHASES < 4:
            return
        W(t, "h0", 16)
        W(t, "yev", 64)
        for r in range(5):
            sh = 1 << r
            nf, rem = (32 - sh) // 4, (32 - sh) % 4
            for wv in range(16):
                widx = r * 16 + wv
                bank = widx % 2
                if widx >= 2:
                    W(t, "pfa", widx - 1)
                for ci in range(2):
                    chhi = wv * 2 + ci
                    for q in range(4):
                        lhs = BA(O_ALP + 2 * (r * 512 + chhi * 16),
                                 32 * q, 16, [[1, 16]])
                        rhs1 = BA(O_H + 2 * (chhi * 32), 32 * q, 16,
                                  [[1024, nf], [8, 4], [1, 8]])
                        t.matmul(PA(bank, 32 * q, 16, ci * 256, [[1, nf * 32]]),
                                 lhs, rhs1, start=True, stop=True,
                                 tile_position=(32 * q, 32 * q))
                        if rem:
                            rhs2 = BA(O_H + 2 * (nf * 1024 + chhi * 32),
                                      32 * q, 16, [[8, rem], [1, 8]])
                            t.matmul(PA(bank, 32 * q, 16,
                                        ci * 256 + nf * 32, [[1, rem * 8]]),
                                     lhs, rhs2, start=True, stop=True,
                                     tile_position=(32 * q, 32 * q))
                I("pfm", t.drain())

    def _prefix_adds(eng, parity):
        for r in range(5):
            sh = 1 << r
            for wv in range(16):
                widx = r * 16 + wv
                if parity is not None and widx % 2 != parity:
                    continue
                bank = widx % 2
                W(eng, "pfm", widx + 1)
                for ci in range(2):
                    chhi = wv * 2 + ci
                    if sh < 4:
                        hd = BA(O_H + 2 * (chhi * 32 + sh * 8), 0, 128,
                                [[1, (4 - sh) * 8]])
                        eng.tensor_tensor(hd, PA(bank, 0, 128, ci * 256,
                                                 [[1, (4 - sh) * 8]]),
                                          hd, ALU.add)
                        bd = BA(O_H + 2 * (chhi * 32 + 1024), 0, 128,
                                [[1024, 7], [8, 4], [1, 8]])
                        eng.tensor_tensor(bd, PA(bank, 0, 128,
                                                 ci * 256 + (4 - sh) * 8,
                                                 [[1, 224]]),
                                          bd, ALU.add)
                    else:
                        nsrc = 8 - sh // 4
                        bd = BA(O_H + 2 * (chhi * 32 + (sh // 4) * 1024),
                                0, 128, [[1024, nsrc], [8, 4], [1, 8]])
                        eng.tensor_tensor(bd, PA(bank, 0, 128, ci * 256,
                                                 [[1, nsrc * 32]]),
                                          bd, ALU.add)
                eng.drain()
                W(eng, "pfa", widx)
                I("pfa", eng.nop())

    @block.vector
    def _(v):
        if PHASES < 4:
            return
        _prefix_adds(v, None)

    @block.vector
    def _(v):
        if PHASES < 4:
            return
        W(v, "pfa", 80)
        v.tensor_copy(BA(O_H2 + 2 * 8, 0, 128,
                         [[1024, 8], [32, 32], [8, 3], [1, 8]]),
                      BA(O_H, 0, 128, [[1024, 8], [32, 32], [8, 3], [1, 8]]))
        v.tensor_copy(BA(O_H2 + 2 * 1024, 0, 128,
                         [[1024, 7], [32, 32], [1, 8]]),
                      BA(O_H + 2 * 24, 0, 128, [[1024, 7], [32, 32], [1, 8]]))
        v.memset(BA(O_H2, 0, 128, [[32, 32], [1, 8]]), 0.0)
        v.drain()
        I("hsh", v.nop())

    @block.sync
    def _(sync):
        if PHASES < 4:
            return
        W(sync, "hsh", 1)
        for tgt in range(R):
            I("hd", sync.dma_start(
                out=AP(tensor=h_blob, offset=tgt * SSEC,
                       ap=[[1024, 128], [1, 1024]]),
                in_=BA(O_H2 + 2 * (tgt * 1024), 0, 128, [[1, 1024]])), 16)

    @block.gpsimd
    def _(g):
        if PHASES < 4:
            return
        W(g, "hd", 16 * R)
        with allow_ici_all_to_all():
            I("a22", g.collective_compute(
                "AllToAll", ALU.bypass, replica_groups=[list(range(R))],
                ins=[h_blob.ap().opt()], outs=[h_a2a.ap().opt()]))

    @block.sync
    def _(sync):
        if PHASES < 4:
            return
        W(sync, "a22", 1)
        for bt in range(64):
            gch0 = bt * 16
            rank, chl0 = gch0 // DCH, gch0 % DCH
            chhi0 = chl0 // 4
            if bt >= 2:
                W(sync, "wmm", bt - 1)
            srcw = AP(tensor=blob_ag,
                      offset=rank * BLOB + KSEC + USEC + chhi0 * 128,
                      ap=[[32 * 128, 128], [1, 512]])
            DSER(sync, "wld")
            I("wld", sync.dma_start(
                out=AP(tensor=wstr, offset=(bt % 2) * 512,
                       ap=[[1024, 128], [1, 512]]),
                in_=srcw), 16)
            srch = AP(tensor=h_a2a, offset=rank * SSEC + chhi0 * 32,
                      ap=[[1024, 128], [1, 128]])
            DSER(sync, "hld")
            I("hld", sync.dma_start(
                out=AP(tensor=hstr, offset=(bt % 2) * 128,
                       ap=[[256, 128], [1, 128]]),
                in_=srch), 16)

    @block.tensor
    def _(t):
        if PHASES < 4:
            return
        for bt in range(64):
            W(t, "wld", 16 * (bt + 1))
            W(t, "hld", 16 * (bt + 1))
            bank = 2 + (bt % 2)
            if bt >= 2:
                W(t, "wev", bt - 1)
            for q in range(4):
                for k4 in range(4):
                    ci = k4 * 4 + q
                    chs = ci // 4
                    lhs = AP(tensor=wstr, offset=32 * q * 1024
                             + (bt % 2) * 512 + chs * 128,
                             ap=[[1024, 16], [1, 128]])
                    rhs = AP(tensor=hstr, offset=32 * q * 256
                             + (bt % 2) * 128 + chs * 32,
                             ap=[[256, 16], [1, 32]])
                    t.matmul(PA(bank, 0, 128, ci * 32, [[1, 32]]),
                             lhs, rhs, start=True, stop=True,
                             tile_position=(32 * q, 0))
                t.drain()
            I("wmm", t.drain())

    @block.vector
    def _(v):
        if PHASES < 4:
            return
        for bt in range(64):
            W(v, "wmm", bt + 1)
            bank = 2 + (bt % 2)
            ydst = BA(O_Y + 2 * (bt * 16), 0, 128,
                      [[1, 16], [D, GL], [GL * D, B]])
            v.tensor_tensor(ydst,
                            PA(bank, 0, 128, 0, [[CB, 16], [B, GL], [1, B]]),
                            ydst, ALU.add)
            v.drain()
            I("wev", v.nop())

    def _wload(g, src_t, offb):
        return g.dma_start(
            out=BA(offb, 0, 128, [[1024, 8], [1, 1024]]),
            in_=AP(tensor=src_t, offset=0,
                   ap=[[1024, 128], [131072, 8], [1, 1024]]))

    @block.gpsimd
    def _(g):
        if PHASES < 5:
            return
        W(g, "hd", 16 * R)
        DSER(g, "wgt"); I("wgt", _wload(g, woutT_in, O_WOT), 16)
        W(g, "kmm", 64)
        DSER(g, "wgt"); I("wgt", _wload(g, fw["wf"], OF["wf"]), 16)
        DSER(g, "wgt"); I("wgt", _wload(g, fw["uf"], OF["uf"]), 16)
        DSER(g, "wgt"); I("wgt", _wload(g, fw["wi"], OF["wi"]), 16)

    @block.sync
    def _(sync):
        if PHASES < 5:
            return
        for pair in range(16):
            if pair >= 2:
                W(sync, "trm", pair - 1)
            for hb in range(2):
                blk = (pair // 2) * GL + (pair % 2) * 2 + hb
                DSER(sync, "r1l")
                I("r1l", sync.dma_start(
                    out=BA(O_R1D + (pair % 2) * 4096 + hb * 2048,
                           0, 128, [[1, 1024]]),
                    in_=AP(tensor=r1_sp, offset=blk * 128 * D,
                           ap=[[D, 128], [1, D]])), 16)
            if pair % 2 == 1:
                b = pair // 2
                W(sync, "aev", (b * 2 + 2) * 8)
                I("asp", sync.dma_start(
                    out=AP(tensor=at_sp, offset=b * 8 * 128 * SL,
                           ap=[[SL, 128], [128 * SL, 8], [1, SL]]),
                    in_=BA(O_ATS, 0, 128, [[512, 8], [1, SL]])), 16)

    @block.vector
    def _(v):
        if PHASES < 5:
            return
        W(v, "wev", 64)
        W(v, "gld", 16 * 22)
        for pair in range(16):
            b, cp = pair // 2, pair % 2
            for hb in range(2):
                bi = pair * 2 + hb
                c = cp * 2 + hb
                yblk = BA(O_Y + 2 * (b * GL * D + c * D), 0, 128, [[1, D]])
                v.reduce_sum(stat1[:, 0:1], yblk, axis=AXL.X)
                W(v, "sqa", 32 + bi + 1)
                v.drain()
                v.tensor_scalar(stat1[:, 2:3], stat1[:, 0:1], 1.0 / D, None,
                                ALU.mult)
                v.tensor_scalar(stat1[:, 4:5], stat1[:, 1:2], 1.0 / D, None,
                                ALU.mult)
                v.drain()
                v.scalar_tensor_tensor(stat1[:, 4:5], stat1[:, 2:3],
                                       stat1[:, 2:3], stat1[:, 4:5],
                                       ALU.mult, ALU.subtract)
                v.drain()
                v.tensor_scalar(stat1[:, 4:5], stat1[:, 4:5], -1.0, EPS,
                                ALU.mult, ALU.add)
                v.drain()
                v.reciprocal(stat1[:, 4:5], stat1[:, 4:5])
                v.drain()
                I("ln1s", v.nop())
                W(v, "sqt", 32 + bi + 1)
                if bi >= 2:
                    W(v, "l2m", pair)
                    W(v, "gtrs", bi - 1)
                zsl = BA(O_ZB + hb * 2048, 0, 128, [[1, D]])
                v.tensor_scalar(zsl, yblk, stat1[:, 2:3], stat1[:, 3:4],
                                ALU.subtract, ALU.mult)
                v.drain()
                v.tensor_tensor(zsl, zsl, lnw_rep[:, :], ALU.mult)
                v.drain()
                v.tensor_tensor(zsl, zsl, lnb_rep[:, :], ALU.add)
                v.drain()
                I("iln", v.nop())
            W(v, "gtrs", pair * 2 + 2)
            W(v, "aev", (pair + 1) * 8)
            atsl = BA(O_ATS + 2 * (cp * 256), 0, 128, [[512, 8], [1, 256]])
            v.tensor_tensor(BA(O_ZB, 0, 128, [[256, 8], [1, 256]]),
                            atsl, atsl, ALU.mult)
            v.drain()
            I("l2e", v.nop())
            W(v, "l2m", pair + 1)
            v.tensor_copy(FA(O_MU3 + pair * 2048, 0, 128, [[1, 256]]),
                          PA(4, 0, 128, 0, [[1, 256]]))
            v.tensor_copy(FA(O_RS3 + pair * 2048, 0, 128, [[1, 256]]),
                          PA(4, 0, 128, 256, [[1, 256]]))
            v.drain()
            I("l2e", v.nop())

    @block.scalar
    def _(sc):
        if PHASES < 5:
            return
        W(sc, "wev", 64)
        for pair in range(16):
            b, cp = pair // 2, pair % 2
            for hb in range(2):
                bi = pair * 2 + hb
                c = cp * 2 + hb
                yblk = BA(O_Y + 2 * (b * GL * D + c * D), 0, 128, [[1, D]])
                if bi >= 1:
                    W(sc, "ln1s", 32 + bi)
                sc.activation(FA(O_SQ2, 0, 128, [[1, D]]), yblk, AF.Square,
                              accum_out=stat1[:, 1:2])
                sc.drain()
                I("sqa", sc.nop())
                W(sc, "ln1s", 32 + bi + 1)
                sc.activation(stat1[:, 3:4], stat1[:, 4:5], AF.Sqrt)
                sc.drain()
                I("sqt", sc.nop())
                W(sc, "iln", bi + 1)
                zsl = BA(O_ZB + hb * 2048, 0, 128, [[1, D]])
                sc.activation(zsl, zsl, AF.Gelu)
                sc.drain()
                I("gev", sc.nop())
            W(sc, "gtrs", pair * 2 + 2)
            if pair >= 1:
                W(sc, "trm", pair)
            for hb in range(2):
                sc.activation(BA(O_GF + 2 * (hb * 128), 0, 128,
                                 [[256, 8], [1, 128]]),
                              PAB(6 + hb, 0, 128, 0, [[128, 8], [1, 128]]),
                              AF.Identity)
            sc.drain()
            I("gfe", sc.nop())
            if pair == 0:
                W(sc, "cst", 96)
            W(sc, "trm", pair + 1)
            if b >= 1:
                W(sc, "asp", 16 * b)
            for dck in range(8):
                bank = dck // 2
                sc.activation(
                    BA(O_ATS + 2 * (dck * 512 + cp * 256), 0, 128, [[1, 256]]),
                    PA(bank, 0, 128, (dck % 2) * 256, [[1, 256]]),
                    AF.Identity,
                    bias=fvec[:, FB["bout"] * 8 + dck:FB["bout"] * 8 + dck + 1])
            sc.drain()
            I("aev", sc.nop(), 8)

    @block.tensor
    def _(t):
        if PHASES < 5:
            return
        W(t, "wgt", 16)
        for pair in range(16):
            b, cp = pair // 2, pair % 2
            for hb in range(2):
                bi = pair * 2 + hb
                W(t, "gev", bi + 1)
                if pair >= 1:
                    W(t, "gfe", pair)
                gsl = O_ZB + hb * 2048
                for dk in range(8):
                    t.transpose(PAB(6 + hb, 0, 128, dk * 128, [[1, 128]]),
                                BA(gsl + 2 * (dk * 128), 0, 128, [[1, 128]]),
                                identb[:, :])
                I("gtrs", t.drain())
            W(t, "gfe", pair + 1)
            W(t, "r1l", 16 * 2 * (pair + 1))
            if pair >= 1:
                W(t, "aev", pair * 8)
            for dck in range(8):
                bank = dck // 2
                for dk in range(8):
                    t.matmul(PA(bank, 0, 128, (dck % 2) * 256, [[1, 256]]),
                             BA(O_WOT + 2 * (dk * 1024 + dck * 128),
                                0, 128, [[1, 128]]),
                             BA(O_GF + 2 * (dk * 256), 0, 128, [[1, 256]]),
                             start=(dk == 0), stop=False)
                for hb in range(2):
                    t.matmul(PA(bank, 0, 128, (dck % 2) * 256 + hb * 128,
                                [[1, 128]]),
                             BA(O_R1D + (pair % 2) * 4096 + hb * 2048
                                + 2 * (dck * 128), 0, 128, [[1, 128]]),
                             identb[:, :], start=False, stop=(hb == 1))
            I("trm", t.drain())
            W(t, "l2e", pair * 2 + 1)
            for dck in range(8):
                t.matmul(PA(4, 0, 128, 0, [[1, 256]]), onesb[:, :],
                         BA(O_ATS + 2 * (dck * 512 + cp * 256), 0, 128,
                            [[1, 256]]),
                         start=(dck == 0), stop=(dck == 7))
            for dck in range(8):
                t.matmul(PA(4, 0, 128, 256, [[1, 256]]), onesb[:, :],
                         BA(O_ZB + 2 * (dck * 256), 0, 128, [[1, 256]]),
                         start=(dck == 0), stop=(dck == 7))
            I("l2m", t.drain())



    MUALL = [[512, 16], [1, 256]]
    @block.vector
    def _(v):
        if PHASES < 5:
            return
        W(v, "l2e", 32)
        v.tensor_scalar(FA(O_MU3, 0, 128, MUALL),
                        FA(O_MU3, 0, 128, MUALL), 1.0 / D, None, ALU.mult)
        v.tensor_scalar(FA(O_RS3, 0, 128, MUALL),
                        FA(O_RS3, 0, 128, MUALL), 1.0 / D, None, ALU.mult)
        v.drain()
        v.tensor_tensor(FA(O_SQ2, 0, 128, [[1, S]]),
                        FA(O_MU3, 0, 128, MUALL),
                        FA(O_MU3, 0, 128, MUALL), ALU.mult)
        v.drain()
        v.tensor_tensor(FA(O_RS3, 0, 128, MUALL),
                        FA(O_RS3, 0, 128, MUALL),
                        FA(O_SQ2, 0, 128, [[1, S]]), ALU.subtract)
        v.drain()
        v.tensor_scalar(FA(O_RS3, 0, 128, MUALL),
                        FA(O_RS3, 0, 128, MUALL), 1.0, EPS, ALU.mult,
                        ALU.add)
        v.drain()
        v.reciprocal(FA(O_RS3, 0, 128, MUALL), FA(O_RS3, 0, 128, MUALL))
        v.drain()
        I("l2f", v.nop())

    @block.scalar
    def _(sc):
        if PHASES < 5:
            return
        W(sc, "l2f", 1)
        sc.activation(FA(O_RS3, 0, 128, MUALL), FA(O_RS3, 0, 128, MUALL),
                      AF.Sqrt)
        sc.drain()
        I("l2f", sc.nop())

    @block.gpsimd
    def _(g):
        if PHASES < 6:
            return
        W(g, "iln", 32)
        DSER(g, "wgt"); I("wgt", _wload(g, fw["ui"], OF["ui"]), 16)
        W(g, "trm", 16)
        W(g, "l2f", 1)
        DSER(g, "wgt"); I("wgt", _wload(g, fw["wo"], OF["wo"]), 16)
        W(g, "x2c", 8)
        DSER(g, "wgt"); I("wgt", _wload(g, fw["wc"], OF["wc"]), 16)
        DSER(g, "wgt"); I("wgt", _wload(g, fw["uc"], OF["uc"]), 16)
        DSER(g, "wgt"); I("wgt", _wload(g, fw["uo"], OF["uo"]), 16)

    @block.sync
    def _(sync):
        if PHASES < 6:
            return
        W(sync, "asp", 16 * 8)
        W(sync, "l2f", 1)
        for t_ in range(T):
            if t_ >= 1:
                W(sync, "x2c", t_)
            DSER(sync, "xsl")
            I("xsl", sync.dma_start(
                out=BA(O_ATD2, 0, 128, [[512, 8], [1, SL]]),
                in_=AP(tensor=at_sp, offset=t_ * 8 * 128 * SL,
                       ap=[[SL, 128], [128 * SL, 8], [1, SL]])), 16)
            W(sync, "x2c", t_ + 1)
            h = (t_ % 2) * 8192
            DSER(sync, "x2d")
            I("x2d", sync.dma_start(
                out=AP(tensor=xl2_sp, offset=t_ * 8 * 128 * SL,
                       ap=[[SL, 128], [128 * SL, 8], [1, SL]]),
                in_=BA(O_X2DB + h, 0, 128, [[512, 8], [1, SL]])), 16)
            for dck in range(8):
                gi = t_ * 8 + dck
                W(sync, "pwe", gi + 1)
                DSER(sync, "pwd")
                I("pwd", sync.dma_start(
                    out=AP(tensor=pwo_sp, offset=(t_ * 8 + dck) * 128 * SL,
                           ap=[[SL, 128], [1, SL]]),
                    in_=FA(O_PWEV + 4 * ((gi % 2) * SL), 0, 128,
                           [[1, SL]])), 16)

    @block.vector
    def _(v):
        if PHASES < 6:
            return
        W(v, "l2f", 2)
        for t_ in range(T):
            W(v, "xsl", 16 * (t_ + 1))
            if t_ >= 2:
                W(v, "x2d", 16 * (t_ - 1))
            h = (t_ % 2) * 8192
            v.tensor_tensor(BA(O_X2DB + h, 0, 128, [[512, 8], [1, SL]]),
                            BA(O_ATD2, 0, 128, [[512, 8], [1, SL]]),
                            FA(O_MU3 + t_ * 4096, 0, 128,
                               [[0, 8], [512, 2], [1, 256]]), ALU.subtract)
            v.drain()
            v.tensor_tensor(BA(O_X2DB + h, 0, 128, [[512, 8], [1, SL]]),
                            BA(O_X2DB + h, 0, 128, [[512, 8], [1, SL]]),
                            FA(O_RS3 + t_ * 4096, 0, 128,
                               [[0, 8], [512, 2], [1, 256]]), ALU.mult)
            v.drain()
            I("x2c", v.nop())
            for dck in range(8):
                gi = t_ * 8 + dck
                W(v, "pwm", gi + 1)
                if gi >= 2:
                    W(v, "pwd", 16 * (gi - 1))
                v.tensor_copy(FA(O_PWEV + 4 * ((gi % 2) * SL), 0, 128,
                                 [[1, SL]]),
                              PA(gi % 2, 0, 128, 0, [[1, SL]]))
                v.drain()
                I("pwe", v.nop())

    @block.tensor
    def _(t):
        if PHASES < 6:
            return
        W(t, "wgt", 16 * 6)
        for t_ in range(T):
            W(t, "x2c", t_ + 1)
            h = (t_ % 2) * 8192
            for dck in range(8):
                gi = t_ * 8 + dck
                bank = gi % 2
                if gi >= 2:
                    W(t, "pwe", gi - 1)
                for dk in range(8):
                    t.matmul(PA(bank, 0, 128, 0, [[1, SL]]),
                             BA(OF["wo"] + 2 * (dk * 1024 + dck * 128),
                                0, 128, [[1, 128]]),
                             BA(O_X2DB + h + 2 * (dk * SL), 0, 128, [[1, SL]]),
                             start=(dk == 0), stop=(dk == 7))
                I("pwm", t.drain())

    @block.vector
    def _(v):
        if PHASES < 6:
            return
        W(v, "pwm", 64)
        W(v, "x2c", 8)
        W(v, "cst", 96)
        for dk in range(8):
            v.tensor_copy(FA(O_CF + 4 * (dk * SL), 0, 128, [[1, SL]]),
                          AP(tensor=fvec, offset=FB["inc"] * 8 + dk,
                             ap=[[48, 128], [0, SL]]))
        v.drain()
        v.tensor_copy(BA(O_CBF, 0, 128, [[1, 8 * SL]]),
                      FA(O_CF, 0, 128, [[1, 8 * SL]]))
        v.drain()
        I("cbfs", v.nop())

    @block.sync
    def _(sync):
        if PHASES < 6:
            return
        W(sync, "x2d", 16 * 8)
        for t_ in range(T):
            if t_ >= 2:
                W(sync, "gmm", (t_ - 1) * 8)
            DSER(sync, "x2l")
            I("x2l", sync.dma_start(
                out=BA(O_X2DB + (t_ % 2) * 8192, 0, 128, [[512, 8], [1, SL]]),
                in_=AP(tensor=xl2_sp, offset=t_ * 8 * 128 * SL,
                       ap=[[SL, 128], [128 * SL, 8], [1, SL]])), 16)

    @block.tensor
    def _(t):
        if PHASES < 6:
            return
        W(t, "wgt", 16 * 8)
        for t_ in range(T):
            W(t, "x2l", 16 * (t_ + 1))
            W(t, "cbfs", t_ + 1)
            h = (t_ % 2) * 8192
            for dck in range(8):
                gi = t_ * 8 + dck
                b0 = (dck % 2) * 3
                if gi >= 2:
                    W(t, "cup", gi - 1)
                for gname, wn, un, bank in (("f", "wf", "uf", b0),
                                            ("i", "wi", "ui", b0 + 1),
                                            ("g", "wc", "uc", b0 + 2)):
                    for dk in range(8):
                        t.matmul(PA(bank, 0, 128, 0, [[1, SL]]),
                                 BA(OF[wn] + 2 * (dk * 1024 + dck * 128),
                                    0, 128, [[1, 128]]),
                                 BA(O_X2DB + h + 2 * (dk * SL), 0, 128,
                                    [[1, SL]]),
                                 start=(dk == 0), stop=False)
                    for dk in range(8):
                        t.matmul(PA(bank, 0, 128, 0, [[1, SL]]),
                                 BA(OF[un] + 2 * (dk * 1024 + dck * 128),
                                    0, 128, [[1, 128]]),
                                 BA(O_CBF + 2 * (dk * SL), 0, 128, [[1, SL]]),
                                 start=False, stop=(dk == 7))
                I("gmm", t.drain())

    @block.scalar
    def _(sc):
        if PHASES < 6:
            return
        W(sc, "cst", 96)
        W(sc, "pwd", 16 * 64)
        for t_ in range(T):
            for dck in range(8):
                gi = t_ * 8 + dck
                b0 = (dck % 2) * 3
                W(sc, "gmm", gi + 1)
                if gi >= 1:
                    W(sc, "cup", gi)
                sc.activation(FA(O_GTF, 0, 128, [[1, SL]]),
                              PA(b0, 0, 128, 0, [[1, SL]]), AF.Sigmoid,
                              bias=fvec[:, FB["bf"] * 8 + dck:
                                        FB["bf"] * 8 + dck + 1])
                sc.activation(FA(O_GTI, 0, 128, [[1, SL]]),
                              PA(b0 + 1, 0, 128, 0, [[1, SL]]), AF.Sigmoid,
                              bias=fvec[:, FB["bi"] * 8 + dck:
                                        FB["bi"] * 8 + dck + 1])
                sc.activation(FA(O_GTG, 0, 128, [[1, SL]]),
                              PA(b0 + 2, 0, 128, 0, [[1, SL]]), AF.Tanh,
                              bias=fvec[:, FB["bc"] * 8 + dck:
                                        FB["bc"] * 8 + dck + 1])
                sc.drain()
                I("gac", sc.nop())

    @block.vector
    def _(v):
        if PHASES < 6:
            return
        for t_ in range(T):
            for dck in range(8):
                gi = t_ * 8 + dck
                W(v, "gac", gi + 1)
                csl = FA(O_CF + 4 * (dck * SL), 0, 128, [[1, SL]])
                v.tensor_tensor(csl, FA(O_GTF, 0, 128, [[1, SL]]), csl,
                                ALU.mult)
                v.tensor_tensor(FA(O_CT1, 0, 128, [[1, SL]]),
                                FA(O_GTI, 0, 128, [[1, SL]]),
                                FA(O_GTG, 0, 128, [[1, SL]]), ALU.mult)
                v.drain()
                v.tensor_tensor(csl, csl, FA(O_CT1, 0, 128, [[1, SL]]),
                                ALU.add)
                v.drain()
                I("cup", v.nop())
            v.tensor_copy(BA(O_CBF, 0, 128, [[1, 8 * SL]]),
                          FA(O_CF, 0, 128, [[1, 8 * SL]]))
            v.drain()
            I("cbfs", v.nop())

    @block.tensor
    def _(t):
        if PHASES < 7:
            return
        W(t, "cbfs", 9)
        W(t, "wgt", 16 * 9)
        for dck in range(8):
            bank = 6 + dck % 2
            if dck >= 2:
                W(t, "uoe", dck - 1)
            for dk in range(8):
                t.matmul(PA(bank, 0, 128, 0, [[1, SL]]),
                         BA(OF["uo"] + 2 * (dk * 1024 + dck * 128), 0, 128,
                            [[1, 128]]),
                         BA(O_CBF + 2 * (dk * SL), 0, 128, [[1, SL]]),
                         start=(dk == 0), stop=(dk == 7))
            I("uom", t.drain())

    @block.vector
    def _(v):
        if PHASES < 7:
            return
        for dck in range(8):
            W(v, "uom", dck + 1)
            v.tensor_copy(FA(O_UOO + 4 * (dck * SL), 0, 128, [[1, SL]]),
                          PA(6 + dck % 2, 0, 128, 0, [[1, SL]]))
            v.drain()
            I("uoe", v.nop())

    @block.sync
    def _(sync):
        if PHASES < 7:
            return
        W(sync, "gmm", 64)
        W(sync, "pwd", 16 * 64)
        for t_ in range(T):
            if t_ >= 1:
                W(sync, "hsl", 9 * t_)
            DSER(sync, "pwl")
            I("pwl", sync.dma_start(
                out=FA(O_PWDB, 0, 128, [[1, 8 * SL]]),
                in_=AP(tensor=pwo_sp, offset=t_ * 8 * 128 * SL,
                       ap=[[SL, 128], [128 * SL, 8], [1, SL]])), 16)
            if t_ >= 1:
                W(sync, "oev", t_)
                DSER(sync, "od")
                I("od", sync.dma_start(
                    out=AP(tensor=out_t, offset=(t_ - 1) * 8 * 128 * SL,
                           ap=[[SL, 128], [128 * SL, 8], [1, SL]]),
                    in_=FA(O_OSL, 0, 128, [[1, 8 * SL]])), 16)
            DSER(sync, "atl")
            I("atl", sync.dma_start(
                out=BA(O_ATD2, 0, 128, [[512, 8], [1, SL]]),
                in_=AP(tensor=at_sp, offset=t_ * 8 * 128 * SL,
                       ap=[[SL, 128], [128 * SL, 8], [1, SL]])), 16)

    @block.vector
    def _(v):
        if PHASES < 7:
            return
        W(v, "uoe", 8)
        for t_ in range(T):
            W(v, "pwl", 16 * (t_ + 1))
            v.tensor_tensor(FA(O_PWDB, 0, 128, [[1, 8 * SL]]),
                            FA(O_PWDB, 0, 128, [[1, 8 * SL]]),
                            FA(O_UOO, 0, 128, [[1, 8 * SL]]), ALU.add)
            v.drain()
            I("hsl", v.nop())
            W(v, "hsl", t_ * 9 + 9)
            W(v, "atl", 16 * (t_ + 1))
            if t_ >= 1:
                W(v, "od", 16 * t_)
            v.tensor_tensor(FA(O_OSL, 0, 128, [[1, 8 * SL]]),
                            BA(O_ATD2, 0, 128, [[1, 8 * SL]]),
                            BA(O_HSL, 0, 128, [[1, 8 * SL]]), ALU.add)
            v.drain()
            I("oev", v.nop())

    @block.scalar
    def _(sc):
        if PHASES < 7:
            return
        W(sc, "cst", 96)
        for t_ in range(T):
            W(sc, "hsl", t_ * 9 + 1)
            for dck in range(8):
                sc.activation(BA(O_HSL + 2 * (dck * SL), 0, 128, [[1, SL]]),
                              FA(O_PWDB + 4 * (dck * SL), 0, 128, [[1, SL]]),
                              AF.Tanh,
                              bias=fvec[:, FB["bo"] * 8 + dck:
                                        FB["bo"] * 8 + dck + 1])
            sc.drain()
            I("hsl", sc.nop(), 8)

    @block.sync
    def _(sync):
        if PHASES < 7:
            return
        W(sync, "oev", T)
        DSER(sync, "od")
        I("od", sync.dma_start(
            out=AP(tensor=out_t, offset=(T - 1) * 8 * 128 * SL,
                   ap=[[SL, 128], [128 * SL, 8], [1, SL]]),
            in_=FA(O_OSL, 0, 128, [[1, 8 * SL]])), 16)
        W(sync, "od", 16 * 8)

    if PHASES < 7:
        @block.sync
        def _(sync):
            for nm, val in PHASE_END[PHASES]:
                W(sync, nm, val)
            for t_ in range(T):
                I("od", sync.dma_start(
                    out=AP(tensor=out_t, offset=t_ * 8 * 128 * SL,
                           ap=[[SL, 128], [128 * SL, 8], [1, SL]]),
                    in_=FA(0, 0, 128, [[1, 8 * SL]])), 16)
            W(sync, "od", 16 * 8)

    blkcm.__exit__(None, None, None)
    for cm in ctxs:
        cm.__exit__(None, None, None)
    return nc


_NC_CACHE = {}


def kernel(**inputs):
    key = "v1-%d" % PHASES
    if key not in _NC_CACHE:
        _NC_CACHE[key] = build_nc()
    nc = _NC_CACHE[key]
    f32 = lambda v: np.ascontiguousarray(np.asarray(v, dtype=np.float32))
    x = f32(inputs["L0_attn_x"])
    A = f32(inputs["L0_attn_A"])
    Bm = f32(inputs["L0_attn_Bm"])
    Cm = f32(inputs["L0_attn_Cm"])
    Dm = f32(inputs["L0_attn_Dm"])
    in_maps = []
    for k in range(R):
        m = {
            "x": np.ascontiguousarray(x[:, k * SL:(k + 1) * SL, :]),
            "attn_A": np.ascontiguousarray(A[k * DCH:(k + 1) * DCH]),
            "attn_Bm": np.ascontiguousarray(Bm[k * DCH:(k + 1) * DCH]),
            "attn_Cm": np.ascontiguousarray(Cm[k * DCH:(k + 1) * DCH]),
            "attn_Dm": np.ascontiguousarray(Dm[k * DCH:(k + 1) * DCH]),
            "ln_w": f32(inputs["L0_attn_ln_w"]),
            "ln_b": f32(inputs["L0_attn_ln_b"]),
            "W_out": f32(inputs["L0_attn_W_out"]),
            "b_out": f32(inputs["L0_attn_b_out"]),
        }
        for nm in ("wf", "uf", "wi", "ui", "wc", "uc", "wo", "uo"):
            m[nm + "T"] = np.ascontiguousarray(
                np.asarray(inputs["L0_ffn_" + nm], dtype=np.float32).T)
        m["WoutT"] = np.ascontiguousarray(
            np.asarray(inputs["L0_attn_W_out"], dtype=np.float32).T)
        for nm in ("bf", "bi", "bc", "bo", "in_c"):
            m[nm] = f32(inputs["L0_ffn_" + nm])
        in_maps.append(m)
    res = run_bass_kernel_spmd(nc, in_maps, core_ids=list(range(R)))
    out = np.empty((B, S, D), dtype=np.float32)
    for k in range(R):
        o = res.results[k]["out"]
        out[:, k * SL:(k + 1) * SL, :] = np.transpose(
            o, (0, 3, 1, 2)).reshape(B, SL, D)
    return out


# version 07
# speedup vs baseline: 1.0126x; 1.0126x over previous
import numpy as np
import concourse.bass as bass
import concourse.mybir as mybir
from concourse.bass_utils import run_bass_kernel_spmd
from concourse.aot_env import allow_ici_all_to_all

F32 = mybir.dt.float32
BF16 = mybir.dt.bfloat16
AF = mybir.ActivationFunctionType
ALU = mybir.AluOpType
AXL = mybir.AxisListType
AP = bass.AP

R, B, S, D, NST, L = 8, 8, 4096, 1024, 16, 128
SL = S // R
GL = SL // L
GG = S // L
CB = GL * B
DCH = D // R
T = B
EPS = 1e-5
KB = 1024
NRND = 11


PHASES = 7



def _ul(r):
    return (1 << r) if r <= 6 else 0


def _cl(r):
    return (1 << r) if r <= 6 else (1 if r == 7 else 0)


def chain_waves():
    waves = []
    bank = 0
    for r in range(NRND):
        colw = _ul(r) + _cl(r) + 32
        per = max(1, min(8, 512 // colw))
        ch0 = 0
        while ch0 < 32:
            nch = min(per, 32 - ch0)
            waves.append((r, ch0, nch, colw, bank))
            bank ^= 1
            ch0 += nch
    return waves


CWAVES = chain_waves()
RSTART = {}
for _i, (_r, _c0, _n, _cw, _bk) in enumerate(CWAVES):
    RSTART.setdefault(_r, _i)

PHASE_END = {
    1: [("r1d", 16 * 32), ("ln1", 32), ("cst", 16 * 6), ("gld", 16 * 22)],
    2: [("che", 1 + len(CWAVES)), ("chm", 2 + len(CWAVES)),
        ("r1d", 16 * 32), ("cst", 16 * 6)],
    2.2: [("kbe", 2)],
    2.21: [("kbm", 1)],
    2.3: [("kbe", 6), ("kbm", 4)],
    2.4: [("kbe", 10), ("kbm", 8)],
    2.45: [("kbe", 14), ("kbm", 12)],
    2.5: [("kbe", 46), ("r1d", 16 * 32), ("cst", 16 * 6)],
    3: [("yev", 64), ("sev", 16), ("a21", 1), ("r1d", 16 * 32),
        ("cst", 16 * 6)],
    4: [("wev", 64), ("r1d", 16 * 32), ("cst", 16 * 6)],
    5: [("l2q", 16), ("asp", 16 * 8), ("aev", 128), ("gev", 32),
        ("wgt", 16 * 4)],
    6: [("gmm", 64), ("cbfs", 9), ("pwd", 16 * 64), ("x2d", 16 * 8),
        ("cup", 64), ("gac", 64), ("wgt", 16 * 8)],
}


def build_nc(dump=()):
    nc = bass.Bass("TRN2", target_bir_lowering=False, debug=False)

    x_in = nc.dram_tensor("x", [B, SL, D], F32, kind="ExternalInput")
    a_in = nc.dram_tensor("attn_A", [DCH, NST, NST], F32, kind="ExternalInput")
    bm_in = nc.dram_tensor("attn_Bm", [DCH, NST], F32, kind="ExternalInput")
    cm_in = nc.dram_tensor("attn_Cm", [DCH, NST], F32, kind="ExternalInput")
    dm_in = nc.dram_tensor("attn_Dm", [DCH], F32, kind="ExternalInput")
    lnw_in = nc.dram_tensor("ln_w", [D], F32, kind="ExternalInput")
    lnb_in = nc.dram_tensor("ln_b", [D], F32, kind="ExternalInput")
    wout_in = nc.dram_tensor("W_out", [D, D], F32, kind="ExternalInput")
    bout_in = nc.dram_tensor("b_out", [D], F32, kind="ExternalInput")
    fw = {nm: nc.dram_tensor(nm + "T", [D, D], F32, kind="ExternalInput")
          for nm in ("wf", "uf", "wi", "ui", "wc", "uc", "wo", "uo")}
    woutT_in = nc.dram_tensor("WoutT", [D, D], F32, kind="ExternalInput")
    fv = {nm: nc.dram_tensor(nm, [D], F32, kind="ExternalInput")
          for nm in ("bf", "bi", "bc", "bo", "in_c")}
    out_t = nc.dram_tensor("out", [T, 8, 128, SL], F32, kind="ExternalOutput")

    KSEC = 128 * DCH * 128
    USEC = 128 * DCH * NST
    WSEC = 128 * 32 * 128
    BLOB = KSEC + USEC + WSEC
    blob_loc = nc.dram_tensor("blob_loc", [BLOB], BF16)
    blob_ag = nc.dram_tensor("blob_ag", [R * BLOB], BF16)
    SSEC = 128 * 32 * GL * B
    s_blob = nc.dram_tensor("s_blob", [R * SSEC], BF16)
    s_a2a = nc.dram_tensor("s_a2a", [R * SSEC], BF16)
    h_blob = nc.dram_tensor("h_blob", [R * SSEC], BF16)
    h_a2a = nc.dram_tensor("h_a2a", [R * SSEC], BF16)
    r1_sp = nc.dram_tensor("r1_sp", [32 * 128 * D], BF16)
    at_sp = nc.dram_tensor("at_sp", [T * 8 * 128 * SL], BF16)
    xl2_sp = nc.dram_tensor("xl2_sp", [T * 8 * 128 * SL], BF16)
    pwo_sp = nc.dram_tensor("pwo_sp", [T * 8 * 128 * SL], F32)

    dbg = {}
    def DBG(name, shape, dt=F32):
        if name in dump and name not in dbg:
            dbg[name] = nc.dram_tensor("dbg_" + name, shape, dt,
                                       kind="ExternalOutput")
        return dbg.get(name)

    sbA = nc.alloc_sbuf_tensor
    identb = sbA("identb", [128, 128], BF16)
    revb = sbA("revb", [128, 128], BF16)
    widen = sbA("widen", [128, 256], BF16)
    onesb = sbA("onesb", [128, 128], BF16)
    lnw_rep = sbA("lnw_rep", [128, D], BF16)
    lnb_rep = sbA("lnb_rep", [128, D], BF16)
    fvec = sbA("fvec", [128, 6 * 8], F32)
    FB = {"bout": 0, "bf": 1, "bi": 2, "bc": 3, "bo": 4, "inc": 5}
    xstg = sbA("xstg", [128, 2 * D], F32)
    stat1 = sbA("stat1", [128, 16], F32)
    kstr = sbA("kstr", [128, 2 * 16 * 128], BF16)
    wstr = sbA("wstr", [128, 2 * 4 * 128], BF16)
    hstr = sbA("hstr", [128, 2 * 4 * CB], BF16)
    vscr = sbA("vscr", [128, 64], F32)
    dmrow = sbA("dmrow", [1, DCH], F32)

    ARNB = 175 * KB
    arena = nc.alloc_sbuf_tensor("arena", [128, ARNB // 4], F32)
    arb = arena.bitcast(BF16)
    PSF = ARNB // 4
    PSB = ARNB // 2

    def BA(offb, part0, nparts, dims):
        assert offb % 2 == 0
        return AP(tensor=arb, offset=part0 * PSB + offb // 2,
                  ap=[[PSB, nparts]] + [list(d) for d in dims])

    def FA(offb, part0, nparts, dims):
        assert offb % 4 == 0
        return AP(tensor=arena, offset=part0 * PSF + offb // 4,
                  ap=[[PSF, nparts]] + [list(d) for d in dims])

    O_XL1 = 0
    O_PC, O_PN, O_PTC, O_PTN = 64 * KB, 65 * KB, 66 * KB, 67 * KB
    O_UST, O_CST, O_PRD, O_ALP = 68 * KB, 76 * KB, 85 * KB, 167 * KB
    O_BMB, O_KRT, O_CMS = 98 * KB, 98 * KB + 256, 98 * KB + 1024
    O_KLOC, O_ULOC = 99 * KB, 131 * KB
    O_Y = 64 * KB
    O_SST = 99 * KB
    O_MU3, O_RS3 = O_Y, O_Y + 1024
    O_USTR = 135 * KB
    O_SQ1, O_R1T = 139 * KB, 143 * KB
    O_H, O_H2 = 135 * KB, 151 * KB
    O_WOT, O_WSTG = 135 * KB, 151 * KB
    O_ZB, O_GF, O_R1D, O_ATS = 151 * KB, 155 * KB, 159 * KB, 167 * KB
    O_SQ2 = 131 * KB
    OF = {"wf": 0, "uf": 16 * KB, "wi": 32 * KB, "ui": 48 * KB,
          "wc": 64 * KB, "uc": 80 * KB, "wo": 135 * KB, "uo": 99 * KB}
    O_X2DB, O_ATD2, O_PWEV = 151 * KB, 167 * KB, 123 * KB
    O_ATD2B = 56 * KB
    O_X2DBB = 48 * KB
    O_CF, O_CBF = 135 * KB, 115 * KB
    O_GTF, O_GTI, O_GTG, O_CT1, O_UOS = (123 * KB, 125 * KB, 127 * KB,
                                         129 * KB, 131 * KB)
    O_PWDB, O_UOO, O_HSL, O_OSL = 64 * KB, 80 * KB, 151 * KB, 135 * KB

    pbk = [nc.alloc_psum_tensor("pb%d" % i, [128, 512], F32) for i in range(8)]
    pbb = [p.bitcast(BF16) for p in pbk]

    def PA(bank, part0, nparts, offe, dims):
        return AP(tensor=pbk[bank], offset=part0 * 512 + offe,
                  ap=[[512, nparts]] + [list(d) for d in dims])

    def PAB(bank, part0, nparts, offe, dims):
        return AP(tensor=pbb[bank], offset=part0 * 1024 + offe,
                  ap=[[1024, nparts]] + [list(d) for d in dims])

    class Sem:
        def __init__(self, h):
            self.h = h
            self.n = 0

    sems = {}
    ctxs = []
    for _n in ("x cst seed sqa ln1s sqt ln1 r1c r1d chm che kbm kbe bd ag1 "
               "ust umm sev sd a21 kst kmm yev h0 pfm pfa hsh hd a22 hld wmm "
               "wev wld iln trm gev aev l2m l2e asp r1l l2f l2p l2q xsl x2c "
               "x2d xpl pwm "
               "pwe pwd x2l gmm gac cup cbfs uom uoe pwl atl hsl oev od "
               "wcst pini gtrs gfe gld wgt").split():
        cm = nc.semaphore("s_" + _n)
        sems[_n] = Sem(cm.__enter__())
        ctxs.append(cm)

    def W(eng, name, val):
        eng.wait_ge(sems[name].h, val)

    def I(name, instr, k=1):
        instr.then_inc(sems[name].h, k)
        sems[name].n += k
        return instr

    def DSER(eng, name):
        if sems[name].n:
            eng.wait_ge(sems[name].h, sems[name].n)

    blkcm = nc.Block()
    block = blkcm.__enter__()

    @block.vector
    def _(v):
        for i in range(8):
            v.memset(pbk[i][:, :], 0.0)
        v.memset(BA(O_UST, 0, 128, [[1, 32 * 128]]), 0.0)
        v.memset(BA(O_CST, 0, 128, [[1, 32 * 132]]), 0.0)
        v.memset(BA(O_BMB, 0, 128, [[1, 32]]), 0.0)
        v.memset(BA(O_CMS, 0, 128, [[1, 32]]), 0.0)
        v.memset(BA(O_PC, 0, 128, [[1, 512]]), 0.0)
        v.drain()
        I("pini", v.nop())

    @block.gpsimd
    def _(g):
        g.memset(onesb[:, :], 1.0)
        g.memset(hstr[:, :], 0.0)
        g.drain()
        g.affine_select(identb[:, :], onesb[:, :], [[-1, 128]], ALU.is_equal,
                        0.0, base=0, channel_multiplier=1)
        g.affine_select(revb[:, :], onesb[:, :], [[1, 128]], ALU.is_equal,
                        0.0, base=-127, channel_multiplier=1)
        g.memset(widen[:, :], 0.0)
        g.drain()
        g.tensor_copy(widen[:, 0:128], revb[:, :])
        g.drain()
        I("seed", g.drain())
        W(g, "pini", 1)
        for q in range(4):
            src = AP(tensor=a_in, offset=256 * q,
                     ap=[[16, 16], [1024, 32], [1, 16]])
            DSER(g, "gld")
            I("gld", g.dma_start(out=BA(O_PC, 32 * q, 16,
                                        [[16, 32], [1, 16]]), in_=src), 16)
        with nc.allow_non_contiguous_dma(reason="tiny seed scatters"):
            for q in range(4):
                src = AP(tensor=bm_in, offset=16 * q, ap=[[1, 16], [64, 32]])
                DSER(g, "gld")
                I("gld", g.dma_start(out=BA(O_BMB, 32 * q, 16, [[1, 32]]),
                                     in_=src), 16)
            for q in range(4):
                src = AP(tensor=bm_in, offset=16 * q, ap=[[1, 16], [64, 32]])
                DSER(g, "gld")
                I("gld", g.dma_start(out=BA(O_UST, 32 * q, 16, [[128, 32]]),
                                     in_=src), 16)
            for q in range(4):
                src = AP(tensor=cm_in, offset=16 * q, ap=[[1, 16], [64, 32]])
                DSER(g, "gld")
                I("gld", g.dma_start(out=BA(O_CST, 32 * q, 16, [[132, 32]]),
                                     in_=src), 16)
        with nc.allow_non_contiguous_dma(reason="tiny Dm scatter"):
            for q in range(4):
                src = AP(tensor=dm_in, offset=q, ap=[[0, 16], [4, 32]])
                DSER(g, "gld")
                I("gld", g.dma_start(out=BA(O_CMS, 32 * q, 16, [[1, 32]]),
                                     in_=src), 16)
        repw = AP(tensor=lnw_in, offset=0, ap=[[0, 128], [1, D]])
        DSER(g, "gld")
        I("gld", g.dma_start(out=lnw_rep[:, :], in_=repw), 16)
        repb = AP(tensor=lnb_in, offset=0, ap=[[0, 128], [1, D]])
        DSER(g, "gld")
        I("gld", g.dma_start(out=lnb_rep[:, :], in_=repb), 16)

    @block.sync
    def _(sync):
        for nm in ("bout", "bf", "bi", "bc", "bo", "inc"):
            tens = {"bout": bout_in, "bf": fv["bf"], "bi": fv["bi"],
                    "bc": fv["bc"], "bo": fv["bo"], "inc": fv["in_c"]}[nm]
            src = AP(tensor=tens, offset=0, ap=[[1, 128], [128, 8]])
            dstv = AP(tensor=fvec, offset=FB[nm] * 8, ap=[[48, 128], [1, 8]])
            with nc.allow_non_contiguous_dma(reason="tiny vec transpose"):
                I("cst", sync.dma_start(out=dstv, in_=src), 16)
        def r1spill(blk):
            W(sync, "r1c", blk + 1)
            DSER(sync, "r1d")
            I("r1d", sync.dma_start(
                out=AP(tensor=r1_sp, offset=blk * 128 * D,
                       ap=[[D, 128], [1, D]]),
                in_=BA(O_R1T + (blk % 2) * 2 * D, 0, 128, [[1, D]])), 16)
        for blk in range(32):
            b, c = blk // GL, blk % GL
            if blk >= 2:
                W(sync, "ln1", blk - 1)
                r1spill(blk - 2)
            DSER(sync, "x")
            I("x", sync.dma_start(
                out=xstg[:, (blk % 2) * D:(blk % 2) * D + D],
                in_=x_in[b, c * 128:(c + 1) * 128, :]), 16)
        r1spill(30)
        r1spill(31)

    @block.scalar
    def _(sc):
        for blk in range(32):
            W(sc, "x", 16 * (blk + 1))
            xt = xstg[:, (blk % 2) * D:(blk % 2) * D + D]
            sc.activation(FA(O_SQ1, 0, 128, [[1, D]]), xt, AF.Square,
                          accum_out=stat1[:, 1:2])
            sc.drain()
            I("sqa", sc.nop())
            W(sc, "ln1s", blk + 1)
            sc.activation(stat1[:, 3:4], stat1[:, 4:5], AF.Sqrt)
            sc.drain()
            I("sqt", sc.nop())
            W(sc, "ln1", blk + 1)
            if blk >= 2:
                W(sc, "r1d", 16 * (blk - 1))
            sc.activation(BA(O_R1T + (blk % 2) * 2 * D, 0, 128, [[1, D]]),
                          xt, AF.Identity, bias=stat1[:, 5:6],
                          scale=stat1[:, 6:7])
            sc.drain()
            I("r1c", sc.nop())

    @block.vector
    def _(v):
        for blk in range(32):
            b, c = blk // GL, blk % GL
            W(v, "x", 16 * (blk + 1))
            xt = xstg[:, (blk % 2) * D:(blk % 2) * D + D]
            v.reduce_sum(stat1[:, 0:1], xt, axis=AXL.X)
            W(v, "sqa", blk + 1)
            v.drain()
            v.tensor_scalar(stat1[:, 2:3], stat1[:, 0:1], 1.0 / D, None,
                            ALU.mult)
            v.tensor_scalar(stat1[:, 4:5], stat1[:, 1:2], 1.0 / D, None,
                            ALU.mult)
            v.drain()
            v.scalar_tensor_tensor(stat1[:, 4:5], stat1[:, 2:3],
                                   stat1[:, 2:3], stat1[:, 4:5],
                                   ALU.mult, ALU.subtract)
            v.drain()
            v.tensor_scalar(stat1[:, 4:5], stat1[:, 4:5], -1.0, EPS,
                            ALU.mult, ALU.add)
            v.drain()
            v.reciprocal(stat1[:, 4:5], stat1[:, 4:5])
            v.drain()
            I("ln1s", v.nop())
            W(v, "sqt", blk + 1)
            v.tensor_scalar(stat1[:, 5:6], stat1[:, 2:3], stat1[:, 3:4],
                            -1.0, ALU.mult, ALU.mult)
            v.tensor_scalar(stat1[:, 6:7], stat1[:, 3:4], 1.0, None, ALU.add)
            v.drain()
            v.tensor_scalar(BA(O_XL1 + 2 * (c * B + b), 0, 128, [[CB, D]]),
                            xt, stat1[:, 2:3], stat1[:, 3:4],
                            ALU.subtract, ALU.mult)
            v.drain()
            I("ln1", v.nop())

    @block.tensor
    def _(t):
        if PHASES < 2:
            return
        W(t, "gld", 16 * 16)
        W(t, "seed", 1)
        W(t, "pini", 1)
        for chhi in range(32):
            bank = chhi % 2
            for q in range(4):
                t.transpose(PAB(bank, 32 * q, 16, 16 * (chhi // 2), [[1, 16]]),
                            BA(O_PC + 32 * chhi, 32 * q, 16, [[1, 16]]),
                            identb[32 * q:32 * q + 16, 32 * q:32 * q + 16],
                            tile_position=(32 * q, 32 * q))
        I("chm", t.drain(), 2)
        for widx, (r, ch0, nch, colw, bank) in enumerate(CWAVES):
            ul, cl = _ul(r), _cl(r)
            pco = O_PC if r % 2 == 0 else O_PN
            pto = O_PTC if r % 2 == 0 else O_PTN
            need = 1 + RSTART[r]
            if widx >= 2:
                need = max(need, 1 + widx - 1)
            W(t, "che", need)
            for ci in range(nch):
                chhi = ch0 + ci
                cb0 = ci * colw
                for q in range(4):
                    tp = (32 * q, 32 * q)
                    ptt = BA(pto + 32 * chhi, 32 * q, 16, [[1, 16]])
                    pti = BA(pco + 32 * chhi, 32 * q, 16, [[1, 16]])
                    if ul:
                        t.matmul(PA(bank, 32 * q, 16, cb0, [[1, ul]]),
                                 ptt, BA(O_UST + 256 * chhi, 32 * q, 16,
                                         [[1, ul]]),
                                 start=True, stop=True, tile_position=tp)
                    if cl:
                        t.matmul(PA(bank, 32 * q, 16, cb0 + ul, [[1, cl]]),
                                 pti, BA(O_CST + 264 * chhi, 32 * q, 16,
                                         [[1, cl]]),
                                 start=True, stop=True, tile_position=tp)
                    t.matmul(PA(bank, 32 * q, 16, cb0 + ul + cl, [[1, 16]]),
                             ptt, pti, start=True, stop=True, tile_position=tp)
                    t.matmul(PA(bank, 32 * q, 16, cb0 + ul + cl + 16,
                                [[1, 16]]),
                             pti, ptt, start=True, stop=True, tile_position=tp)
            I("chm", t.drain())

    @block.vector
    def _(v):
        if PHASES < 2:
            return
        W(v, "chm", 2)
        for bank in range(2):
            v.tensor_copy(BA(O_PTC + 32 * bank, 0, 128, [[32, 16], [1, 16]]),
                          PAB(bank, 0, 128, 0, [[16, 16], [1, 16]]))
        v.drain()
        I("che", v.nop())
        for widx, (r, ch0, nch, colw, bank) in enumerate(CWAVES):
            ul, cl = _ul(r), _cl(r)
            pno = O_PN if r % 2 == 0 else O_PC
            ptno = O_PTN if r % 2 == 0 else O_PTC
            W(v, "chm", 3 + widx)
            if ul:
                v.tensor_copy(
                    BA(O_UST + 2 * (ch0 * 128 + (1 << r)), 0, 128,
                       [[128, nch], [1, ul]]),
                    PA(bank, 0, 128, 0, [[colw, nch], [1, ul]]))
            if cl:
                cdst = (1 << r) if r <= 6 else 128
                v.tensor_copy(
                    BA(O_CST + 2 * (ch0 * 132 + cdst), 0, 128,
                       [[132, nch], [1, cl]]),
                    PA(bank, 0, 128, ul, [[colw, nch], [1, cl]]))
            v.tensor_copy(BA(pno + 32 * ch0, 0, 128, [[16, nch], [1, 16]]),
                          PA(bank, 0, 128, ul + cl, [[colw, nch], [1, 16]]))
            v.tensor_copy(BA(ptno + 32 * ch0, 0, 128, [[16, nch], [1, 16]]),
                          PA(bank, 0, 128, ul + cl + 16,
                             [[colw, nch], [1, 16]]))
            if r >= 6:
                v.tensor_copy(
                    BA(O_ALP + 2 * ((r - 6) * 512 + ch0 * 16), 0, 128,
                       [[16, nch], [1, 16]]),
                    PA(bank, 0, 128, ul + cl + 16, [[colw, nch], [1, 16]]))
            v.drain()
            I("che", v.nop())

    CH_DONE = 1 + len(CWAVES)

    @block.vector
    def _(v):
        if PHASES < 2.2:
            return
        W(v, "che", CH_DONE)
        v.tensor_tensor(BA(O_PRD, 0, 128, [[128, 32], [1, 128]]),
                        BA(O_CST, 0, 128, [[132, 32], [1, 128]]),
                        BA(O_BMB, 0, 128, [[1, 32], [0, 128]]),
                        ALU.mult)
        v.drain()
        I("kbe", v.nop())
        W(v, "gld", 16 * 20)
        v.scalar_tensor_tensor(BA(O_PRD, 0, 128, [[128, 32]]),
                               BA(O_CMS, 0, 128, [[1, 32]]), 1.0 / 16,
                               BA(O_PRD, 0, 128, [[128, 32]]),
                               ALU.mult, ALU.add)
        v.drain()
        I("kbe", v.nop())

    O_UBSCR = 64 * KB

    @block.tensor
    def _(t):
        if PHASES < 2.21:
            return
        W(t, "kbe", 2)
        NGRP_KR = 1 if PHASES < 2.3 else 4
        for grp in range(NGRP_KR):
            bank = 2 + (grp % 2)
            if grp >= 2:
                W(t, "kbe", grp + 1)
            for q in range(4):
                for k8 in range(8):
                    ci = k8 * 4 + q
                    chl = grp * 32 + ci
                    chhi = chl // 4
                    lhsT = AP(tensor=arb,
                              offset=32 * q * PSB + (O_PRD // 2) + chhi * 128,
                              ap=[[PSB, 16], [1, 128]])
                    t.matmul(PA(bank, 0, 128, ci, [[1, 1]]),
                             lhsT, onesb[32 * q:32 * q + 16, 0:1],
                             start=True, stop=True,
                             tile_position=(32 * q, 0))
                t.drain()
            I("kbm", t.drain())
        if PHASES < 2.4:
            return
        for grp in range(4):
            bank = 4 + (grp % 2)
            if grp >= 2:
                W(t, "kbe", grp + 5)
            for q in range(4):
                for k8 in range(8):
                    ci = k8 * 4 + q
                    chl = grp * 32 + ci
                    chhi = chl // 4
                    lhsT = AP(tensor=arb,
                              offset=32 * q * PSB + (O_UST // 2) + chhi * 128,
                              ap=[[PSB, 16], [1, 128]])
                    t.matmul(PA(bank, 0, 128, 16 * ci, [[1, 16]]),
                             lhsT, identb[32 * q:32 * q + 16,
                                          32 * q:32 * q + 16],
                             start=True, stop=True,
                             tile_position=(32 * q, 0))
                t.drain()
            I("kbm", t.drain())
        if PHASES < 2.45:
            return
        for grp in range(4):
            bank = 6 + (grp % 2)
            W(t, "kbe", 7 + grp)
            if grp >= 2:
                W(t, "kbe", 11 + grp - 2)
            t.matmul(PA(bank, 0, 128, 0, [[1, 512]]),
                     revb[:, :],
                     BA(O_UBSCR + (grp % 2) * 1024, 0, 128, [[1, 512]]),
                     start=True, stop=True)
            I("kbm", t.drain())
        if PHASES < 2.5:
            return
        W(t, "kbe", 14)
        for jj in range(32):
            bank = jj % 2
            if jj >= 2:
                W(t, "kbe", 14 + jj - 1)
            for j4 in range(4):
                j = jj * 4 + j4
                t.matmul(PA(bank, 0, 128, 128 * j4, [[1, 128]]),
                         widen[:, 127 - j:127 - j + 128],
                         BA(O_KRT, 0, 128, [[1, 128]]),
                         start=True, stop=True)
            I("kbm", t.drain())

    @block.vector
    def _(v):
        if PHASES < 2.3:
            return
        for grp in range(4):
            bank = 2 + (grp % 2)
            W(v, "kbm", grp + 1)
            with nc.allow_low_precision(reason="bf16 kernel coeffs"):
                v.tensor_copy(BA(O_KRT + 2 * (32 * grp), 0, 128, [[1, 32]]),
                              PA(bank, 0, 128, 0, [[1, 32]]))
            v.drain()
            I("kbe", v.nop())
        if PHASES < 2.4:
            return
        for grp in range(4):
            bank = 4 + (grp % 2)
            W(v, "kbm", 4 + grp + 1)
            if grp >= 2:
                W(v, "kbm", 9 + grp - 2)
            with nc.allow_low_precision(reason="bf16 U operator"):
                v.tensor_copy(BA(O_UBSCR + (grp % 2) * 1024, 0, 128,
                                 [[16, 32], [1, 16]]),
                              PA(bank, 0, 128, 0, [[16, 32], [1, 16]]))
            v.drain()
            I("kbe", v.nop())
        if PHASES < 2.45:
            return
        for grp in range(4):
            bank = 6 + (grp % 2)
            W(v, "kbm", 9 + grp)
            v.tensor_copy(BA(O_ULOC + 2 * (32 * grp * 16), 0, 128,
                             [[16, 32], [1, 16]]),
                          PA(bank, 0, 128, 0, [[16, 32], [1, 16]]))
            v.drain()
            I("kbe", v.nop())
        if PHASES < 2.5:
            return
        for jj in range(32):
            bank = jj % 2
            W(v, "kbm", 12 + jj + 1)
            v.tensor_copy(
                BA(O_KLOC + 2 * (jj * 4), 0, 128, [[128, 128], [1, 4]]),
                PA(bank, 0, 128, 0, [[1, 128], [128, 4]]))
            v.drain()
            I("kbe", v.nop())

    KB_DONE = 46

    @block.sync
    def _(sync):
        if PHASES < 3:
            return
        W(sync, "kbe", KB_DONE)
        I("bd", sync.dma_start(
            out=AP(tensor=blob_loc, offset=0, ap=[[DCH * 128, 128], [1, DCH * 128]]),
            in_=BA(O_KLOC, 0, 128, [[1, DCH * 128]])), 16)
        I("bd", sync.dma_start(
            out=AP(tensor=blob_loc, offset=KSEC, ap=[[DCH * 16, 128], [1, DCH * 16]]),
            in_=BA(O_ULOC, 0, 128, [[1, DCH * 16]])), 16)
        I("bd", sync.dma_start(
            out=AP(tensor=blob_loc, offset=KSEC + USEC,
                   ap=[[32 * 128, 128], [1, 32 * 128]]),
            in_=BA(O_CST + 2, 0, 128, [[132, 32], [1, 128]])), 16)

    @block.gpsimd
    def _(g):
        if PHASES < 3:
            return
        W(g, "bd", 16 * 3)
        I("ag1", g.collective_compute(
            "AllGather", ALU.bypass, replica_groups=[list(range(R))],
            ins=[blob_loc.ap().opt()], outs=[blob_ag.ap().opt()]))

    @block.sync
    def _(sync):
        if PHASES < 3:
            return
        for wv in range(16):
            gch0 = wv * 64
            rank, chl0 = gch0 // DCH, gch0 % DCH
            if wv >= 2:
                W(sync, "umm", wv - 1)
            W(sync, "ag1", 1)
            src = AP(tensor=blob_ag, offset=rank * BLOB + KSEC + chl0 * 16,
                     ap=[[DCH * 16, 128], [1, 64 * 16]])
            DSER(sync, "ust")
            I("ust", sync.dma_start(
                out=AP(tensor=arb, offset=(O_USTR // 2) + (wv % 2) * 1024,
                       ap=[[PSB, 128], [1, 1024]]),
                in_=src), 16)

    @block.tensor
    def _(t):
        if PHASES < 3:
            return
        W(t, "ln1", 32)
        for wv in range(16):
            W(t, "ust", 16 * (wv + 1))
            bank = 6 + (wv % 2)
            if wv >= 2:
                W(t, "sev", wv - 1)
            for q in range(4):
                for k16 in range(16):
                    ci = k16 * 4 + q
                    ch = wv * 64 + ci
                    chhi = (ch // 4) % 16
                    lhs = AP(tensor=arb,
                             offset=(O_USTR // 2) + (wv % 2) * 1024 + ci * 16,
                             ap=[[PSB, 128], [1, 16]])
                    rhs = BA(O_XL1 + 2 * ch * CB, 0, 128, [[1, CB]])
                    t.matmul(PA(bank, 32 * q, 16, 32 * chhi, [[1, CB]]),
                             lhs, rhs, start=True, stop=True,
                             tile_position=(0, 32 * q))
                t.drain()
            I("umm", t.drain())

    @block.vector
    def _(v):
        if PHASES < 3:
            return
        for wv in range(16):
            W(v, "umm", wv + 1)
            bank = 6 + (wv % 2)
            v.tensor_copy(BA(O_SST + 2 * (wv * 16 * CB), 0, 128,
                             [[CB, 16], [1, CB]]),
                          PA(bank, 0, 128, 0, [[CB, 16], [1, CB]]))
            v.drain()
            I("sev", v.nop())

    @block.sync
    def _(sync):
        if PHASES < 3:
            return
        W(sync, "sev", 16)
        for tgt in range(R):
            I("sd", sync.dma_start(
                out=AP(tensor=s_blob, offset=tgt * SSEC,
                       ap=[[32 * CB, 128], [1, 32 * CB]]),
                in_=BA(O_SST + 2 * (tgt * 32 * CB), 0, 128, [[1, 32 * CB]])), 16)

    @block.gpsimd
    def _(g):
        if PHASES < 3:
            return
        W(g, "sd", 16 * R)
        with allow_ici_all_to_all():
            I("a21", g.collective_compute(
                "AllToAll", ALU.bypass, replica_groups=[list(range(R))],
                ins=[s_blob.ap().opt()], outs=[s_a2a.ap().opt()]))

    @block.sync
    def _(sync):
        if PHASES < 3:
            return
        for bt in range(64):
            gch0 = bt * 16
            rank, chl0 = gch0 // DCH, gch0 % DCH
            if bt >= 2:
                W(sync, "kmm", bt - 1)
            W(sync, "ag1", 1)
            src = AP(tensor=blob_ag, offset=rank * BLOB + chl0 * 128,
                     ap=[[DCH * 128, 128], [1, 16 * 128]])
            DSER(sync, "kst")
            I("kst", sync.dma_start(
                out=AP(tensor=kstr, offset=(bt % 2) * 2048,
                       ap=[[4096, 128], [1, 2048]]),
                in_=src), 16)

    @block.tensor
    def _(t):
        if PHASES < 3:
            return
        for bt in range(64):
            W(t, "kst", 16 * (bt + 1))
            bank = bt % 2
            if bt >= 2:
                W(t, "yev", bt - 1)
            for ci in range(16):
                ch = bt * 16 + ci
                lhs = AP(tensor=kstr, offset=(bt % 2) * 2048 + ci * 128,
                         ap=[[4096, 128], [1, 128]])
                rhs = BA(O_XL1 + 2 * ch * CB, 0, 128, [[1, CB]])
                t.matmul(PA(bank, 0, 128, CB * ci, [[1, CB]]),
                         lhs, rhs, start=True, stop=True)
            I("kmm", t.drain())

    @block.vector
    def _(v):
        if PHASES < 3:
            return
        W(v, "sd", 16 * R)
        for bt in range(64):
            W(v, "kmm", bt + 1)
            bank = bt % 2
            v.tensor_copy(
                BA(O_Y + 2 * (bt * 16), 0, 128,
                   [[1, 16], [D, GL], [GL * D, B]]),
                PA(bank, 0, 128, 0, [[CB, 16], [B, GL], [1, B]]))
            v.drain()
            I("yev", v.nop())

    @block.sync
    def _(sync):
        if PHASES < 4:
            return
        W(sync, "a21", 1)
        W(sync, "r1d", 16 * 32)
        I("h0", sync.dma_start(
            out=AP(tensor=arb, offset=O_H // 2,
                   ap=[[PSB, 128], [1024, 8], [1, 1024]]),
            in_=AP(tensor=s_a2a, offset=0,
                   ap=[[1024, 128], [SSEC, 8], [1, 1024]])), 16)


    @block.tensor
    def _(t):
        if PHASES < 4:
            return
        W(t, "h0", 16)
        W(t, "yev", 64)
        for r in range(5):
            sh = 1 << r
            nf, rem = (32 - sh) // 4, (32 - sh) % 4
            for wv in range(16):
                widx = r * 16 + wv
                bank = widx % 2
                if widx >= 2:
                    W(t, "pfa", widx - 1)
                for ci in range(2):
                    chhi = wv * 2 + ci
                    for q in range(4):
                        lhs = BA(O_ALP + 2 * (r * 512 + chhi * 16),
                                 32 * q, 16, [[1, 16]])
                        rhs1 = BA(O_H + 2 * (chhi * 32), 32 * q, 16,
                                  [[1024, nf], [8, 4], [1, 8]])
                        t.matmul(PA(bank, 32 * q, 16, ci * 256, [[1, nf * 32]]),
                                 lhs, rhs1, start=True, stop=True,
                                 tile_position=(32 * q, 32 * q))
                        if rem:
                            rhs2 = BA(O_H + 2 * (nf * 1024 + chhi * 32),
                                      32 * q, 16, [[8, rem], [1, 8]])
                            t.matmul(PA(bank, 32 * q, 16,
                                        ci * 256 + nf * 32, [[1, rem * 8]]),
                                     lhs, rhs2, start=True, stop=True,
                                     tile_position=(32 * q, 32 * q))
                I("pfm", t.drain())

    def _prefix_adds(eng, parity):
        for r in range(5):
            sh = 1 << r
            for wv in range(16):
                widx = r * 16 + wv
                if parity is not None and widx % 2 != parity:
                    continue
                bank = widx % 2
                W(eng, "pfm", widx + 1)
                for ci in range(2):
                    chhi = wv * 2 + ci
                    if sh < 4:
                        hd = BA(O_H + 2 * (chhi * 32 + sh * 8), 0, 128,
                                [[1, (4 - sh) * 8]])
                        eng.tensor_tensor(hd, PA(bank, 0, 128, ci * 256,
                                                 [[1, (4 - sh) * 8]]),
                                          hd, ALU.add)
                        bd = BA(O_H + 2 * (chhi * 32 + 1024), 0, 128,
                                [[1024, 7], [8, 4], [1, 8]])
                        eng.tensor_tensor(bd, PA(bank, 0, 128,
                                                 ci * 256 + (4 - sh) * 8,
                                                 [[1, 224]]),
                                          bd, ALU.add)
                    else:
                        nsrc = 8 - sh // 4
                        bd = BA(O_H + 2 * (chhi * 32 + (sh // 4) * 1024),
                                0, 128, [[1024, nsrc], [8, 4], [1, 8]])
                        eng.tensor_tensor(bd, PA(bank, 0, 128, ci * 256,
                                                 [[1, nsrc * 32]]),
                                          bd, ALU.add)
                eng.drain()
                W(eng, "pfa", widx)
                I("pfa", eng.nop())

    @block.vector
    def _(v):
        if PHASES < 4:
            return
        _prefix_adds(v, None)

    @block.vector
    def _(v):
        if PHASES < 4:
            return
        W(v, "pfa", 80)
        v.tensor_copy(BA(O_H2 + 2 * 8, 0, 128,
                         [[1024, 8], [32, 32], [8, 3], [1, 8]]),
                      BA(O_H, 0, 128, [[1024, 8], [32, 32], [8, 3], [1, 8]]))
        v.tensor_copy(BA(O_H2 + 2 * 1024, 0, 128,
                         [[1024, 7], [32, 32], [1, 8]]),
                      BA(O_H + 2 * 24, 0, 128, [[1024, 7], [32, 32], [1, 8]]))
        v.memset(BA(O_H2, 0, 128, [[32, 32], [1, 8]]), 0.0)
        v.drain()
        I("hsh", v.nop())

    @block.sync
    def _(sync):
        if PHASES < 4:
            return
        W(sync, "hsh", 1)
        for tgt in range(R):
            I("hd", sync.dma_start(
                out=AP(tensor=h_blob, offset=tgt * SSEC,
                       ap=[[1024, 128], [1, 1024]]),
                in_=BA(O_H2 + 2 * (tgt * 1024), 0, 128, [[1, 1024]])), 16)

    @block.gpsimd
    def _(g):
        if PHASES < 4:
            return
        W(g, "hd", 16 * R)
        with allow_ici_all_to_all():
            I("a22", g.collective_compute(
                "AllToAll", ALU.bypass, replica_groups=[list(range(R))],
                ins=[h_blob.ap().opt()], outs=[h_a2a.ap().opt()]))

    @block.sync
    def _(sync):
        if PHASES < 4:
            return
        W(sync, "a22", 1)
        for bt in range(64):
            gch0 = bt * 16
            rank, chl0 = gch0 // DCH, gch0 % DCH
            chhi0 = chl0 // 4
            if bt >= 2:
                W(sync, "wmm", bt - 1)
            srcw = AP(tensor=blob_ag,
                      offset=rank * BLOB + KSEC + USEC + chhi0 * 128,
                      ap=[[32 * 128, 128], [1, 512]])
            DSER(sync, "wld")
            I("wld", sync.dma_start(
                out=AP(tensor=wstr, offset=(bt % 2) * 512,
                       ap=[[1024, 128], [1, 512]]),
                in_=srcw), 16)
            srch = AP(tensor=h_a2a, offset=rank * SSEC + chhi0 * 32,
                      ap=[[1024, 128], [1, 128]])
            DSER(sync, "hld")
            I("hld", sync.dma_start(
                out=AP(tensor=hstr, offset=(bt % 2) * 128,
                       ap=[[256, 128], [1, 128]]),
                in_=srch), 16)

    @block.tensor
    def _(t):
        if PHASES < 4:
            return
        for bt in range(64):
            W(t, "wld", 16 * (bt + 1))
            W(t, "hld", 16 * (bt + 1))
            bank = 2 + (bt % 2)
            if bt >= 2:
                W(t, "wev", bt - 1)
            for q in range(4):
                for k4 in range(4):
                    ci = k4 * 4 + q
                    chs = ci // 4
                    lhs = AP(tensor=wstr, offset=32 * q * 1024
                             + (bt % 2) * 512 + chs * 128,
                             ap=[[1024, 16], [1, 128]])
                    rhs = AP(tensor=hstr, offset=32 * q * 256
                             + (bt % 2) * 128 + chs * 32,
                             ap=[[256, 16], [1, 32]])
                    t.matmul(PA(bank, 0, 128, ci * 32, [[1, 32]]),
                             lhs, rhs, start=True, stop=True,
                             tile_position=(32 * q, 0))
                t.drain()
            I("wmm", t.drain())

    @block.vector
    def _(v):
        if PHASES < 4:
            return
        for bt in range(64):
            W(v, "wmm", bt + 1)
            bank = 2 + (bt % 2)
            ydst = BA(O_Y + 2 * (bt * 16), 0, 128,
                      [[1, 16], [D, GL], [GL * D, B]])
            v.tensor_tensor(ydst,
                            PA(bank, 0, 128, 0, [[CB, 16], [B, GL], [1, B]]),
                            ydst, ALU.add)
            v.drain()
            I("wev", v.nop())

    def _wload(g, src_t, offb):
        return g.dma_start(
            out=BA(offb, 0, 128, [[1024, 8], [1, 1024]]),
            in_=AP(tensor=src_t, offset=0,
                   ap=[[1024, 128], [131072, 8], [1, 1024]]))

    @block.gpsimd
    def _(g):
        if PHASES < 5:
            return
        W(g, "hd", 16 * R)
        DSER(g, "wgt"); I("wgt", _wload(g, woutT_in, O_WOT), 16)
        W(g, "kmm", 64)
        DSER(g, "wgt"); I("wgt", _wload(g, fw["wf"], OF["wf"]), 16)
        DSER(g, "wgt"); I("wgt", _wload(g, fw["uf"], OF["uf"]), 16)
        DSER(g, "wgt"); I("wgt", _wload(g, fw["wi"], OF["wi"]), 16)

    @block.sync
    def _(sync):
        if PHASES < 5:
            return
        for pair in range(16):
            if pair >= 2:
                W(sync, "trm", pair - 1)
            for hb in range(2):
                blk = (pair // 2) * GL + (pair % 2) * 2 + hb
                DSER(sync, "r1l")
                I("r1l", sync.dma_start(
                    out=BA(O_R1D + (pair % 2) * 4096 + hb * 2048,
                           0, 128, [[1, 1024]]),
                    in_=AP(tensor=r1_sp, offset=blk * 128 * D,
                           ap=[[D, 128], [1, D]])), 16)
            if pair % 2 == 1:
                b = pair // 2
                W(sync, "aev", (b * 2 + 2) * 8)
                I("asp", sync.dma_start(
                    out=AP(tensor=at_sp, offset=b * 8 * 128 * SL,
                           ap=[[SL, 128], [128 * SL, 8], [1, SL]]),
                    in_=BA(O_ATS, 0, 128, [[512, 8], [1, SL]])), 16)
                if PHASES >= 6:
                    t_ = b
                    W(sync, "asp", 16 * (t_ + 1))
                    if t_ >= 1:
                        W(sync, "x2c", t_)
                    DSER(sync, "xsl")
                    I("xsl", sync.dma_start(
                        out=BA(O_ATD2B, 0, 128, [[512, 8], [1, SL]]),
                        in_=AP(tensor=at_sp, offset=t_ * 8 * 128 * SL,
                               ap=[[SL, 128], [128 * SL, 8], [1, SL]])), 16)
                    if t_ >= 1:
                        DSER(sync, "x2d")
                        I("x2d", sync.dma_start(
                            out=AP(tensor=xl2_sp,
                                   offset=(t_ - 1) * 8 * 128 * SL,
                                   ap=[[SL, 128], [128 * SL, 8], [1, SL]]),
                            in_=BA(O_X2DBB, 0, 128,
                                   [[512, 8], [1, SL]])), 16)

    @block.vector
    def _(v):
        if PHASES < 5:
            return
        W(v, "wev", 64)
        W(v, "gld", 16 * 22)
        for pair in range(16):
            b, cp = pair // 2, pair % 2
            for hb in range(2):
                bi = pair * 2 + hb
                c = cp * 2 + hb
                yblk = BA(O_Y + 2 * (b * GL * D + c * D), 0, 128, [[1, D]])
                v.reduce_sum(stat1[:, 0:1], yblk, axis=AXL.X)
                W(v, "sqa", 32 + bi + 1)
                v.drain()
                v.tensor_scalar(stat1[:, 2:3], stat1[:, 0:1], 1.0 / D, None,
                                ALU.mult)
                v.tensor_scalar(stat1[:, 4:5], stat1[:, 1:2], 1.0 / D, None,
                                ALU.mult)
                v.drain()
                v.scalar_tensor_tensor(stat1[:, 4:5], stat1[:, 2:3],
                                       stat1[:, 2:3], stat1[:, 4:5],
                                       ALU.mult, ALU.subtract)
                v.drain()
                v.tensor_scalar(stat1[:, 4:5], stat1[:, 4:5], -1.0, EPS,
                                ALU.mult, ALU.add)
                v.drain()
                v.reciprocal(stat1[:, 4:5], stat1[:, 4:5])
                v.drain()
                I("ln1s", v.nop())
                W(v, "sqt", 32 + bi + 1)
                if bi >= 2:
                    W(v, "l2m", pair)
                    W(v, "gtrs", bi - 1)
                zsl = BA(O_ZB + hb * 2048, 0, 128, [[1, D]])
                v.tensor_scalar(zsl, yblk, stat1[:, 2:3], stat1[:, 3:4],
                                ALU.subtract, ALU.mult)
                v.drain()
                v.tensor_tensor(zsl, zsl, lnw_rep[:, :], ALU.mult)
                v.drain()
                v.tensor_tensor(zsl, zsl, lnb_rep[:, :], ALU.add)
                v.drain()
                I("iln", v.nop())
            W(v, "gtrs", pair * 2 + 2)
            W(v, "aev", (pair + 1) * 8)
            atsl = BA(O_ATS + 2 * (cp * 256), 0, 128, [[512, 8], [1, 256]])
            v.tensor_tensor(BA(O_ZB, 0, 128, [[256, 8], [1, 256]]),
                            atsl, atsl, ALU.mult)
            v.drain()
            I("l2e", v.nop())
            W(v, "l2m", pair + 1)
            mu_p = FA(O_MU3 + pair * 2048, 0, 128, [[1, 256]])
            rs_p = FA(O_RS3 + pair * 2048, 0, 128, [[1, 256]])
            v.tensor_copy(mu_p, PA(4, 0, 128, 0, [[1, 256]]))
            v.tensor_copy(rs_p, PA(4, 0, 128, 256, [[1, 256]]))
            v.drain()
            I("l2e", v.nop())
            v.tensor_scalar(mu_p, mu_p, 1.0 / D, None, ALU.mult)
            v.tensor_scalar(rs_p, rs_p, 1.0 / D, None, ALU.mult)
            v.drain()
            v.tensor_tensor(PA(4, 0, 128, 0, [[1, 256]]), mu_p, mu_p,
                            ALU.mult)
            v.drain()
            v.tensor_tensor(rs_p, rs_p, PA(4, 0, 128, 0, [[1, 256]]),
                            ALU.subtract)
            v.drain()
            v.tensor_scalar(rs_p, rs_p, 1.0, EPS, ALU.mult, ALU.add)
            v.drain()
            v.reciprocal(rs_p, rs_p)
            v.drain()
            I("l2p", v.nop())
            if PHASES >= 6 and pair % 2 == 1 and pair >= 3:
                t_ = (pair - 3) // 2
                W(v, "l2q", 2 * t_ + 2)
                W(v, "xsl", 16 * (t_ + 1))
                if t_ >= 1:
                    W(v, "x2d", 16 * t_)
                v.tensor_tensor(BA(O_X2DBB, 0, 128, [[512, 8], [1, SL]]),
                                BA(O_ATD2B, 0, 128, [[512, 8], [1, SL]]),
                                FA(O_MU3 + t_ * 4096, 0, 128,
                                   [[0, 8], [512, 2], [1, 256]]),
                                ALU.subtract)
                v.drain()
                v.tensor_tensor(BA(O_X2DBB, 0, 128, [[512, 8], [1, SL]]),
                                BA(O_X2DBB, 0, 128, [[512, 8], [1, SL]]),
                                FA(O_RS3 + t_ * 4096, 0, 128,
                                   [[0, 8], [512, 2], [1, 256]]), ALU.mult)
                v.drain()
                I("x2c", v.nop())

    @block.scalar
    def _(sc):
        if PHASES < 5:
            return
        W(sc, "wev", 64)
        for pair in range(16):
            b, cp = pair // 2, pair % 2
            for hb in range(2):
                bi = pair * 2 + hb
                c = cp * 2 + hb
                yblk = BA(O_Y + 2 * (b * GL * D + c * D), 0, 128, [[1, D]])
                if bi >= 1:
                    W(sc, "ln1s", 32 + bi)
                sc.activation(FA(O_SQ2, 0, 128, [[1, D]]), yblk, AF.Square,
                              accum_out=stat1[:, 1:2])
                sc.drain()
                I("sqa", sc.nop())
                W(sc, "ln1s", 32 + bi + 1)
                sc.activation(stat1[:, 3:4], stat1[:, 4:5], AF.Sqrt)
                sc.drain()
                I("sqt", sc.nop())
                W(sc, "iln", bi + 1)
                zsl = BA(O_ZB + hb * 2048, 0, 128, [[1, D]])
                sc.activation(zsl, zsl, AF.Gelu)
                sc.drain()
                I("gev", sc.nop())
            W(sc, "gtrs", pair * 2 + 2)
            if pair >= 1:
                W(sc, "trm", pair)
            for hb in range(2):
                sc.activation(BA(O_GF + 2 * (hb * 128), 0, 128,
                                 [[256, 8], [1, 128]]),
                              PAB(6 + hb, 0, 128, 0, [[128, 8], [1, 128]]),
                              AF.Identity)
            sc.drain()
            I("gfe", sc.nop())
            if pair == 0:
                W(sc, "cst", 96)
            W(sc, "trm", pair + 1)
            if b >= 1:
                W(sc, "asp", 16 * b)
            for dck in range(8):
                bank = dck // 2
                sc.activation(
                    BA(O_ATS + 2 * (dck * 512 + cp * 256), 0, 128, [[1, 256]]),
                    PA(bank, 0, 128, (dck % 2) * 256, [[1, 256]]),
                    AF.Identity,
                    bias=fvec[:, FB["bout"] * 8 + dck:FB["bout"] * 8 + dck + 1])
            sc.drain()
            I("aev", sc.nop(), 8)
            W(sc, "l2p", pair + 1)
            sc.activation(FA(O_RS3 + pair * 2048, 0, 128, [[1, 256]]),
                          FA(O_RS3 + pair * 2048, 0, 128, [[1, 256]]),
                          AF.Sqrt)
            sc.drain()
            I("l2q", sc.nop())

    @block.tensor
    def _(t):
        if PHASES < 5:
            return
        W(t, "wgt", 16)
        for pair in range(16):
            b, cp = pair // 2, pair % 2
            for hb in range(2):
                bi = pair * 2 + hb
                W(t, "gev", bi + 1)
                if pair >= 1:
                    W(t, "gfe", pair)
                gsl = O_ZB + hb * 2048
                for dk in range(8):
                    t.transpose(PAB(6 + hb, 0, 128, dk * 128, [[1, 128]]),
                                BA(gsl + 2 * (dk * 128), 0, 128, [[1, 128]]),
                                identb[:, :])
                I("gtrs", t.drain())
            W(t, "gfe", pair + 1)
            W(t, "r1l", 16 * 2 * (pair + 1))
            if pair >= 1:
                W(t, "aev", pair * 8)
            for dck in range(8):
                bank = dck // 2
                for dk in range(8):
                    t.matmul(PA(bank, 0, 128, (dck % 2) * 256, [[1, 256]]),
                             BA(O_WOT + 2 * (dk * 1024 + dck * 128),
                                0, 128, [[1, 128]]),
                             BA(O_GF + 2 * (dk * 256), 0, 128, [[1, 256]]),
                             start=(dk == 0), stop=False)
                for hb in range(2):
                    t.matmul(PA(bank, 0, 128, (dck % 2) * 256 + hb * 128,
                                [[1, 128]]),
                             BA(O_R1D + (pair % 2) * 4096 + hb * 2048
                                + 2 * (dck * 128), 0, 128, [[1, 128]]),
                             identb[:, :], start=False, stop=(hb == 1))
            I("trm", t.drain())
            W(t, "l2e", pair * 2 + 1)
            for dck in range(8):
                t.matmul(PA(4, 0, 128, 0, [[1, 256]]), onesb[:, :],
                         BA(O_ATS + 2 * (dck * 512 + cp * 256), 0, 128,
                            [[1, 256]]),
                         start=(dck == 0), stop=(dck == 7))
            for dck in range(8):
                t.matmul(PA(4, 0, 128, 256, [[1, 256]]), onesb[:, :],
                         BA(O_ZB + 2 * (dck * 256), 0, 128, [[1, 256]]),
                         start=(dck == 0), stop=(dck == 7))
            I("l2m", t.drain())





    @block.gpsimd
    def _(g):
        if PHASES < 6:
            return
        W(g, "trm", 16)
        DSER(g, "wgt"); I("wgt", _wload(g, fw["wo"], OF["wo"]), 16)
        W(g, "iln", 32)
        DSER(g, "wgt"); I("wgt", _wload(g, fw["uo"], OF["uo"]), 16)
        W(g, "x2c", 8)
        DSER(g, "wgt"); I("wgt", _wload(g, fw["wc"], OF["wc"]), 16)
        DSER(g, "wgt"); I("wgt", _wload(g, fw["uc"], OF["uc"]), 16)
        W(g, "x2d", 16 * 8)
        DSER(g, "wgt"); I("wgt", _wload(g, fw["ui"], OF["ui"]), 16)

    @block.sync
    def _(sync):
        if PHASES < 6:
            return
        W(sync, "x2c", T)
        DSER(sync, "x2d")
        I("x2d", sync.dma_start(
            out=AP(tensor=xl2_sp, offset=(T - 1) * 8 * 128 * SL,
                   ap=[[SL, 128], [128 * SL, 8], [1, SL]]),
            in_=BA(O_X2DBB, 0, 128, [[512, 8], [1, SL]])), 16)
        W(sync, "trm", 16)

        def _pwds(tt):
            for dck in range(8):
                gi = tt * 8 + dck
                W(sync, "pwe", gi + 1)
                DSER(sync, "pwd")
                I("pwd", sync.dma_start(
                    out=AP(tensor=pwo_sp, offset=(tt * 8 + dck) * 128 * SL,
                           ap=[[SL, 128], [1, SL]]),
                    in_=FA(O_PWEV + 4 * ((gi % 2) * SL), 0, 128,
                           [[1, SL]])), 16)
        for t_ in range(T):
            W(sync, "x2d", 16 * (t_ + 1))
            if t_ >= 2:
                W(sync, "pwm", 8 * (t_ - 1))
            DSER(sync, "xpl")
            I("xpl", sync.dma_start(
                out=BA(O_X2DB + (t_ % 2) * 8192, 0, 128,
                       [[512, 8], [1, SL]]),
                in_=AP(tensor=xl2_sp, offset=t_ * 8 * 128 * SL,
                       ap=[[SL, 128], [128 * SL, 8], [1, SL]])), 16)
            if t_ >= 1:
                _pwds(t_ - 1)
        _pwds(T - 1)

    @block.vector
    def _(v):
        if PHASES < 6:
            return
        for t_ in (7,):
            W(v, "l2q", 2 * t_ + 2)
            W(v, "xsl", 16 * (t_ + 1))
            W(v, "x2d", 16 * t_)
            v.tensor_tensor(BA(O_X2DBB, 0, 128, [[512, 8], [1, SL]]),
                            BA(O_ATD2B, 0, 128, [[512, 8], [1, SL]]),
                            FA(O_MU3 + t_ * 4096, 0, 128,
                               [[0, 8], [512, 2], [1, 256]]), ALU.subtract)
            v.drain()
            v.tensor_tensor(BA(O_X2DBB, 0, 128, [[512, 8], [1, SL]]),
                            BA(O_X2DBB, 0, 128, [[512, 8], [1, SL]]),
                            FA(O_RS3 + t_ * 4096, 0, 128,
                               [[0, 8], [512, 2], [1, 256]]), ALU.mult)
            v.drain()
            I("x2c", v.nop())
        for t_ in range(T):
            for dck in range(8):
                gi = t_ * 8 + dck
                W(v, "pwm", gi + 1)
                if gi >= 2:
                    W(v, "pwd", 16 * (gi - 1))
                v.tensor_copy(FA(O_PWEV + 4 * ((gi % 2) * SL), 0, 128,
                                 [[1, SL]]),
                              PA(gi % 2, 0, 128, 0, [[1, SL]]))
                v.drain()
                I("pwe", v.nop())

    @block.tensor
    def _(t):
        if PHASES < 6:
            return
        W(t, "wgt", 16 * 5)
        for t_ in range(T):
            W(t, "xpl", 16 * (t_ + 1))
            h = (t_ % 2) * 8192
            for dck in range(8):
                gi = t_ * 8 + dck
                bank = gi % 2
                if gi >= 2:
                    W(t, "pwe", gi - 1)
                for dk in range(8):
                    t.matmul(PA(bank, 0, 128, 0, [[1, SL]]),
                             BA(OF["wo"] + 2 * (dk * 1024 + dck * 128),
                                0, 128, [[1, 128]]),
                             BA(O_X2DB + h + 2 * (dk * SL), 0, 128,
                                [[1, SL]]),
                             start=(dk == 0), stop=(dk == 7))
                I("pwm", t.drain())

    @block.vector
    def _(v):
        if PHASES < 6:
            return
        W(v, "pwm", 64)
        W(v, "x2c", 8)
        W(v, "cst", 96)
        for dk in range(8):
            v.tensor_copy(FA(O_CF + 4 * (dk * SL), 0, 128, [[1, SL]]),
                          AP(tensor=fvec, offset=FB["inc"] * 8 + dk,
                             ap=[[48, 128], [0, SL]]))
        v.drain()
        v.tensor_copy(BA(O_CBF, 0, 128, [[1, 8 * SL]]),
                      FA(O_CF, 0, 128, [[1, 8 * SL]]))
        v.drain()
        I("cbfs", v.nop())

    @block.sync
    def _(sync):
        if PHASES < 6:
            return
        W(sync, "x2d", 16 * 8)
        W(sync, "pwm", 64)
        for t_ in range(T):
            if t_ >= 2:
                W(sync, "gmm", (t_ - 1) * 8)
            DSER(sync, "x2l")
            I("x2l", sync.dma_start(
                out=BA(O_X2DB + (t_ % 2) * 8192, 0, 128, [[512, 8], [1, SL]]),
                in_=AP(tensor=xl2_sp, offset=t_ * 8 * 128 * SL,
                       ap=[[SL, 128], [128 * SL, 8], [1, SL]])), 16)

    @block.tensor
    def _(t):
        if PHASES < 6:
            return
        W(t, "wgt", 16 * 9)
        for t_ in range(T):
            W(t, "x2l", 16 * (t_ + 1))
            W(t, "cbfs", t_ + 1)
            h = (t_ % 2) * 8192
            for dck in range(8):
                gi = t_ * 8 + dck
                b0 = (dck % 2) * 3
                if gi >= 2:
                    W(t, "cup", gi - 1)
                for gname, wn, un, bank in (("f", "wf", "uf", b0),
                                            ("i", "wi", "ui", b0 + 1),
                                            ("g", "wc", "uc", b0 + 2)):
                    for dk in range(8):
                        t.matmul(PA(bank, 0, 128, 0, [[1, SL]]),
                                 BA(OF[wn] + 2 * (dk * 1024 + dck * 128),
                                    0, 128, [[1, 128]]),
                                 BA(O_X2DB + h + 2 * (dk * SL), 0, 128,
                                    [[1, SL]]),
                                 start=(dk == 0), stop=False)
                    for dk in range(8):
                        t.matmul(PA(bank, 0, 128, 0, [[1, SL]]),
                                 BA(OF[un] + 2 * (dk * 1024 + dck * 128),
                                    0, 128, [[1, 128]]),
                                 BA(O_CBF + 2 * (dk * SL), 0, 128, [[1, SL]]),
                                 start=False, stop=(dk == 7))
                I("gmm", t.drain())

    @block.scalar
    def _(sc):
        if PHASES < 6:
            return
        W(sc, "cst", 96)
        W(sc, "pwd", 16 * 64)
        for t_ in range(T):
            for dck in range(8):
                gi = t_ * 8 + dck
                b0 = (dck % 2) * 3
                W(sc, "gmm", gi + 1)
                if gi >= 1:
                    W(sc, "cup", gi)
                sc.activation(FA(O_GTF, 0, 128, [[1, SL]]),
                              PA(b0, 0, 128, 0, [[1, SL]]), AF.Sigmoid,
                              bias=fvec[:, FB["bf"] * 8 + dck:
                                        FB["bf"] * 8 + dck + 1])
                sc.activation(FA(O_GTI, 0, 128, [[1, SL]]),
                              PA(b0 + 1, 0, 128, 0, [[1, SL]]), AF.Sigmoid,
                              bias=fvec[:, FB["bi"] * 8 + dck:
                                        FB["bi"] * 8 + dck + 1])
                sc.activation(FA(O_GTG, 0, 128, [[1, SL]]),
                              PA(b0 + 2, 0, 128, 0, [[1, SL]]), AF.Tanh,
                              bias=fvec[:, FB["bc"] * 8 + dck:
                                        FB["bc"] * 8 + dck + 1])
                sc.drain()
                I("gac", sc.nop())

    @block.vector
    def _(v):
        if PHASES < 6:
            return
        for t_ in range(T):
            for dck in range(8):
                gi = t_ * 8 + dck
                W(v, "gac", gi + 1)
                csl = FA(O_CF + 4 * (dck * SL), 0, 128, [[1, SL]])
                v.tensor_tensor(csl, FA(O_GTF, 0, 128, [[1, SL]]), csl,
                                ALU.mult)
                v.tensor_tensor(FA(O_CT1, 0, 128, [[1, SL]]),
                                FA(O_GTI, 0, 128, [[1, SL]]),
                                FA(O_GTG, 0, 128, [[1, SL]]), ALU.mult)
                v.drain()
                v.tensor_tensor(csl, csl, FA(O_CT1, 0, 128, [[1, SL]]),
                                ALU.add)
                v.drain()
                I("cup", v.nop())
            v.tensor_copy(BA(O_CBF, 0, 128, [[1, 8 * SL]]),
                          FA(O_CF, 0, 128, [[1, 8 * SL]]))
            v.drain()
            I("cbfs", v.nop())

    @block.tensor
    def _(t):
        if PHASES < 7:
            return
        W(t, "cbfs", 9)
        W(t, "wgt", 16 * 9)
        for dck in range(8):
            bank = 6 + dck % 2
            if dck >= 2:
                W(t, "uoe", dck - 1)
            for dk in range(8):
                t.matmul(PA(bank, 0, 128, 0, [[1, SL]]),
                         BA(OF["uo"] + 2 * (dk * 1024 + dck * 128), 0, 128,
                            [[1, 128]]),
                         BA(O_CBF + 2 * (dk * SL), 0, 128, [[1, SL]]),
                         start=(dk == 0), stop=(dk == 7))
            I("uom", t.drain())

    @block.vector
    def _(v):
        if PHASES < 7:
            return
        for dck in range(8):
            W(v, "uom", dck + 1)
            v.tensor_copy(FA(O_UOO + 4 * (dck * SL), 0, 128, [[1, SL]]),
                          PA(6 + dck % 2, 0, 128, 0, [[1, SL]]))
            v.drain()
            I("uoe", v.nop())

    @block.sync
    def _(sync):
        if PHASES < 7:
            return
        W(sync, "gmm", 64)
        W(sync, "pwd", 16 * 64)
        for t_ in range(T):
            if t_ >= 1:
                W(sync, "hsl", 9 * t_)
            DSER(sync, "pwl")
            I("pwl", sync.dma_start(
                out=FA(O_PWDB, 0, 128, [[1, 8 * SL]]),
                in_=AP(tensor=pwo_sp, offset=t_ * 8 * 128 * SL,
                       ap=[[SL, 128], [128 * SL, 8], [1, SL]])), 16)
            if t_ >= 1:
                W(sync, "oev", t_)
                DSER(sync, "od")
                I("od", sync.dma_start(
                    out=AP(tensor=out_t, offset=(t_ - 1) * 8 * 128 * SL,
                           ap=[[SL, 128], [128 * SL, 8], [1, SL]]),
                    in_=FA(O_OSL, 0, 128, [[1, 8 * SL]])), 16)
            DSER(sync, "atl")
            I("atl", sync.dma_start(
                out=BA(O_ATD2, 0, 128, [[512, 8], [1, SL]]),
                in_=AP(tensor=at_sp, offset=t_ * 8 * 128 * SL,
                       ap=[[SL, 128], [128 * SL, 8], [1, SL]])), 16)

    @block.vector
    def _(v):
        if PHASES < 7:
            return
        W(v, "uoe", 8)
        for t_ in range(T):
            W(v, "pwl", 16 * (t_ + 1))
            v.tensor_tensor(FA(O_PWDB, 0, 128, [[1, 8 * SL]]),
                            FA(O_PWDB, 0, 128, [[1, 8 * SL]]),
                            FA(O_UOO, 0, 128, [[1, 8 * SL]]), ALU.add)
            v.drain()
            I("hsl", v.nop())
            W(v, "hsl", t_ * 9 + 9)
            W(v, "atl", 16 * (t_ + 1))
            if t_ >= 1:
                W(v, "od", 16 * t_)
            v.tensor_tensor(FA(O_OSL, 0, 128, [[1, 8 * SL]]),
                            BA(O_ATD2, 0, 128, [[1, 8 * SL]]),
                            BA(O_HSL, 0, 128, [[1, 8 * SL]]), ALU.add)
            v.drain()
            I("oev", v.nop())

    @block.scalar
    def _(sc):
        if PHASES < 7:
            return
        W(sc, "cst", 96)
        for t_ in range(T):
            W(sc, "hsl", t_ * 9 + 1)
            for dck in range(8):
                sc.activation(BA(O_HSL + 2 * (dck * SL), 0, 128, [[1, SL]]),
                              FA(O_PWDB + 4 * (dck * SL), 0, 128, [[1, SL]]),
                              AF.Tanh,
                              bias=fvec[:, FB["bo"] * 8 + dck:
                                        FB["bo"] * 8 + dck + 1])
            sc.drain()
            I("hsl", sc.nop(), 8)

    @block.sync
    def _(sync):
        if PHASES < 7:
            return
        W(sync, "oev", T)
        DSER(sync, "od")
        I("od", sync.dma_start(
            out=AP(tensor=out_t, offset=(T - 1) * 8 * 128 * SL,
                   ap=[[SL, 128], [128 * SL, 8], [1, SL]]),
            in_=FA(O_OSL, 0, 128, [[1, 8 * SL]])), 16)
        W(sync, "od", 16 * 8)

    if PHASES < 7:
        @block.sync
        def _(sync):
            for nm, val in PHASE_END[PHASES]:
                W(sync, nm, val)
            for t_ in range(T):
                I("od", sync.dma_start(
                    out=AP(tensor=out_t, offset=t_ * 8 * 128 * SL,
                           ap=[[SL, 128], [128 * SL, 8], [1, SL]]),
                    in_=FA(0, 0, 128, [[1, 8 * SL]])), 16)
            W(sync, "od", 16 * 8)

    blkcm.__exit__(None, None, None)
    for cm in ctxs:
        cm.__exit__(None, None, None)
    return nc


_NC_CACHE = {}


def kernel(**inputs):
    key = "v1-%d" % PHASES
    if key not in _NC_CACHE:
        _NC_CACHE[key] = build_nc()
    nc = _NC_CACHE[key]
    f32 = lambda v: np.ascontiguousarray(np.asarray(v, dtype=np.float32))
    x = f32(inputs["L0_attn_x"])
    A = f32(inputs["L0_attn_A"])
    Bm = f32(inputs["L0_attn_Bm"])
    Cm = f32(inputs["L0_attn_Cm"])
    Dm = f32(inputs["L0_attn_Dm"])
    in_maps = []
    for k in range(R):
        m = {
            "x": np.ascontiguousarray(x[:, k * SL:(k + 1) * SL, :]),
            "attn_A": np.ascontiguousarray(A[k * DCH:(k + 1) * DCH]),
            "attn_Bm": np.ascontiguousarray(Bm[k * DCH:(k + 1) * DCH]),
            "attn_Cm": np.ascontiguousarray(Cm[k * DCH:(k + 1) * DCH]),
            "attn_Dm": np.ascontiguousarray(Dm[k * DCH:(k + 1) * DCH]),
            "ln_w": f32(inputs["L0_attn_ln_w"]),
            "ln_b": f32(inputs["L0_attn_ln_b"]),
            "W_out": f32(inputs["L0_attn_W_out"]),
            "b_out": f32(inputs["L0_attn_b_out"]),
        }
        for nm in ("wf", "uf", "wi", "ui", "wc", "uc", "wo", "uo"):
            m[nm + "T"] = np.ascontiguousarray(
                np.asarray(inputs["L0_ffn_" + nm], dtype=np.float32).T)
        m["WoutT"] = np.ascontiguousarray(
            np.asarray(inputs["L0_attn_W_out"], dtype=np.float32).T)
        for nm in ("bf", "bi", "bc", "bo", "in_c"):
            m[nm] = f32(inputs["L0_ffn_" + nm])
        in_maps.append(m)
    res = run_bass_kernel_spmd(nc, in_maps, core_ids=list(range(R)))
    out = np.empty((B, S, D), dtype=np.float32)
    for k in range(R):
        o = res.results[k]["out"]
        out[:, k * SL:(k + 1) * SL, :] = np.transpose(
            o, (0, 3, 1, 2)).reshape(B, SL, D)
    return out


# version 08 (final)
# speedup vs baseline: 1.0357x; 1.0228x over previous
import numpy as np
import concourse.bass as bass
import concourse.mybir as mybir
from concourse.bass_utils import run_bass_kernel_spmd
from concourse.aot_env import allow_ici_all_to_all

F32 = mybir.dt.float32
BF16 = mybir.dt.bfloat16
AF = mybir.ActivationFunctionType
ALU = mybir.AluOpType
AXL = mybir.AxisListType
AP = bass.AP

R, B, S, D, NST, L = 8, 8, 4096, 1024, 16, 128
SL = S // R
GL = SL // L
GG = S // L
CB = GL * B
DCH = D // R
T = B
EPS = 1e-5
KB = 1024
NRND = 11


PHASES = 7



def _ul(r):
    return (1 << r) if r <= 6 else 0


def _cl(r):
    return (1 << r) if r <= 6 else (1 if r == 7 else 0)


def chain_waves():
    waves = []
    bank = 0
    for r in range(NRND):
        colw = _ul(r) + _cl(r) + 32
        per = max(1, min(8, 512 // colw))
        ch0 = 0
        while ch0 < 32:
            nch = min(per, 32 - ch0)
            waves.append((r, ch0, nch, colw, bank))
            bank ^= 1
            ch0 += nch
    return waves


CWAVES = chain_waves()
RSTART = {}
for _i, (_r, _c0, _n, _cw, _bk) in enumerate(CWAVES):
    RSTART.setdefault(_r, _i)

PHASE_END = {
    1: [("r1d", 16 * 32), ("ln1", 32), ("cst", 16 * 6), ("gld", 16 * 22)],
    2: [("che", 1 + len(CWAVES)), ("chm", 2 + len(CWAVES)),
        ("r1d", 16 * 32), ("cst", 16 * 6)],
    2.2: [("kbe", 2)],
    2.21: [("kbm", 1)],
    2.3: [("kbe", 6), ("kbm", 4)],
    2.4: [("kbe", 10), ("kbm", 8)],
    2.45: [("kbe", 14), ("kbm", 12)],
    2.5: [("kbe", 46), ("r1d", 16 * 32), ("cst", 16 * 6)],
    3: [("sev", 16), ("a21", 1), ("r1d", 16 * 32), ("cst", 16 * 6)],
    4: [("yev", 64), ("r1d", 16 * 32), ("cst", 16 * 6)],
    5: [("l2q", 16), ("asp", 16 * 8), ("aev", 128), ("gev", 32),
        ("wgt", 16 * 4)],
    6: [("gmm", 64), ("cbfs", 9), ("pwd", 16 * 64), ("x2d", 16 * 8),
        ("cup", 64), ("gac", 64), ("wgt", 16 * 8)],
}


def build_nc(dump=()):
    nc = bass.Bass("TRN2", target_bir_lowering=False, debug=False)

    x_in = nc.dram_tensor("x", [B, SL, D], F32, kind="ExternalInput")
    a_in = nc.dram_tensor("attn_A", [DCH, NST, NST], F32, kind="ExternalInput")
    bm_in = nc.dram_tensor("attn_Bm", [DCH, NST], F32, kind="ExternalInput")
    cm_in = nc.dram_tensor("attn_Cm", [DCH, NST], F32, kind="ExternalInput")
    dm_in = nc.dram_tensor("attn_Dm", [DCH], F32, kind="ExternalInput")
    lnw_in = nc.dram_tensor("ln_w", [D], F32, kind="ExternalInput")
    lnb_in = nc.dram_tensor("ln_b", [D], F32, kind="ExternalInput")
    wout_in = nc.dram_tensor("W_out", [D, D], F32, kind="ExternalInput")
    bout_in = nc.dram_tensor("b_out", [D], F32, kind="ExternalInput")
    fw = {nm: nc.dram_tensor(nm + "T", [D, D], F32, kind="ExternalInput")
          for nm in ("wf", "uf", "wi", "ui", "wc", "uc", "wo", "uo")}
    woutT_in = nc.dram_tensor("WoutT", [D, D], F32, kind="ExternalInput")
    fv = {nm: nc.dram_tensor(nm, [D], F32, kind="ExternalInput")
          for nm in ("bf", "bi", "bc", "bo", "in_c")}
    out_t = nc.dram_tensor("out", [T, 8, 128, SL], F32, kind="ExternalOutput")

    KSEC = 128 * DCH * 128
    USEC = 128 * DCH * NST
    WSEC = 128 * 32 * 128
    BLOB = KSEC + USEC + WSEC
    blob_loc = nc.dram_tensor("blob_loc", [BLOB], BF16)
    blob_ag = nc.dram_tensor("blob_ag", [R * BLOB], BF16)
    SSEC = 128 * 32 * GL * B
    s_blob = nc.dram_tensor("s_blob", [R * SSEC], BF16)
    s_a2a = nc.dram_tensor("s_a2a", [R * SSEC], BF16)
    h_blob = nc.dram_tensor("h_blob", [R * SSEC], BF16)
    h_a2a = nc.dram_tensor("h_a2a", [R * SSEC], BF16)
    r1_sp = nc.dram_tensor("r1_sp", [32 * 128 * D], BF16)
    at_sp = nc.dram_tensor("at_sp", [T * 8 * 128 * SL], BF16)
    xl2_sp = nc.dram_tensor("xl2_sp", [T * 8 * 128 * SL], BF16)
    pwo_sp = nc.dram_tensor("pwo_sp", [T * 8 * 128 * SL], F32)

    dbg = {}
    def DBG(name, shape, dt=F32):
        if name in dump and name not in dbg:
            dbg[name] = nc.dram_tensor("dbg_" + name, shape, dt,
                                       kind="ExternalOutput")
        return dbg.get(name)

    sbA = nc.alloc_sbuf_tensor
    identb = sbA("identb", [128, 128], BF16)
    revb = sbA("revb", [128, 128], BF16)
    widen = sbA("widen", [128, 256], BF16)
    onesb = sbA("onesb", [128, 128], BF16)
    lnw_rep = sbA("lnw_rep", [128, D], BF16)
    lnb_rep = sbA("lnb_rep", [128, D], BF16)
    fvec = sbA("fvec", [128, 6 * 8], F32)
    FB = {"bout": 0, "bf": 1, "bi": 2, "bc": 3, "bo": 4, "inc": 5}
    xstg = sbA("xstg", [128, 2 * D], F32)
    stat1 = sbA("stat1", [128, 16], F32)
    kstr = sbA("kstr", [128, 2 * 16 * 128], BF16)
    wstr = sbA("wstr", [128, 2 * 4 * 128], BF16)
    hstr = sbA("hstr", [128, 2 * 4 * CB], BF16)
    vscr = sbA("vscr", [128, 64], F32)
    dmrow = sbA("dmrow", [1, DCH], F32)

    ARNB = 175 * KB
    arena = nc.alloc_sbuf_tensor("arena", [128, ARNB // 4], F32)
    arb = arena.bitcast(BF16)
    PSF = ARNB // 4
    PSB = ARNB // 2

    def BA(offb, part0, nparts, dims):
        assert offb % 2 == 0
        return AP(tensor=arb, offset=part0 * PSB + offb // 2,
                  ap=[[PSB, nparts]] + [list(d) for d in dims])

    def FA(offb, part0, nparts, dims):
        assert offb % 4 == 0
        return AP(tensor=arena, offset=part0 * PSF + offb // 4,
                  ap=[[PSF, nparts]] + [list(d) for d in dims])

    O_XL1 = 0
    O_PC, O_PN, O_PTC, O_PTN = 64 * KB, 65 * KB, 66 * KB, 67 * KB
    O_UST, O_CST, O_PRD, O_ALP = 68 * KB, 76 * KB, 85 * KB, 167 * KB
    O_BMB, O_KRT, O_CMS = 98 * KB, 98 * KB + 256, 98 * KB + 1024
    O_KLOC, O_ULOC = 99 * KB, 131 * KB
    O_Y = 64 * KB
    O_SST = 99 * KB
    O_MU3, O_RS3 = O_Y, O_Y + 1024
    O_USTR = 135 * KB
    O_SQ1, O_R1T = 139 * KB, 143 * KB
    O_H, O_H2 = 135 * KB, 151 * KB
    O_WOT, O_WSTG = 135 * KB, 151 * KB
    O_ZB, O_GF, O_R1D, O_ATS = 151 * KB, 155 * KB, 159 * KB, 167 * KB
    O_SQ2 = 131 * KB
    OF = {"wf": 0, "uf": 16 * KB, "wi": 32 * KB, "ui": 48 * KB,
          "wc": 64 * KB, "uc": 80 * KB, "wo": 135 * KB, "uo": 99 * KB}
    O_X2DB, O_ATD2, O_PWEV = 151 * KB, 167 * KB, 123 * KB
    O_ATD2B = 56 * KB
    O_X2DBB = 48 * KB
    O_CF, O_CBF = 135 * KB, 115 * KB
    O_GTF, O_GTI, O_GTG, O_CT1, O_UOS = (123 * KB, 125 * KB, 127 * KB,
                                         129 * KB, 131 * KB)
    O_PWDB, O_UOO, O_HSL, O_OSL = 64 * KB, 80 * KB, 151 * KB, 135 * KB

    pbk = [nc.alloc_psum_tensor("pb%d" % i, [128, 512], F32) for i in range(8)]
    pbb = [p.bitcast(BF16) for p in pbk]

    def PA(bank, part0, nparts, offe, dims):
        return AP(tensor=pbk[bank], offset=part0 * 512 + offe,
                  ap=[[512, nparts]] + [list(d) for d in dims])

    def PAB(bank, part0, nparts, offe, dims):
        return AP(tensor=pbb[bank], offset=part0 * 1024 + offe,
                  ap=[[1024, nparts]] + [list(d) for d in dims])

    class Sem:
        def __init__(self, h):
            self.h = h
            self.n = 0

    sems = {}
    ctxs = []
    for _n in ("x cst seed sqa ln1s sqt ln1 r1c r1d chm che kbm kbe bd ag1 "
               "ust umm sev sd a21 kst kmm yev h0 pfm pfa hsh hd a22 hld wmm "
               "wev wld iln trm gev aev l2m l2e asp r1l l2f l2p l2q xsl x2c "
               "x2d xpl pwm "
               "pwe pwd x2l gmm gac cup cbfs uom uoe pwl atl hsl oev od "
               "wcst pini gtrs gfe gld wgt").split():
        cm = nc.semaphore("s_" + _n)
        sems[_n] = Sem(cm.__enter__())
        ctxs.append(cm)

    def W(eng, name, val):
        eng.wait_ge(sems[name].h, val)

    def I(name, instr, k=1):
        instr.then_inc(sems[name].h, k)
        sems[name].n += k
        return instr

    def DSER(eng, name):
        if sems[name].n:
            eng.wait_ge(sems[name].h, sems[name].n)

    blkcm = nc.Block()
    block = blkcm.__enter__()

    @block.vector
    def _(v):
        for i in range(8):
            v.memset(pbk[i][:, :], 0.0)
        v.memset(BA(O_UST, 0, 128, [[1, 32 * 128]]), 0.0)
        v.memset(BA(O_CST, 0, 128, [[1, 32 * 132]]), 0.0)
        v.memset(BA(O_BMB, 0, 128, [[1, 32]]), 0.0)
        v.memset(BA(O_CMS, 0, 128, [[1, 32]]), 0.0)
        v.memset(BA(O_PC, 0, 128, [[1, 512]]), 0.0)
        v.drain()
        I("pini", v.nop())

    @block.gpsimd
    def _(g):
        g.memset(onesb[:, :], 1.0)
        g.memset(hstr[:, :], 0.0)
        g.drain()
        g.affine_select(identb[:, :], onesb[:, :], [[-1, 128]], ALU.is_equal,
                        0.0, base=0, channel_multiplier=1)
        g.affine_select(revb[:, :], onesb[:, :], [[1, 128]], ALU.is_equal,
                        0.0, base=-127, channel_multiplier=1)
        g.memset(widen[:, :], 0.0)
        g.drain()
        g.tensor_copy(widen[:, 0:128], revb[:, :])
        g.drain()
        I("seed", g.drain())
        W(g, "pini", 1)
        for q in range(4):
            src = AP(tensor=a_in, offset=256 * q,
                     ap=[[16, 16], [1024, 32], [1, 16]])
            DSER(g, "gld")
            I("gld", g.dma_start(out=BA(O_PC, 32 * q, 16,
                                        [[16, 32], [1, 16]]), in_=src), 16)
        with nc.allow_non_contiguous_dma(reason="tiny seed scatters"):
            for q in range(4):
                src = AP(tensor=bm_in, offset=16 * q, ap=[[1, 16], [64, 32]])
                DSER(g, "gld")
                I("gld", g.dma_start(out=BA(O_BMB, 32 * q, 16, [[1, 32]]),
                                     in_=src), 16)
            for q in range(4):
                src = AP(tensor=bm_in, offset=16 * q, ap=[[1, 16], [64, 32]])
                DSER(g, "gld")
                I("gld", g.dma_start(out=BA(O_UST, 32 * q, 16, [[128, 32]]),
                                     in_=src), 16)
            for q in range(4):
                src = AP(tensor=cm_in, offset=16 * q, ap=[[1, 16], [64, 32]])
                DSER(g, "gld")
                I("gld", g.dma_start(out=BA(O_CST, 32 * q, 16, [[132, 32]]),
                                     in_=src), 16)
        with nc.allow_non_contiguous_dma(reason="tiny Dm scatter"):
            for q in range(4):
                src = AP(tensor=dm_in, offset=q, ap=[[0, 16], [4, 32]])
                DSER(g, "gld")
                I("gld", g.dma_start(out=BA(O_CMS, 32 * q, 16, [[1, 32]]),
                                     in_=src), 16)
        repw = AP(tensor=lnw_in, offset=0, ap=[[0, 128], [1, D]])
        DSER(g, "gld")
        I("gld", g.dma_start(out=lnw_rep[:, :], in_=repw), 16)
        repb = AP(tensor=lnb_in, offset=0, ap=[[0, 128], [1, D]])
        DSER(g, "gld")
        I("gld", g.dma_start(out=lnb_rep[:, :], in_=repb), 16)

    @block.sync
    def _(sync):
        for nm in ("bout", "bf", "bi", "bc", "bo", "inc"):
            tens = {"bout": bout_in, "bf": fv["bf"], "bi": fv["bi"],
                    "bc": fv["bc"], "bo": fv["bo"], "inc": fv["in_c"]}[nm]
            src = AP(tensor=tens, offset=0, ap=[[1, 128], [128, 8]])
            dstv = AP(tensor=fvec, offset=FB[nm] * 8, ap=[[48, 128], [1, 8]])
            with nc.allow_non_contiguous_dma(reason="tiny vec transpose"):
                I("cst", sync.dma_start(out=dstv, in_=src), 16)
        def r1spill(blk):
            W(sync, "r1c", blk + 1)
            DSER(sync, "r1d")
            I("r1d", sync.dma_start(
                out=AP(tensor=r1_sp, offset=blk * 128 * D,
                       ap=[[D, 128], [1, D]]),
                in_=BA(O_R1T + (blk % 2) * 2 * D, 0, 128, [[1, D]])), 16)
        for blk in range(32):
            b, c = blk // GL, blk % GL
            if blk >= 2:
                W(sync, "ln1", blk - 1)
                r1spill(blk - 2)
            DSER(sync, "x")
            I("x", sync.dma_start(
                out=xstg[:, (blk % 2) * D:(blk % 2) * D + D],
                in_=x_in[b, c * 128:(c + 1) * 128, :]), 16)
        r1spill(30)
        r1spill(31)

    @block.scalar
    def _(sc):
        for blk in range(32):
            W(sc, "x", 16 * (blk + 1))
            xt = xstg[:, (blk % 2) * D:(blk % 2) * D + D]
            sc.activation(FA(O_SQ1, 0, 128, [[1, D]]), xt, AF.Square,
                          accum_out=stat1[:, 1:2])
            sc.drain()
            I("sqa", sc.nop())
            W(sc, "ln1s", blk + 1)
            sc.activation(stat1[:, 3:4], stat1[:, 4:5], AF.Sqrt)
            sc.drain()
            I("sqt", sc.nop())
            W(sc, "ln1", blk + 1)
            if blk >= 2:
                W(sc, "r1d", 16 * (blk - 1))
            sc.activation(BA(O_R1T + (blk % 2) * 2 * D, 0, 128, [[1, D]]),
                          xt, AF.Identity, bias=stat1[:, 5:6],
                          scale=stat1[:, 6:7])
            sc.drain()
            I("r1c", sc.nop())

    @block.vector
    def _(v):
        for blk in range(32):
            b, c = blk // GL, blk % GL
            W(v, "x", 16 * (blk + 1))
            xt = xstg[:, (blk % 2) * D:(blk % 2) * D + D]
            v.reduce_sum(stat1[:, 0:1], xt, axis=AXL.X)
            W(v, "sqa", blk + 1)
            v.drain()
            v.tensor_scalar(stat1[:, 2:3], stat1[:, 0:1], 1.0 / D, None,
                            ALU.mult)
            v.tensor_scalar(stat1[:, 4:5], stat1[:, 1:2], 1.0 / D, None,
                            ALU.mult)
            v.drain()
            v.scalar_tensor_tensor(stat1[:, 4:5], stat1[:, 2:3],
                                   stat1[:, 2:3], stat1[:, 4:5],
                                   ALU.mult, ALU.subtract)
            v.drain()
            v.tensor_scalar(stat1[:, 4:5], stat1[:, 4:5], -1.0, EPS,
                            ALU.mult, ALU.add)
            v.drain()
            v.reciprocal(stat1[:, 4:5], stat1[:, 4:5])
            v.drain()
            I("ln1s", v.nop())
            W(v, "sqt", blk + 1)
            v.tensor_scalar(stat1[:, 5:6], stat1[:, 2:3], stat1[:, 3:4],
                            -1.0, ALU.mult, ALU.mult)
            v.tensor_scalar(stat1[:, 6:7], stat1[:, 3:4], 1.0, None, ALU.add)
            v.drain()
            v.tensor_scalar(BA(O_XL1 + 2 * (c * B + b), 0, 128, [[CB, D]]),
                            xt, stat1[:, 2:3], stat1[:, 3:4],
                            ALU.subtract, ALU.mult)
            v.drain()
            I("ln1", v.nop())

    @block.tensor
    def _(t):
        if PHASES < 2:
            return
        W(t, "gld", 16 * 16)
        W(t, "seed", 1)
        W(t, "pini", 1)
        for chhi in range(32):
            bank = chhi % 2
            for q in range(4):
                t.transpose(PAB(bank, 32 * q, 16, 16 * (chhi // 2), [[1, 16]]),
                            BA(O_PC + 32 * chhi, 32 * q, 16, [[1, 16]]),
                            identb[32 * q:32 * q + 16, 32 * q:32 * q + 16],
                            tile_position=(32 * q, 32 * q))
        I("chm", t.drain(), 2)
        for widx, (r, ch0, nch, colw, bank) in enumerate(CWAVES):
            ul, cl = _ul(r), _cl(r)
            pco = O_PC if r % 2 == 0 else O_PN
            pto = O_PTC if r % 2 == 0 else O_PTN
            need = 1 + RSTART[r]
            if widx >= 2:
                need = max(need, 1 + widx - 1)
            W(t, "che", need)
            for ci in range(nch):
                chhi = ch0 + ci
                cb0 = ci * colw
                for q in range(4):
                    tp = (32 * q, 32 * q)
                    ptt = BA(pto + 32 * chhi, 32 * q, 16, [[1, 16]])
                    pti = BA(pco + 32 * chhi, 32 * q, 16, [[1, 16]])
                    if ul:
                        t.matmul(PA(bank, 32 * q, 16, cb0, [[1, ul]]),
                                 ptt, BA(O_UST + 256 * chhi, 32 * q, 16,
                                         [[1, ul]]),
                                 start=True, stop=True, tile_position=tp)
                    if cl:
                        t.matmul(PA(bank, 32 * q, 16, cb0 + ul, [[1, cl]]),
                                 pti, BA(O_CST + 264 * chhi, 32 * q, 16,
                                         [[1, cl]]),
                                 start=True, stop=True, tile_position=tp)
                    t.matmul(PA(bank, 32 * q, 16, cb0 + ul + cl, [[1, 16]]),
                             ptt, pti, start=True, stop=True, tile_position=tp)
                    t.matmul(PA(bank, 32 * q, 16, cb0 + ul + cl + 16,
                                [[1, 16]]),
                             pti, ptt, start=True, stop=True, tile_position=tp)
            I("chm", t.drain())

    @block.vector
    def _(v):
        if PHASES < 2:
            return
        W(v, "chm", 2)
        for bank in range(2):
            v.tensor_copy(BA(O_PTC + 32 * bank, 0, 128, [[32, 16], [1, 16]]),
                          PAB(bank, 0, 128, 0, [[16, 16], [1, 16]]))
        v.drain()
        I("che", v.nop())
        for widx, (r, ch0, nch, colw, bank) in enumerate(CWAVES):
            ul, cl = _ul(r), _cl(r)
            pno = O_PN if r % 2 == 0 else O_PC
            ptno = O_PTN if r % 2 == 0 else O_PTC
            W(v, "chm", 3 + widx)
            if ul:
                v.tensor_copy(
                    BA(O_UST + 2 * (ch0 * 128 + (1 << r)), 0, 128,
                       [[128, nch], [1, ul]]),
                    PA(bank, 0, 128, 0, [[colw, nch], [1, ul]]))
            if cl:
                cdst = (1 << r) if r <= 6 else 128
                v.tensor_copy(
                    BA(O_CST + 2 * (ch0 * 132 + cdst), 0, 128,
                       [[132, nch], [1, cl]]),
                    PA(bank, 0, 128, ul, [[colw, nch], [1, cl]]))
            v.tensor_copy(BA(pno + 32 * ch0, 0, 128, [[16, nch], [1, 16]]),
                          PA(bank, 0, 128, ul + cl, [[colw, nch], [1, 16]]))
            v.tensor_copy(BA(ptno + 32 * ch0, 0, 128, [[16, nch], [1, 16]]),
                          PA(bank, 0, 128, ul + cl + 16,
                             [[colw, nch], [1, 16]]))
            if r >= 6:
                v.tensor_copy(
                    BA(O_ALP + 2 * ((r - 6) * 512 + ch0 * 16), 0, 128,
                       [[16, nch], [1, 16]]),
                    PA(bank, 0, 128, ul + cl + 16, [[colw, nch], [1, 16]]))
            v.drain()
            I("che", v.nop())

    CH_DONE = 1 + len(CWAVES)

    @block.vector
    def _(v):
        if PHASES < 2.2:
            return
        W(v, "che", CH_DONE)
        v.tensor_tensor(BA(O_PRD, 0, 128, [[128, 32], [1, 128]]),
                        BA(O_CST, 0, 128, [[132, 32], [1, 128]]),
                        BA(O_BMB, 0, 128, [[1, 32], [0, 128]]),
                        ALU.mult)
        v.drain()
        I("kbe", v.nop())
        W(v, "gld", 16 * 20)
        v.scalar_tensor_tensor(BA(O_PRD, 0, 128, [[128, 32]]),
                               BA(O_CMS, 0, 128, [[1, 32]]), 1.0 / 16,
                               BA(O_PRD, 0, 128, [[128, 32]]),
                               ALU.mult, ALU.add)
        v.drain()
        I("kbe", v.nop())

    O_UBSCR = 64 * KB

    @block.tensor
    def _(t):
        if PHASES < 2.21:
            return
        W(t, "kbe", 2)
        NGRP_KR = 1 if PHASES < 2.3 else 4
        for grp in range(NGRP_KR):
            bank = 2 + (grp % 2)
            if grp >= 2:
                W(t, "kbe", grp + 1)
            for q in range(4):
                for k8 in range(8):
                    ci = k8 * 4 + q
                    chl = grp * 32 + ci
                    chhi = chl // 4
                    lhsT = AP(tensor=arb,
                              offset=32 * q * PSB + (O_PRD // 2) + chhi * 128,
                              ap=[[PSB, 16], [1, 128]])
                    t.matmul(PA(bank, 0, 128, ci, [[1, 1]]),
                             lhsT, onesb[32 * q:32 * q + 16, 0:1],
                             start=True, stop=True,
                             tile_position=(32 * q, 0))
                t.drain()
            I("kbm", t.drain())
        if PHASES < 2.4:
            return
        for grp in range(4):
            bank = 4 + (grp % 2)
            if grp >= 2:
                W(t, "kbe", grp + 5)
            for q in range(4):
                for k8 in range(8):
                    ci = k8 * 4 + q
                    chl = grp * 32 + ci
                    chhi = chl // 4
                    lhsT = AP(tensor=arb,
                              offset=32 * q * PSB + (O_UST // 2) + chhi * 128,
                              ap=[[PSB, 16], [1, 128]])
                    t.matmul(PA(bank, 0, 128, 16 * ci, [[1, 16]]),
                             lhsT, identb[32 * q:32 * q + 16,
                                          32 * q:32 * q + 16],
                             start=True, stop=True,
                             tile_position=(32 * q, 0))
                t.drain()
            I("kbm", t.drain())
        if PHASES < 2.45:
            return
        for grp in range(4):
            bank = 6 + (grp % 2)
            W(t, "kbe", 7 + grp)
            if grp >= 2:
                W(t, "kbe", 11 + grp - 2)
            t.matmul(PA(bank, 0, 128, 0, [[1, 512]]),
                     revb[:, :],
                     BA(O_UBSCR + (grp % 2) * 1024, 0, 128, [[1, 512]]),
                     start=True, stop=True)
            I("kbm", t.drain())
        if PHASES < 2.5:
            return
        W(t, "kbe", 14)
        for jj in range(32):
            bank = jj % 2
            if jj >= 2:
                W(t, "kbe", 14 + jj - 1)
            for j4 in range(4):
                j = jj * 4 + j4
                t.matmul(PA(bank, 0, 128, 128 * j4, [[1, 128]]),
                         widen[:, 127 - j:127 - j + 128],
                         BA(O_KRT, 0, 128, [[1, 128]]),
                         start=True, stop=True)
            I("kbm", t.drain())

    @block.vector
    def _(v):
        if PHASES < 2.3:
            return
        for grp in range(4):
            bank = 2 + (grp % 2)
            W(v, "kbm", grp + 1)
            with nc.allow_low_precision(reason="bf16 kernel coeffs"):
                v.tensor_copy(BA(O_KRT + 2 * (32 * grp), 0, 128, [[1, 32]]),
                              PA(bank, 0, 128, 0, [[1, 32]]))
            v.drain()
            I("kbe", v.nop())
        if PHASES < 2.4:
            return
        for grp in range(4):
            bank = 4 + (grp % 2)
            W(v, "kbm", 4 + grp + 1)
            if grp >= 2:
                W(v, "kbm", 9 + grp - 2)
            with nc.allow_low_precision(reason="bf16 U operator"):
                v.tensor_copy(BA(O_UBSCR + (grp % 2) * 1024, 0, 128,
                                 [[16, 32], [1, 16]]),
                              PA(bank, 0, 128, 0, [[16, 32], [1, 16]]))
            v.drain()
            I("kbe", v.nop())
        if PHASES < 2.45:
            return
        for grp in range(4):
            bank = 6 + (grp % 2)
            W(v, "kbm", 9 + grp)
            v.tensor_copy(BA(O_ULOC + 2 * (32 * grp * 16), 0, 128,
                             [[16, 32], [1, 16]]),
                          PA(bank, 0, 128, 0, [[16, 32], [1, 16]]))
            v.drain()
            I("kbe", v.nop())
        if PHASES < 2.5:
            return
        for jj in range(32):
            bank = jj % 2
            W(v, "kbm", 12 + jj + 1)
            v.tensor_copy(
                BA(O_KLOC + 2 * (jj * 4), 0, 128, [[128, 128], [1, 4]]),
                PA(bank, 0, 128, 0, [[1, 128], [128, 4]]))
            v.drain()
            I("kbe", v.nop())

    KB_DONE = 46

    @block.sync
    def _(sync):
        if PHASES < 3:
            return
        W(sync, "kbe", KB_DONE)
        I("bd", sync.dma_start(
            out=AP(tensor=blob_loc, offset=0, ap=[[DCH * 128, 128], [1, DCH * 128]]),
            in_=BA(O_KLOC, 0, 128, [[1, DCH * 128]])), 16)
        I("bd", sync.dma_start(
            out=AP(tensor=blob_loc, offset=KSEC, ap=[[DCH * 16, 128], [1, DCH * 16]]),
            in_=BA(O_ULOC, 0, 128, [[1, DCH * 16]])), 16)
        I("bd", sync.dma_start(
            out=AP(tensor=blob_loc, offset=KSEC + USEC,
                   ap=[[32 * 128, 128], [1, 32 * 128]]),
            in_=BA(O_CST + 2, 0, 128, [[132, 32], [1, 128]])), 16)

    @block.gpsimd
    def _(g):
        if PHASES < 3:
            return
        W(g, "bd", 16 * 3)
        I("ag1", g.collective_compute(
            "AllGather", ALU.bypass, replica_groups=[list(range(R))],
            ins=[blob_loc.ap().opt()], outs=[blob_ag.ap().opt()]))

    @block.sync
    def _(sync):
        if PHASES < 3:
            return
        for wv in range(16):
            gch0 = wv * 64
            rank, chl0 = gch0 // DCH, gch0 % DCH
            if wv >= 2:
                W(sync, "umm", wv - 1)
            W(sync, "ag1", 1)
            src = AP(tensor=blob_ag, offset=rank * BLOB + KSEC + chl0 * 16,
                     ap=[[DCH * 16, 128], [1, 64 * 16]])
            DSER(sync, "ust")
            I("ust", sync.dma_start(
                out=AP(tensor=arb, offset=(O_USTR // 2) + (wv % 2) * 1024,
                       ap=[[PSB, 128], [1, 1024]]),
                in_=src), 16)

    @block.tensor
    def _(t):
        if PHASES < 3:
            return
        W(t, "ln1", 32)
        for wv in range(16):
            W(t, "ust", 16 * (wv + 1))
            bank = 6 + (wv % 2)
            if wv >= 2:
                W(t, "sev", wv - 1)
            for q in range(4):
                for k16 in range(16):
                    ci = k16 * 4 + q
                    ch = wv * 64 + ci
                    chhi = (ch // 4) % 16
                    lhs = AP(tensor=arb,
                             offset=(O_USTR // 2) + (wv % 2) * 1024 + ci * 16,
                             ap=[[PSB, 128], [1, 16]])
                    rhs = BA(O_XL1 + 2 * ch * CB, 0, 128, [[1, CB]])
                    t.matmul(PA(bank, 32 * q, 16, 32 * chhi, [[1, CB]]),
                             lhs, rhs, start=True, stop=True,
                             tile_position=(0, 32 * q))
                t.drain()
            I("umm", t.drain())

    @block.vector
    def _(v):
        if PHASES < 3:
            return
        for wv in range(16):
            W(v, "umm", wv + 1)
            bank = 6 + (wv % 2)
            v.tensor_copy(BA(O_SST + 2 * (wv * 16 * CB), 0, 128,
                             [[CB, 16], [1, CB]]),
                          PA(bank, 0, 128, 0, [[CB, 16], [1, CB]]))
            v.drain()
            I("sev", v.nop())

    @block.sync
    def _(sync):
        if PHASES < 3:
            return
        W(sync, "sev", 16)
        for tgt in range(R):
            I("sd", sync.dma_start(
                out=AP(tensor=s_blob, offset=tgt * SSEC,
                       ap=[[32 * CB, 128], [1, 32 * CB]]),
                in_=BA(O_SST + 2 * (tgt * 32 * CB), 0, 128, [[1, 32 * CB]])), 16)

    @block.gpsimd
    def _(g):
        if PHASES < 3:
            return
        W(g, "sd", 16 * R)
        with allow_ici_all_to_all():
            I("a21", g.collective_compute(
                "AllToAll", ALU.bypass, replica_groups=[list(range(R))],
                ins=[s_blob.ap().opt()], outs=[s_a2a.ap().opt()]))

    @block.sync
    def _(sync):
        if PHASES < 4:
            return
        W(sync, "a21", 1)
        W(sync, "r1d", 16 * 32)
        I("h0", sync.dma_start(
            out=AP(tensor=arb, offset=O_H // 2,
                   ap=[[PSB, 128], [1024, 8], [1, 1024]]),
            in_=AP(tensor=s_a2a, offset=0,
                   ap=[[1024, 128], [SSEC, 8], [1, 1024]])), 16)


    @block.tensor
    def _(t):
        if PHASES < 4:
            return
        W(t, "h0", 16)
        for r in range(5):
            sh = 1 << r
            nf, rem = (32 - sh) // 4, (32 - sh) % 4
            for wv in range(16):
                widx = r * 16 + wv
                bank = widx % 2
                if widx >= 2:
                    W(t, "pfa", widx - 1)
                for ci in range(2):
                    chhi = wv * 2 + ci
                    for q in range(4):
                        lhs = BA(O_ALP + 2 * (r * 512 + chhi * 16),
                                 32 * q, 16, [[1, 16]])
                        rhs1 = BA(O_H + 2 * (chhi * 32), 32 * q, 16,
                                  [[1024, nf], [8, 4], [1, 8]])
                        t.matmul(PA(bank, 32 * q, 16, ci * 256, [[1, nf * 32]]),
                                 lhs, rhs1, start=True, stop=True,
                                 tile_position=(32 * q, 32 * q))
                        if rem:
                            rhs2 = BA(O_H + 2 * (nf * 1024 + chhi * 32),
                                      32 * q, 16, [[8, rem], [1, 8]])
                            t.matmul(PA(bank, 32 * q, 16,
                                        ci * 256 + nf * 32, [[1, rem * 8]]),
                                     lhs, rhs2, start=True, stop=True,
                                     tile_position=(32 * q, 32 * q))
                I("pfm", t.drain())

    def _prefix_adds(eng, parity):
        for r in range(5):
            sh = 1 << r
            for wv in range(16):
                widx = r * 16 + wv
                if parity is not None and widx % 2 != parity:
                    continue
                bank = widx % 2
                W(eng, "pfm", widx + 1)
                for ci in range(2):
                    chhi = wv * 2 + ci
                    if sh < 4:
                        hd = BA(O_H + 2 * (chhi * 32 + sh * 8), 0, 128,
                                [[1, (4 - sh) * 8]])
                        eng.tensor_tensor(hd, PA(bank, 0, 128, ci * 256,
                                                 [[1, (4 - sh) * 8]]),
                                          hd, ALU.add)
                        bd = BA(O_H + 2 * (chhi * 32 + 1024), 0, 128,
                                [[1024, 7], [8, 4], [1, 8]])
                        eng.tensor_tensor(bd, PA(bank, 0, 128,
                                                 ci * 256 + (4 - sh) * 8,
                                                 [[1, 224]]),
                                          bd, ALU.add)
                    else:
                        nsrc = 8 - sh // 4
                        bd = BA(O_H + 2 * (chhi * 32 + (sh // 4) * 1024),
                                0, 128, [[1024, nsrc], [8, 4], [1, 8]])
                        eng.tensor_tensor(bd, PA(bank, 0, 128, ci * 256,
                                                 [[1, nsrc * 32]]),
                                          bd, ALU.add)
                eng.drain()
                W(eng, "pfa", widx)
                I("pfa", eng.nop())

    @block.vector
    def _(v):
        if PHASES < 4:
            return
        _prefix_adds(v, None)

    @block.vector
    def _(v):
        if PHASES < 4:
            return
        W(v, "pfa", 80)
        v.tensor_copy(BA(O_H2 + 2 * 8, 0, 128,
                         [[1024, 8], [32, 32], [8, 3], [1, 8]]),
                      BA(O_H, 0, 128, [[1024, 8], [32, 32], [8, 3], [1, 8]]))
        v.tensor_copy(BA(O_H2 + 2 * 1024, 0, 128,
                         [[1024, 7], [32, 32], [1, 8]]),
                      BA(O_H + 2 * 24, 0, 128, [[1024, 7], [32, 32], [1, 8]]))
        v.memset(BA(O_H2, 0, 128, [[32, 32], [1, 8]]), 0.0)
        v.drain()
        I("hsh", v.nop())

    @block.sync
    def _(sync):
        if PHASES < 4:
            return
        W(sync, "hsh", 1)
        for tgt in range(R):
            I("hd", sync.dma_start(
                out=AP(tensor=h_blob, offset=tgt * SSEC,
                       ap=[[1024, 128], [1, 1024]]),
                in_=BA(O_H2 + 2 * (tgt * 1024), 0, 128, [[1, 1024]])), 16)

    @block.gpsimd
    def _(g):
        if PHASES < 4:
            return
        W(g, "hd", 16 * R)
        with allow_ici_all_to_all():
            I("a22", g.collective_compute(
                "AllToAll", ALU.bypass, replica_groups=[list(range(R))],
                ins=[h_blob.ap().opt()], outs=[h_a2a.ap().opt()]))

    @block.sync
    def _(sync):
        if PHASES < 4:
            return
        W(sync, "ag1", 1)
        W(sync, "a22", 1)
        for bt in range(64):
            gch0 = bt * 16
            rank, chl0 = gch0 // DCH, gch0 % DCH
            chhi0 = chl0 // 4
            if bt >= 2:
                W(sync, "kmm", bt - 1)
            src = AP(tensor=blob_ag, offset=rank * BLOB + chl0 * 128,
                     ap=[[DCH * 128, 128], [1, 16 * 128]])
            DSER(sync, "kst")
            I("kst", sync.dma_start(
                out=AP(tensor=kstr, offset=(bt % 2) * 2048,
                       ap=[[4096, 128], [1, 2048]]),
                in_=src), 16)
            srcw = AP(tensor=blob_ag,
                      offset=rank * BLOB + KSEC + USEC + chhi0 * 128,
                      ap=[[32 * 128, 128], [1, 512]])
            DSER(sync, "wld")
            I("wld", sync.dma_start(
                out=AP(tensor=wstr, offset=(bt % 2) * 512,
                       ap=[[1024, 128], [1, 512]]),
                in_=srcw), 16)
            srch = AP(tensor=h_a2a, offset=rank * SSEC + chhi0 * 32,
                      ap=[[1024, 128], [1, 128]])
            DSER(sync, "hld")
            I("hld", sync.dma_start(
                out=AP(tensor=hstr, offset=(bt % 2) * 128,
                       ap=[[256, 128], [1, 128]]),
                in_=srch), 16)

    @block.tensor
    def _(t):
        if PHASES < 4:
            return
        W(t, "pfa", 80)
        for bt in range(64):
            W(t, "kst", 16 * (bt + 1))
            W(t, "wld", 16 * (bt + 1))
            W(t, "hld", 16 * (bt + 1))
            bank = bt % 2
            if bt >= 2:
                W(t, "yev", bt - 1)
            for q in range(4):
                for k4 in range(4):
                    ci = k4 * 4 + q
                    ch = bt * 16 + ci
                    chs = ci // 4
                    lhs = AP(tensor=kstr, offset=(bt % 2) * 2048 + ci * 128,
                             ap=[[4096, 128], [1, 128]])
                    rhs = BA(O_XL1 + 2 * ch * CB, 0, 128, [[1, CB]])
                    t.matmul(PA(bank, 0, 128, CB * ci, [[1, CB]]),
                             lhs, rhs, start=True, stop=False)
                    lhsw = AP(tensor=wstr, offset=32 * q * 1024
                              + (bt % 2) * 512 + chs * 128,
                              ap=[[1024, 16], [1, 128]])
                    rhsh = AP(tensor=hstr, offset=32 * q * 256
                              + (bt % 2) * 128 + chs * 32,
                              ap=[[256, 16], [1, 32]])
                    t.matmul(PA(bank, 0, 128, ci * 32, [[1, 32]]),
                             lhsw, rhsh, start=False, stop=True,
                             tile_position=(32 * q, 0))
                t.drain()
            I("kmm", t.drain())

    @block.vector
    def _(v):
        if PHASES < 4:
            return
        W(v, "sd", 16 * R)
        for bt in range(64):
            W(v, "kmm", bt + 1)
            bank = bt % 2
            v.tensor_copy(
                BA(O_Y + 2 * (bt * 16), 0, 128,
                   [[1, 16], [D, GL], [GL * D, B]]),
                PA(bank, 0, 128, 0, [[CB, 16], [B, GL], [1, B]]))
            v.drain()
            I("yev", v.nop())

    def _wload(g, src_t, offb):
        return g.dma_start(
            out=BA(offb, 0, 128, [[1024, 8], [1, 1024]]),
            in_=AP(tensor=src_t, offset=0,
                   ap=[[1024, 128], [131072, 8], [1, 1024]]))

    @block.gpsimd
    def _(g):
        if PHASES < 5:
            return
        W(g, "hd", 16 * R)
        DSER(g, "wgt"); I("wgt", _wload(g, woutT_in, O_WOT), 16)
        W(g, "kmm", 64)
        DSER(g, "wgt"); I("wgt", _wload(g, fw["wf"], OF["wf"]), 16)
        DSER(g, "wgt"); I("wgt", _wload(g, fw["uf"], OF["uf"]), 16)
        DSER(g, "wgt"); I("wgt", _wload(g, fw["wi"], OF["wi"]), 16)

    @block.sync
    def _(sync):
        if PHASES < 5:
            return
        for pair in range(16):
            if pair >= 2:
                W(sync, "trm", pair - 1)
            for hb in range(2):
                blk = (pair // 2) * GL + (pair % 2) * 2 + hb
                DSER(sync, "r1l")
                I("r1l", sync.dma_start(
                    out=BA(O_R1D + (pair % 2) * 4096 + hb * 2048,
                           0, 128, [[1, 1024]]),
                    in_=AP(tensor=r1_sp, offset=blk * 128 * D,
                           ap=[[D, 128], [1, D]])), 16)
            if pair % 2 == 1:
                b = pair // 2
                W(sync, "aev", (b * 2 + 2) * 8)
                I("asp", sync.dma_start(
                    out=AP(tensor=at_sp, offset=b * 8 * 128 * SL,
                           ap=[[SL, 128], [128 * SL, 8], [1, SL]]),
                    in_=BA(O_ATS, 0, 128, [[512, 8], [1, SL]])), 16)
                if PHASES >= 6:
                    t_ = b
                    W(sync, "asp", 16 * (t_ + 1))
                    if t_ >= 1:
                        W(sync, "x2c", t_)
                    DSER(sync, "xsl")
                    I("xsl", sync.dma_start(
                        out=BA(O_ATD2B, 0, 128, [[512, 8], [1, SL]]),
                        in_=AP(tensor=at_sp, offset=t_ * 8 * 128 * SL,
                               ap=[[SL, 128], [128 * SL, 8], [1, SL]])), 16)
                    if t_ >= 1:
                        DSER(sync, "x2d")
                        I("x2d", sync.dma_start(
                            out=AP(tensor=xl2_sp,
                                   offset=(t_ - 1) * 8 * 128 * SL,
                                   ap=[[SL, 128], [128 * SL, 8], [1, SL]]),
                            in_=BA(O_X2DBB, 0, 128,
                                   [[512, 8], [1, SL]])), 16)

    @block.vector
    def _(v):
        if PHASES < 5:
            return
        W(v, "yev", 64)
        W(v, "gld", 16 * 22)
        for pair in range(16):
            b, cp = pair // 2, pair % 2
            for hb in range(2):
                bi = pair * 2 + hb
                c = cp * 2 + hb
                yblk = BA(O_Y + 2 * (b * GL * D + c * D), 0, 128, [[1, D]])
                v.reduce_sum(stat1[:, 0:1], yblk, axis=AXL.X)
                W(v, "sqa", 32 + bi + 1)
                v.drain()
                v.tensor_scalar(stat1[:, 2:3], stat1[:, 0:1], 1.0 / D, None,
                                ALU.mult)
                v.tensor_scalar(stat1[:, 4:5], stat1[:, 1:2], 1.0 / D, None,
                                ALU.mult)
                v.drain()
                v.scalar_tensor_tensor(stat1[:, 4:5], stat1[:, 2:3],
                                       stat1[:, 2:3], stat1[:, 4:5],
                                       ALU.mult, ALU.subtract)
                v.drain()
                v.tensor_scalar(stat1[:, 4:5], stat1[:, 4:5], -1.0, EPS,
                                ALU.mult, ALU.add)
                v.drain()
                v.reciprocal(stat1[:, 4:5], stat1[:, 4:5])
                v.drain()
                I("ln1s", v.nop())
                W(v, "sqt", 32 + bi + 1)
                if bi >= 2:
                    W(v, "l2m", pair)
                    W(v, "gtrs", bi - 1)
                zsl = BA(O_ZB + hb * 2048, 0, 128, [[1, D]])
                v.tensor_scalar(zsl, yblk, stat1[:, 2:3], stat1[:, 3:4],
                                ALU.subtract, ALU.mult)
                v.drain()
                v.tensor_tensor(zsl, zsl, lnw_rep[:, :], ALU.mult)
                v.drain()
                v.tensor_tensor(zsl, zsl, lnb_rep[:, :], ALU.add)
                v.drain()
                I("iln", v.nop())
            W(v, "gtrs", pair * 2 + 2)
            W(v, "aev", (pair + 1) * 8)
            atsl = BA(O_ATS + 2 * (cp * 256), 0, 128, [[512, 8], [1, 256]])
            v.tensor_tensor(BA(O_ZB, 0, 128, [[256, 8], [1, 256]]),
                            atsl, atsl, ALU.mult)
            v.drain()
            I("l2e", v.nop())
            W(v, "l2m", pair + 1)
            mu_p = FA(O_MU3 + pair * 2048, 0, 128, [[1, 256]])
            rs_p = FA(O_RS3 + pair * 2048, 0, 128, [[1, 256]])
            v.tensor_copy(mu_p, PA(4, 0, 128, 0, [[1, 256]]))
            v.tensor_copy(rs_p, PA(4, 0, 128, 256, [[1, 256]]))
            v.drain()
            I("l2e", v.nop())
            v.tensor_scalar(mu_p, mu_p, 1.0 / D, None, ALU.mult)
            v.tensor_scalar(rs_p, rs_p, 1.0 / D, None, ALU.mult)
            v.drain()
            v.tensor_tensor(PA(4, 0, 128, 0, [[1, 256]]), mu_p, mu_p,
                            ALU.mult)
            v.drain()
            v.tensor_tensor(rs_p, rs_p, PA(4, 0, 128, 0, [[1, 256]]),
                            ALU.subtract)
            v.drain()
            v.tensor_scalar(rs_p, rs_p, 1.0, EPS, ALU.mult, ALU.add)
            v.drain()
            v.reciprocal(rs_p, rs_p)
            v.drain()
            I("l2p", v.nop())
            if PHASES >= 6 and pair % 2 == 1 and pair >= 3:
                t_ = (pair - 3) // 2
                W(v, "l2q", 2 * t_ + 2)
                W(v, "xsl", 16 * (t_ + 1))
                if t_ >= 1:
                    W(v, "x2d", 16 * t_)
                v.tensor_tensor(BA(O_X2DBB, 0, 128, [[512, 8], [1, SL]]),
                                BA(O_ATD2B, 0, 128, [[512, 8], [1, SL]]),
                                FA(O_MU3 + t_ * 4096, 0, 128,
                                   [[0, 8], [512, 2], [1, 256]]),
                                ALU.subtract)
                v.drain()
                v.tensor_tensor(BA(O_X2DBB, 0, 128, [[512, 8], [1, SL]]),
                                BA(O_X2DBB, 0, 128, [[512, 8], [1, SL]]),
                                FA(O_RS3 + t_ * 4096, 0, 128,
                                   [[0, 8], [512, 2], [1, 256]]), ALU.mult)
                v.drain()
                I("x2c", v.nop())

    @block.scalar
    def _(sc):
        if PHASES < 5:
            return
        W(sc, "yev", 64)
        for pair in range(16):
            b, cp = pair // 2, pair % 2
            for hb in range(2):
                bi = pair * 2 + hb
                c = cp * 2 + hb
                yblk = BA(O_Y + 2 * (b * GL * D + c * D), 0, 128, [[1, D]])
                if bi >= 1:
                    W(sc, "ln1s", 32 + bi)
                sc.activation(FA(O_SQ2, 0, 128, [[1, D]]), yblk, AF.Square,
                              accum_out=stat1[:, 1:2])
                sc.drain()
                I("sqa", sc.nop())
                W(sc, "ln1s", 32 + bi + 1)
                sc.activation(stat1[:, 3:4], stat1[:, 4:5], AF.Sqrt)
                sc.drain()
                I("sqt", sc.nop())
                W(sc, "iln", bi + 1)
                zsl = BA(O_ZB + hb * 2048, 0, 128, [[1, D]])
                sc.activation(zsl, zsl, AF.Gelu)
                sc.drain()
                I("gev", sc.nop())
            W(sc, "gtrs", pair * 2 + 2)
            if pair >= 1:
                W(sc, "trm", pair)
            for hb in range(2):
                sc.activation(BA(O_GF + 2 * (hb * 128), 0, 128,
                                 [[256, 8], [1, 128]]),
                              PAB(6 + hb, 0, 128, 0, [[128, 8], [1, 128]]),
                              AF.Identity)
            sc.drain()
            I("gfe", sc.nop())
            if pair == 0:
                W(sc, "cst", 96)
            W(sc, "trm", pair + 1)
            if b >= 1:
                W(sc, "asp", 16 * b)
            for dck in range(8):
                bank = dck // 2
                sc.activation(
                    BA(O_ATS + 2 * (dck * 512 + cp * 256), 0, 128, [[1, 256]]),
                    PA(bank, 0, 128, (dck % 2) * 256, [[1, 256]]),
                    AF.Identity,
                    bias=fvec[:, FB["bout"] * 8 + dck:FB["bout"] * 8 + dck + 1])
            sc.drain()
            I("aev", sc.nop(), 8)
            W(sc, "l2p", pair + 1)
            sc.activation(FA(O_RS3 + pair * 2048, 0, 128, [[1, 256]]),
                          FA(O_RS3 + pair * 2048, 0, 128, [[1, 256]]),
                          AF.Sqrt)
            sc.drain()
            I("l2q", sc.nop())

    @block.tensor
    def _(t):
        if PHASES < 5:
            return
        W(t, "wgt", 16)
        for pair in range(16):
            b, cp = pair // 2, pair % 2
            for hb in range(2):
                bi = pair * 2 + hb
                W(t, "gev", bi + 1)
                if pair >= 1:
                    W(t, "gfe", pair)
                gsl = O_ZB + hb * 2048
                for dk in range(8):
                    t.transpose(PAB(6 + hb, 0, 128, dk * 128, [[1, 128]]),
                                BA(gsl + 2 * (dk * 128), 0, 128, [[1, 128]]),
                                identb[:, :])
                I("gtrs", t.drain())
            W(t, "gfe", pair + 1)
            W(t, "r1l", 16 * 2 * (pair + 1))
            if pair >= 1:
                W(t, "aev", pair * 8)
            for dck in range(8):
                bank = dck // 2
                for dk in range(8):
                    t.matmul(PA(bank, 0, 128, (dck % 2) * 256, [[1, 256]]),
                             BA(O_WOT + 2 * (dk * 1024 + dck * 128),
                                0, 128, [[1, 128]]),
                             BA(O_GF + 2 * (dk * 256), 0, 128, [[1, 256]]),
                             start=(dk == 0), stop=False)
                for hb in range(2):
                    t.matmul(PA(bank, 0, 128, (dck % 2) * 256 + hb * 128,
                                [[1, 128]]),
                             BA(O_R1D + (pair % 2) * 4096 + hb * 2048
                                + 2 * (dck * 128), 0, 128, [[1, 128]]),
                             identb[:, :], start=False, stop=(hb == 1))
            I("trm", t.drain())
            W(t, "l2e", pair * 2 + 1)
            for dck in range(8):
                t.matmul(PA(4, 0, 128, 0, [[1, 256]]), onesb[:, :],
                         BA(O_ATS + 2 * (dck * 512 + cp * 256), 0, 128,
                            [[1, 256]]),
                         start=(dck == 0), stop=(dck == 7))
            for dck in range(8):
                t.matmul(PA(4, 0, 128, 256, [[1, 256]]), onesb[:, :],
                         BA(O_ZB + 2 * (dck * 256), 0, 128, [[1, 256]]),
                         start=(dck == 0), stop=(dck == 7))
            I("l2m", t.drain())





    @block.gpsimd
    def _(g):
        if PHASES < 6:
            return
        W(g, "trm", 16)
        DSER(g, "wgt"); I("wgt", _wload(g, fw["wo"], OF["wo"]), 16)
        W(g, "iln", 32)
        DSER(g, "wgt"); I("wgt", _wload(g, fw["uo"], OF["uo"]), 16)
        W(g, "x2c", 8)
        DSER(g, "wgt"); I("wgt", _wload(g, fw["wc"], OF["wc"]), 16)
        DSER(g, "wgt"); I("wgt", _wload(g, fw["uc"], OF["uc"]), 16)
        W(g, "x2d", 16 * 8)
        DSER(g, "wgt"); I("wgt", _wload(g, fw["ui"], OF["ui"]), 16)

    @block.sync
    def _(sync):
        if PHASES < 6:
            return
        W(sync, "x2c", T)
        DSER(sync, "x2d")
        I("x2d", sync.dma_start(
            out=AP(tensor=xl2_sp, offset=(T - 1) * 8 * 128 * SL,
                   ap=[[SL, 128], [128 * SL, 8], [1, SL]]),
            in_=BA(O_X2DBB, 0, 128, [[512, 8], [1, SL]])), 16)
        W(sync, "trm", 16)

        def _pwds(tt):
            for dck in range(8):
                gi = tt * 8 + dck
                W(sync, "pwe", gi + 1)
                DSER(sync, "pwd")
                I("pwd", sync.dma_start(
                    out=AP(tensor=pwo_sp, offset=(tt * 8 + dck) * 128 * SL,
                           ap=[[SL, 128], [1, SL]]),
                    in_=FA(O_PWEV + 4 * ((gi % 2) * SL), 0, 128,
                           [[1, SL]])), 16)
        for t_ in range(T):
            W(sync, "x2d", 16 * (t_ + 1))
            if t_ >= 2:
                W(sync, "pwm", 8 * (t_ - 1))
            DSER(sync, "xpl")
            I("xpl", sync.dma_start(
                out=BA(O_X2DB + (t_ % 2) * 8192, 0, 128,
                       [[512, 8], [1, SL]]),
                in_=AP(tensor=xl2_sp, offset=t_ * 8 * 128 * SL,
                       ap=[[SL, 128], [128 * SL, 8], [1, SL]])), 16)
            if t_ >= 1:
                _pwds(t_ - 1)
        _pwds(T - 1)

    @block.vector
    def _(v):
        if PHASES < 6:
            return
        for t_ in (7,):
            W(v, "l2q", 2 * t_ + 2)
            W(v, "xsl", 16 * (t_ + 1))
            W(v, "x2d", 16 * t_)
            v.tensor_tensor(BA(O_X2DBB, 0, 128, [[512, 8], [1, SL]]),
                            BA(O_ATD2B, 0, 128, [[512, 8], [1, SL]]),
                            FA(O_MU3 + t_ * 4096, 0, 128,
                               [[0, 8], [512, 2], [1, 256]]), ALU.subtract)
            v.drain()
            v.tensor_tensor(BA(O_X2DBB, 0, 128, [[512, 8], [1, SL]]),
                            BA(O_X2DBB, 0, 128, [[512, 8], [1, SL]]),
                            FA(O_RS3 + t_ * 4096, 0, 128,
                               [[0, 8], [512, 2], [1, 256]]), ALU.mult)
            v.drain()
            I("x2c", v.nop())
        for t_ in range(T):
            for dck in range(8):
                gi = t_ * 8 + dck
                W(v, "pwm", gi + 1)
                if gi >= 2:
                    W(v, "pwd", 16 * (gi - 1))
                v.tensor_copy(FA(O_PWEV + 4 * ((gi % 2) * SL), 0, 128,
                                 [[1, SL]]),
                              PA(gi % 2, 0, 128, 0, [[1, SL]]))
                v.drain()
                I("pwe", v.nop())

    @block.tensor
    def _(t):
        if PHASES < 6:
            return
        W(t, "wgt", 16 * 5)
        for t_ in range(T):
            W(t, "xpl", 16 * (t_ + 1))
            h = (t_ % 2) * 8192
            for dck in range(8):
                gi = t_ * 8 + dck
                bank = gi % 2
                if gi >= 2:
                    W(t, "pwe", gi - 1)
                for dk in range(8):
                    t.matmul(PA(bank, 0, 128, 0, [[1, SL]]),
                             BA(OF["wo"] + 2 * (dk * 1024 + dck * 128),
                                0, 128, [[1, 128]]),
                             BA(O_X2DB + h + 2 * (dk * SL), 0, 128,
                                [[1, SL]]),
                             start=(dk == 0), stop=(dk == 7))
                I("pwm", t.drain())

    @block.vector
    def _(v):
        if PHASES < 6:
            return
        W(v, "pwm", 64)
        W(v, "x2c", 8)
        W(v, "cst", 96)
        for dk in range(8):
            v.tensor_copy(FA(O_CF + 4 * (dk * SL), 0, 128, [[1, SL]]),
                          AP(tensor=fvec, offset=FB["inc"] * 8 + dk,
                             ap=[[48, 128], [0, SL]]))
        v.drain()
        v.tensor_copy(BA(O_CBF, 0, 128, [[1, 8 * SL]]),
                      FA(O_CF, 0, 128, [[1, 8 * SL]]))
        v.drain()
        I("cbfs", v.nop())

    @block.sync
    def _(sync):
        if PHASES < 6:
            return
        W(sync, "x2d", 16 * 8)
        W(sync, "pwm", 64)
        for t_ in range(T):
            if t_ >= 2:
                W(sync, "gmm", (t_ - 1) * 8)
            DSER(sync, "x2l")
            I("x2l", sync.dma_start(
                out=BA(O_X2DB + (t_ % 2) * 8192, 0, 128, [[512, 8], [1, SL]]),
                in_=AP(tensor=xl2_sp, offset=t_ * 8 * 128 * SL,
                       ap=[[SL, 128], [128 * SL, 8], [1, SL]])), 16)

    @block.tensor
    def _(t):
        if PHASES < 6:
            return
        W(t, "wgt", 16 * 9)
        for t_ in range(T):
            W(t, "x2l", 16 * (t_ + 1))
            W(t, "cbfs", t_ + 1)
            h = (t_ % 2) * 8192
            for dck in range(8):
                gi = t_ * 8 + dck
                b0 = (dck % 2) * 3
                if gi >= 2:
                    W(t, "cup", gi - 1)
                for gname, wn, un, bank in (("f", "wf", "uf", b0),
                                            ("i", "wi", "ui", b0 + 1),
                                            ("g", "wc", "uc", b0 + 2)):
                    for dk in range(8):
                        t.matmul(PA(bank, 0, 128, 0, [[1, SL]]),
                                 BA(OF[wn] + 2 * (dk * 1024 + dck * 128),
                                    0, 128, [[1, 128]]),
                                 BA(O_X2DB + h + 2 * (dk * SL), 0, 128,
                                    [[1, SL]]),
                                 start=(dk == 0), stop=False)
                    for dk in range(8):
                        t.matmul(PA(bank, 0, 128, 0, [[1, SL]]),
                                 BA(OF[un] + 2 * (dk * 1024 + dck * 128),
                                    0, 128, [[1, 128]]),
                                 BA(O_CBF + 2 * (dk * SL), 0, 128, [[1, SL]]),
                                 start=False, stop=(dk == 7))
                I("gmm", t.drain())

    @block.scalar
    def _(sc):
        if PHASES < 6:
            return
        W(sc, "cst", 96)
        W(sc, "pwd", 16 * 64)
        for t_ in range(T):
            for dck in range(8):
                gi = t_ * 8 + dck
                b0 = (dck % 2) * 3
                W(sc, "gmm", gi + 1)
                if gi >= 1:
                    W(sc, "cup", gi)
                sc.activation(FA(O_GTF, 0, 128, [[1, SL]]),
                              PA(b0, 0, 128, 0, [[1, SL]]), AF.Sigmoid,
                              bias=fvec[:, FB["bf"] * 8 + dck:
                                        FB["bf"] * 8 + dck + 1])
                sc.activation(FA(O_GTI, 0, 128, [[1, SL]]),
                              PA(b0 + 1, 0, 128, 0, [[1, SL]]), AF.Sigmoid,
                              bias=fvec[:, FB["bi"] * 8 + dck:
                                        FB["bi"] * 8 + dck + 1])
                sc.activation(FA(O_GTG, 0, 128, [[1, SL]]),
                              PA(b0 + 2, 0, 128, 0, [[1, SL]]), AF.Tanh,
                              bias=fvec[:, FB["bc"] * 8 + dck:
                                        FB["bc"] * 8 + dck + 1])
                sc.drain()
                I("gac", sc.nop())

    @block.vector
    def _(v):
        if PHASES < 6:
            return
        for t_ in range(T):
            for dck in range(8):
                gi = t_ * 8 + dck
                W(v, "gac", gi + 1)
                csl = FA(O_CF + 4 * (dck * SL), 0, 128, [[1, SL]])
                v.tensor_tensor(csl, FA(O_GTF, 0, 128, [[1, SL]]), csl,
                                ALU.mult)
                v.tensor_tensor(FA(O_CT1, 0, 128, [[1, SL]]),
                                FA(O_GTI, 0, 128, [[1, SL]]),
                                FA(O_GTG, 0, 128, [[1, SL]]), ALU.mult)
                v.drain()
                v.tensor_tensor(csl, csl, FA(O_CT1, 0, 128, [[1, SL]]),
                                ALU.add)
                v.drain()
                I("cup", v.nop())
            v.tensor_copy(BA(O_CBF, 0, 128, [[1, 8 * SL]]),
                          FA(O_CF, 0, 128, [[1, 8 * SL]]))
            v.drain()
            I("cbfs", v.nop())

    @block.tensor
    def _(t):
        if PHASES < 7:
            return
        W(t, "cbfs", 9)
        W(t, "wgt", 16 * 9)
        for dck in range(8):
            bank = 6 + dck % 2
            if dck >= 2:
                W(t, "uoe", dck - 1)
            for dk in range(8):
                t.matmul(PA(bank, 0, 128, 0, [[1, SL]]),
                         BA(OF["uo"] + 2 * (dk * 1024 + dck * 128), 0, 128,
                            [[1, 128]]),
                         BA(O_CBF + 2 * (dk * SL), 0, 128, [[1, SL]]),
                         start=(dk == 0), stop=(dk == 7))
            I("uom", t.drain())

    @block.vector
    def _(v):
        if PHASES < 7:
            return
        for dck in range(8):
            W(v, "uom", dck + 1)
            v.tensor_copy(FA(O_UOO + 4 * (dck * SL), 0, 128, [[1, SL]]),
                          PA(6 + dck % 2, 0, 128, 0, [[1, SL]]))
            v.drain()
            I("uoe", v.nop())

    @block.sync
    def _(sync):
        if PHASES < 7:
            return
        W(sync, "gmm", 64)
        W(sync, "pwd", 16 * 64)
        for t_ in range(T):
            if t_ >= 1:
                W(sync, "hsl", 9 * t_)
            DSER(sync, "pwl")
            I("pwl", sync.dma_start(
                out=FA(O_PWDB, 0, 128, [[1, 8 * SL]]),
                in_=AP(tensor=pwo_sp, offset=t_ * 8 * 128 * SL,
                       ap=[[SL, 128], [128 * SL, 8], [1, SL]])), 16)
            if t_ >= 1:
                W(sync, "oev", t_)
                DSER(sync, "od")
                I("od", sync.dma_start(
                    out=AP(tensor=out_t, offset=(t_ - 1) * 8 * 128 * SL,
                           ap=[[SL, 128], [128 * SL, 8], [1, SL]]),
                    in_=FA(O_OSL, 0, 128, [[1, 8 * SL]])), 16)
            DSER(sync, "atl")
            I("atl", sync.dma_start(
                out=BA(O_ATD2, 0, 128, [[512, 8], [1, SL]]),
                in_=AP(tensor=at_sp, offset=t_ * 8 * 128 * SL,
                       ap=[[SL, 128], [128 * SL, 8], [1, SL]])), 16)

    @block.vector
    def _(v):
        if PHASES < 7:
            return
        W(v, "uoe", 8)
        for t_ in range(T):
            W(v, "pwl", 16 * (t_ + 1))
            v.tensor_tensor(FA(O_PWDB, 0, 128, [[1, 8 * SL]]),
                            FA(O_PWDB, 0, 128, [[1, 8 * SL]]),
                            FA(O_UOO, 0, 128, [[1, 8 * SL]]), ALU.add)
            v.drain()
            I("hsl", v.nop())
            W(v, "hsl", t_ * 9 + 9)
            W(v, "atl", 16 * (t_ + 1))
            if t_ >= 1:
                W(v, "od", 16 * t_)
            v.tensor_tensor(FA(O_OSL, 0, 128, [[1, 8 * SL]]),
                            BA(O_ATD2, 0, 128, [[1, 8 * SL]]),
                            BA(O_HSL, 0, 128, [[1, 8 * SL]]), ALU.add)
            v.drain()
            I("oev", v.nop())

    @block.scalar
    def _(sc):
        if PHASES < 7:
            return
        W(sc, "cst", 96)
        for t_ in range(T):
            W(sc, "hsl", t_ * 9 + 1)
            for dck in range(8):
                sc.activation(BA(O_HSL + 2 * (dck * SL), 0, 128, [[1, SL]]),
                              FA(O_PWDB + 4 * (dck * SL), 0, 128, [[1, SL]]),
                              AF.Tanh,
                              bias=fvec[:, FB["bo"] * 8 + dck:
                                        FB["bo"] * 8 + dck + 1])
            sc.drain()
            I("hsl", sc.nop(), 8)

    @block.sync
    def _(sync):
        if PHASES < 7:
            return
        W(sync, "oev", T)
        DSER(sync, "od")
        I("od", sync.dma_start(
            out=AP(tensor=out_t, offset=(T - 1) * 8 * 128 * SL,
                   ap=[[SL, 128], [128 * SL, 8], [1, SL]]),
            in_=FA(O_OSL, 0, 128, [[1, 8 * SL]])), 16)
        W(sync, "od", 16 * 8)

    if PHASES < 7:
        @block.sync
        def _(sync):
            for nm, val in PHASE_END[PHASES]:
                W(sync, nm, val)
            for t_ in range(T):
                I("od", sync.dma_start(
                    out=AP(tensor=out_t, offset=t_ * 8 * 128 * SL,
                           ap=[[SL, 128], [128 * SL, 8], [1, SL]]),
                    in_=FA(0, 0, 128, [[1, 8 * SL]])), 16)
            W(sync, "od", 16 * 8)

    blkcm.__exit__(None, None, None)
    for cm in ctxs:
        cm.__exit__(None, None, None)
    return nc


_NC_CACHE = {}


def kernel(**inputs):
    key = "v1-%d" % PHASES
    if key not in _NC_CACHE:
        _NC_CACHE[key] = build_nc()
    nc = _NC_CACHE[key]
    f32 = lambda v: np.ascontiguousarray(np.asarray(v, dtype=np.float32))
    x = f32(inputs["L0_attn_x"])
    A = f32(inputs["L0_attn_A"])
    Bm = f32(inputs["L0_attn_Bm"])
    Cm = f32(inputs["L0_attn_Cm"])
    Dm = f32(inputs["L0_attn_Dm"])
    in_maps = []
    for k in range(R):
        m = {
            "x": np.ascontiguousarray(x[:, k * SL:(k + 1) * SL, :]),
            "attn_A": np.ascontiguousarray(A[k * DCH:(k + 1) * DCH]),
            "attn_Bm": np.ascontiguousarray(Bm[k * DCH:(k + 1) * DCH]),
            "attn_Cm": np.ascontiguousarray(Cm[k * DCH:(k + 1) * DCH]),
            "attn_Dm": np.ascontiguousarray(Dm[k * DCH:(k + 1) * DCH]),
            "ln_w": f32(inputs["L0_attn_ln_w"]),
            "ln_b": f32(inputs["L0_attn_ln_b"]),
            "W_out": f32(inputs["L0_attn_W_out"]),
            "b_out": f32(inputs["L0_attn_b_out"]),
        }
        for nm in ("wf", "uf", "wi", "ui", "wc", "uc", "wo", "uo"):
            m[nm + "T"] = np.ascontiguousarray(
                np.asarray(inputs["L0_ffn_" + nm], dtype=np.float32).T)
        m["WoutT"] = np.ascontiguousarray(
            np.asarray(inputs["L0_attn_W_out"], dtype=np.float32).T)
        for nm in ("bf", "bi", "bc", "bo", "in_c"):
            m[nm] = f32(inputs["L0_ffn_" + nm])
        in_maps.append(m)
    res = run_bass_kernel_spmd(nc, in_maps, core_ids=list(range(R)))
    out = np.empty((B, S, D), dtype=np.float32)
    for k in range(R):
        o = res.results[k]["out"]
        out[:, k * SL:(k + 1) * SL, :] = np.transpose(
            o, (0, 3, 1, 2)).reshape(B, SL, D)
    return out
